# version 05
# baseline (speedup 1.0000x reference)
import math
import numpy as np
import ml_dtypes
import concourse.bass as bass
import concourse.mybir as mybir
from concourse.bass_utils import run_bass_kernel_spmd
from contextlib import ExitStack

F32 = mybir.dt.float32
BF16 = mybir.dt.bfloat16
AF = mybir.ActivationFunctionType
ALU = mybir.AluOpType
AX = mybir.AxisListType

P = 128
D = 1024
DCH = 8
SEQ = 8192
TOK = 2048
NST = 16
STRIPE = 512
MAGIC = 12582912.0
TWO_PI = 2.0 * math.pi
INV_2PI = 1.0 / TWO_PI
SQRT_HD = math.sqrt(128.0)

DBG_SKIP_ROWSUMS = False
DBG_L0_DUMPS = False
DBG_NQS = 4
DBG_F0_DUMPS = False
DBG_SKIP_COLSV = False
C_ZERO, C_ONE, C_NEGSQRT, C_INVSQRT, C_LNEPS, C_NEGHALF, C_HALF_PI, C_EIGHTH = range(8)
DC_INVW, DC_TBK, DC_TBKC, DC_MCK, DC_MCKC, DC_TBQ, DC_TBQC, DC_MCQ, DC_MCQC, DC_TMP = range(10)



class Stages:

    def __init__(self, sem):
        self.sem = sem
        self.items = []

    def add(self, eng, fn):
        self.items.append([(eng, fn)])
        return len(self.items) - 1

    def add_parallel(self, members):
        self.items.append(list(members))
        return len(self.items) - 1

    def run(self, section, name):
        base = 0
        for grp in self.items:
            for eng, fn in grp:
                if eng == name:
                    section.wait_ge(self.sem, base)
                    last = fn(section)
                    last.then_inc(self.sem, 1)
            base += len(grp)

    def stage_end_count(self, idx):
        return sum(len(g) for g in self.items[:idx + 1])

    def attach(self, blk, engines=("tensor", "scalar", "vector", "gpsimd", "sync")):
        st = self
        if "tensor" in engines:
            @blk.tensor
            def _(t):
                st.run(t, "tensor")
        if "scalar" in engines:
            @blk.scalar
            def _(sc):
                st.run(sc, "scalar")
        if "vector" in engines:
            @blk.vector
            def _(v):
                st.run(v, "vector")
        if "gpsimd" in engines:
            @blk.gpsimd
            def _(g):
                st.run(g, "gpsimd")
        if "sync" in engines:
            @blk.sync
            def _(s):
                st.run(s, "sync")


class Env:
    def __init__(self, nc, ctx):
        self.nc = nc
        self.ctx = ctx
        self.scopes = [ctx]
        self.t = {}
        self.a = {}
        self.core = 0
        self.qoff = 0

    def scope(self):
        from contextlib import contextmanager

        @contextmanager
        def _s():
            es = ExitStack()
            self.scopes.append(es)
            try:
                with es:
                    yield es
            finally:
                self.scopes.pop()
        return _s()

    def sb(self, name, shape, dtype):
        self.t[name] = self.scopes[-1].enter_context(self.nc.sbuf_tensor(name + "_s", shape, dtype))
        return self.t[name]

    def sem(self, name):
        self.t[name] = self.ctx.enter_context(self.nc.semaphore(name))
        return self.t[name]

    def mkchains(self):
        self.chn = {}
        for nm in ("vector", "scalar", "gpsimd"):
            self.chn[nm] = [self.sem("ch_" + nm), 0]

    def fence(self, sec, nm):
        sem, n = self.chn[nm]
        if n:
            sec.wait_ge(sem, n)

    def step(self, nm, ins):
        e = self.chn[nm]
        ins.then_inc(e[0], 1)
        e[1] += 1
        return ins



def declare_l0attn_io(nc, env):
    A = env.a
    A["xb"] = nc.dram_tensor("xb", [SEQ, D], F32, kind="ExternalInput").ap()
    A["wqT"] = nc.dram_tensor("wqT", [D, P], F32, kind="ExternalInput").ap()
    A["wkT"] = nc.dram_tensor("wkT", [D, P], F32, kind="ExternalInput").ap()
    A["wvT"] = nc.dram_tensor("wvT", [D, P], F32, kind="ExternalInput").ap()
    A["woT"] = nc.dram_tensor("woT", [P, D], F32, kind="ExternalInput").ap()
    A["cols0"] = nc.dram_tensor("cols0", [P, 8], F32, kind="ExternalInput").ap()
    A["bo_row"] = nc.dram_tensor("bo_row", [1, D], F32, kind="ExternalInput").ap()
    A["consts"] = nc.dram_tensor("consts", [P, 8], F32, kind="ExternalInput").ap()
    A["ident"] = nc.dram_tensor("ident", [P, P], F32, kind="ExternalInput").ap()
    A["xq"] = nc.dram_tensor("xq", [TOK, D], F32, kind="ExternalInput").ap()
    A["xb16"] = nc.dram_tensor("xb16", [SEQ, D], BF16, kind="Internal").ap()
    A["xq16"] = nc.dram_tensor("xq16", [TOK, D], BF16, kind="Internal").ap()
    A["xres_out"] = nc.dram_tensor("xres_out", [D, TOK], F32, kind="ExternalOutput").ap()
    if DBG_L0_DUMPS:
        A["d_cosK"] = nc.dram_tensor("d_cosK", [P, SEQ], BF16, kind="ExternalOutput").ap()
        A["d_sinK"] = nc.dram_tensor("d_sinK", [P, SEQ], BF16, kind="ExternalOutput").ap()
        A["d_cosQ"] = nc.dram_tensor("d_cosQ", [P, TOK], BF16, kind="ExternalOutput").ap()
        A["d_sinQ"] = nc.dram_tensor("d_sinQ", [P, TOK], BF16, kind="ExternalOutput").ap()
        A["d_vtok"] = nc.dram_tensor("d_vtok", [P, 64 * P], BF16, kind="ExternalOutput").ap()
        A["d_attn"] = nc.dram_tensor("d_attn", [P, TOK], F32, kind="ExternalOutput").ap()
        A["d_den"] = nc.dram_tensor("d_den", [1, TOK], F32, kind="ExternalOutput").ap()
        A["d_dcols"] = nc.dram_tensor("d_dcols", [P, 12], F32, kind="ExternalOutput").ap()
        A["d_rs"] = nc.dram_tensor("d_rs", [1, 3 * P], BF16, kind="ExternalOutput").ap()


def alloc_common_sbuf(env):
    sb = env.sb
    sb("x_res", [P, DCH, TOK], F32)
    sb("consts", [P, 8], F32)
    sb("id_f32", [P, P], F32)
    sb("id_bf", [P, P], BF16)
    sb("ones_row", [1, P], BF16)
    sb("ones_col", [P, 1], BF16)
    sb("ones32", [P, 32], BF16)
    sb("ones_row512", [1, STRIPE], BF16)


def alloc_l0attn_sbuf(env):
    sb = env.sb
    sb("cosK", [P, SEQ], BF16)
    sb("sinK", [P, SEQ], BF16)
    sb("v_tok", [P, 64, P], BF16)
    sb("cosQ", [P, TOK], BF16)
    sb("sinQ", [P, TOK], BF16)
    sb("wq_sb", [P, DCH, P], BF16)
    sb("wk_sb", [P, DCH, P], BF16)
    sb("wv_sb", [P, DCH, P], BF16)
    sb("wo_sb", [P, D], BF16)
    sb("bo_bf", [1, DCH, P], BF16)
    sb("cols0", [P, 8], F32)
    sb("dcols", [P, 12], F32)
    sb("rs_bf", [1, 3, P], BF16)
    sb("attn_acc", [P, TOK], F32)
    sb("denom_acc", [1, TOK], F32)
    sb("w_buf", [P, 4, TOK], BF16)
    sb("xt", [P, 2, DCH, STRIPE], BF16)
    sb("sq_bf", [P, 4, STRIPE], BF16)
    sb("row_negm", [1, STRIPE], F32)
    sb("row_var", [1, STRIPE], F32)
    sb("row_rstd", [1, STRIPE], F32)
    sb("row_negm_bf", [1, STRIPE], BF16)
    sb("row_rstd_bf", [1, STRIPE], BF16)
    sb("rstd_bc", [P, STRIPE], F32)
    sb("th", [P, STRIPE], F32)
    sb("uu", [P, STRIPE], F32)
    sb("si_c", [P, STRIPE], F32)
    sb("si_s", [P, STRIPE], F32)
    sb("vT_bf", [P, STRIPE], BF16)
    sb("dinv_bf", [1, STRIPE], BF16)


def ph_l0_setup(env):
    nc = env.nc
    T = env.t
    A = env.a
    ds = env.sem("ds0")
    dg = env.sem("dg0")
    gx = env.sem("gx")
    cs = env.sem("cs0")
    T = env.t

    with nc.Block("setup") as blk:
        @blk.sync
        def _(s):
            s.dma_start(T["cols0"][:], A["cols0"]).then_inc(ds, 16)
            s.dma_start(T["consts"][:], A["consts"]).then_inc(ds, 16)
            s.dma_start(T["id_f32"][:], A["ident"]).then_inc(ds, 16)
            s.wait_ge(ds, 48)

        @blk.gpsimd
        def _(g):
            g.dma_start(T["wq_sb"][:], A["wqT"].rearrange("(c p) h -> p c h", p=P)).then_inc(dg, 16)
            g.dma_start(T["wk_sb"][:], A["wkT"].rearrange("(c p) h -> p c h", p=P)).then_inc(dg, 16)
            g.dma_start(T["wv_sb"][:], A["wvT"].rearrange("(c p) h -> p c h", p=P)).then_inc(dg, 16)
            g.dma_start(T["wo_sb"][:], A["woT"]).then_inc(dg, 16)
            g.dma_start(T["bo_bf"][:], A["bo_row"].rearrange("one (c p) -> one c p", p=P)).then_inc(dg, 16)
            g.dma_start(T["id_bf"][:], A["ident"]).then_inc(dg, 16)
            g.wait_ge(dg, 96)
            g.memset(T["ones_row"][:], 1.0)
            g.memset(T["ones_col"][:], 1.0)
            g.memset(T["ones32"][:], 1.0)
            g.memset(T["ones_row512"][:], 1.0)
            g.memset(T["attn_acc"][:], 0.0)
            g.memset(T["denom_acc"][:], 0.0)

    return


def ph_l0_cols(env):
    nc = env.nc
    T = env.t
    A = env.a
    gx = T["gx"]
    cs = T["cs0"]
    with nc.Block("cols") as blk:
        @blk.vector
        def _(v):
            if DBG_SKIP_COLSV:
                return
            dc = T["dcols"]
            c0 = T["cols0"]
            v.tensor_scalar(dc[:, DC_TMP:DC_TMP + 1], c0[:, 0:1], 1e-8, None, ALU.add).then_inc(cs, 1)
            v.wait_ge(cs, 1)
            v.reciprocal(dc[:, DC_INVW:DC_INVW + 1], dc[:, DC_TMP:DC_TMP + 1]).then_inc(cs, 1)
            v.wait_ge(cs, 2)
            v.tensor_tensor(dc[:, DC_TBK:DC_TBK + 1], dc[:, DC_INVW:DC_INVW + 1], c0[:, 3:4], ALU.mult)
            v.tensor_tensor(dc[:, DC_TBQ:DC_TBQ + 1], dc[:, DC_INVW:DC_INVW + 1], c0[:, 2:3], ALU.mult).then_inc(cs, 1)
            v.wait_ge(cs, 3)
            v.tensor_tensor(dc[:, DC_TBK:DC_TBK + 1], dc[:, DC_TBK:DC_TBK + 1], c0[:, 1:2], ALU.add)
            v.tensor_tensor(dc[:, DC_TBQ:DC_TBQ + 1], dc[:, DC_TBQ:DC_TBQ + 1], c0[:, 1:2], ALU.add).then_inc(cs, 1)
            v.wait_ge(cs, 4)
            v.tensor_scalar(dc[:, DC_TBKC:DC_TBKC + 1], dc[:, DC_TBK:DC_TBK + 1], math.pi / 2, None, ALU.add)
            v.tensor_scalar(dc[:, DC_MCK:DC_MCK + 1], dc[:, DC_TBK:DC_TBK + 1], INV_2PI, MAGIC, ALU.mult, ALU.add)
            v.tensor_scalar(dc[:, DC_TBQC:DC_TBQC + 1], dc[:, DC_TBQ:DC_TBQ + 1], math.pi / 2, None, ALU.add)
            v.tensor_scalar(dc[:, DC_MCQ:DC_MCQ + 1], dc[:, DC_TBQ:DC_TBQ + 1], INV_2PI, MAGIC, ALU.mult, ALU.add).then_inc(cs, 1)
            v.wait_ge(cs, 5)
            v.tensor_scalar(dc[:, DC_MCKC:DC_MCKC + 1], dc[:, DC_TBKC:DC_TBKC + 1], INV_2PI, MAGIC, ALU.mult, ALU.add)
            v.tensor_scalar(dc[:, DC_MCQC:DC_MCQC + 1], dc[:, DC_TBQC:DC_TBQC + 1], INV_2PI, MAGIC, ALU.mult, ALU.add)

    if DBG_SKIP_ROWSUMS:
        return
    with ExitStack() as pctx:
        prows = [pctx.enter_context(nc.psum_tensor(f"prow{i}", [32, STRIPE], F32))
                 for i in range(3)]
        with nc.Block("rowsums") as blk:
            @blk.tensor
            def _(t):
                for i, w in enumerate(("wq_sb", "wk_sb", "wv_sb")):
                    for dch in range(DCH):
                        ins = t.matmul(prows[i][0:32, 0:P], T["ones32"][:], T[w][:, dch, :],
                                       start=(dch == 0), stop=(dch == DCH - 1))
                ins.then_inc(cs, 1)

            @blk.vector
            def _(v):
                v.wait_ge(cs, 6)
                for i in range(3):
                    v.tensor_copy(T["rs_bf"][0:1, i, :], prows[i][0:1, 0:P])


def ph_l0_xres_init(env):
    nc = env.nc
    T = env.t
    A = env.a
    dsx = env.sem("dsx")
    stg = env.sem("stg1")
    stages = Stages(stg)
    stag = T["w_buf"].bitcast(F32)

    with ExitStack() as pctx:
        ptrf = pctx.enter_context(nc.psum_tensor("ptrf", [P, 8, P], F32))

        for st in range(8):
            def load(s, st=st):
                src = A["xq"][st * 256:(st + 1) * 256, :]
                s.dma_start(stag[:, 0:2, :], src.rearrange("(tt p) d -> p tt d", p=P)).then_inc(dsx, 16)
                s.wait_ge(dsx, (st + 1) * 16)
                return s.nop()
            stages.add("sync", load)
            for tt in range(2):
                def trans(t, tt=tt):
                    for dch in range(DCH):
                        ins = t.transpose(ptrf[:, dch, :],
                                          stag[:, tt, dch * P:(dch + 1) * P],
                                          T["id_f32"][:])
                    return ins
                stages.add("tensor", trans)

                def evac(v, st=st, tt=tt):
                    for dch in range(DCH):
                        t0 = st * 256 + tt * P
                        ins = v.tensor_copy(T["x_res"][:, dch, t0:t0 + P], ptrf[:, dch, :])
                    return ins
                stages.add("vector", evac)

        with nc.Block("xres_init") as blk:
            stages.attach(blk, ("sync", "tensor", "vector"))


def ph_l0_prepass(env):
    nc = env.nc
    T = env.t
    A = env.a
    gx = T["gx"]
    dt_ = env.sem("dt")
    stg = env.sem("stg2")
    stages = Stages(stg)

    with ExitStack() as pctx:
        psx = pctx.enter_context(nc.psum_tensor("psx", [32, STRIPE], F32))
        psx2a = pctx.enter_context(nc.psum_tensor("psx2a", [32, STRIPE], F32))
        psx2b = pctx.enter_context(nc.psum_tensor("psx2b", [32, STRIPE], F32))
        pbc = pctx.enter_context(nc.psum_tensor("pbc", [P, STRIPE], F32))
        pk = pctx.enter_context(nc.psum_tensor("pk", [P, STRIPE], F32))
        pv = pctx.enter_context(nc.psum_tensor("pv", [P, STRIPE], F32))
        pq = pctx.enter_context(nc.psum_tensor("pq", [P, STRIPE], F32))
        ptb = pctx.enter_context(nc.psum_tensor("ptb", [P, 8, P], BF16))

        last_xt_reader = {}

        def emit_theta(v, psrc, tb_c, tb_s):
            env.step("vector", v.scalar_tensor_tensor(T["th"][:], psrc,
                                                      T["dcols"][:, DC_INVW:DC_INVW + 1],
                                                      T["rstd_bc"][:], ALU.mult, ALU.mult))
            for tb, si in ((tb_c, "si_c"), (tb_s, "si_s")):
                env.fence(v, "vector")
                env.step("vector", v.tensor_scalar(T[si][:], T["th"][:], 1.0,
                                                   T["dcols"][:, tb:tb + 1], ALU.mult, ALU.add))
                env.fence(v, "vector")
                env.step("vector", v.tensor_scalar(T["uu"][:], T[si][:], INV_2PI, MAGIC,
                                                   ALU.mult, ALU.add))
                env.step("vector", v.tensor_scalar(T["uu"][:], T["uu"][:], MAGIC, TWO_PI,
                                                   ALU.subtract, ALU.mult))
                env.fence(v, "vector")
                ins = v.tensor_sub(T[si][:], T[si][:], T["uu"][:])
                if si == "si_c":
                    env.step("vector", ins)
            return ins

        for st in range(NST + DBG_NQS):
            kv = st < NST
            b = st % 2

            def statsx(t, st=st, b=b):
                t.wait_ge(dt_, (st + 1) * 128)
                for dch in range(DCH):
                    ins = t.matmul(psx[0:32, :], T["ones32"][:], T["xt"][:, b, dch, :],
                                   start=(dch == 0), stop=(dch == DCH - 1))
                return ins
            stages.add("tensor", statsx)
            for half, p2 in ((0, psx2a), (1, psx2b)):
                def sq(sc, st=st, b=b, half=half):
                    if half == 0:
                        sc.wait_ge(dt_, (st + 1) * 128)
                    for j in range(4):
                        dch = half * 4 + j
                        ins = sc.activation(T["sq_bf"][:, j, :], T["xt"][:, b, dch, :], AF.Square)
                    return ins
                stages.add("scalar", sq)

                def stats(t, p2=p2):
                    for j in range(4):
                        ins = t.matmul(p2[0:32, :], T["ones32"][:], T["sq_bf"][:, j, :],
                                       start=(j == 0), stop=(j == 3))
                    return ins
                stages.add("tensor", stats)

            def rows1(v):
                env.step("vector", v.tensor_copy(T["row_var"][0:1, :], psx2a[0:1, :]))
                env.fence(v, "vector")
                env.step("vector", v.tensor_add(T["row_var"][0:1, :], T["row_var"][0:1, :], psx2b[0:1, :]))
                env.step("vector", v.tensor_scalar(T["row_negm"][0:1, :], psx[0:1, :], -1.0 / D, None, ALU.mult))
                env.fence(v, "vector")
                env.step("vector", v.tensor_tensor(T["row_rstd"][0:1, :], T["row_negm"][0:1, :],
                                                   T["row_negm"][0:1, :], ALU.mult))
                env.step("vector", v.tensor_scalar(T["row_var"][0:1, :], T["row_var"][0:1, :], 1.0 / D, None, ALU.mult))
                env.fence(v, "vector")
                env.step("vector", v.tensor_sub(T["row_var"][0:1, :], T["row_var"][0:1, :], T["row_rstd"][0:1, :]))
                return v.tensor_copy(T["row_negm_bf"][0:1, :], T["row_negm"][0:1, :])
            stages.add("vector", rows1)

            def rstd(sc):
                env.step("scalar", sc.activation(T["row_rstd"][0:1, :], T["row_var"][0:1, :], AF.Ln,
                              bias=T["consts"][0:1, C_LNEPS:C_LNEPS + 1], scale=1.0))
                env.fence(sc, "scalar")
                return sc.activation(T["row_var"][0:1, :], T["row_rstd"][0:1, :], AF.Exp,
                                     bias=T["consts"][0:1, C_ZERO:C_ZERO + 1],
                                     scale=T["consts"][0:1, C_NEGHALF:C_NEGHALF + 1])
            stages.add("scalar", rstd)

            def rows2(v):
                return v.tensor_copy(T["row_rstd_bf"][0:1, :], T["row_var"][0:1, :])
            stages.add("vector", rows2)

            def projs(t, b=b, kv=kv):
                t.matmul(pbc[:], T["ones_row"][:], T["row_rstd_bf"][0:1, :], start=True, stop=True)
                wl_ = (("wk_sb", pk, 1), ("wv_sb", pv, 2)) if kv else (("wq_sb", pq, 0),)
                for w, pdst, rsi in wl_:
                    for dch in range(DCH):
                        t.matmul(pdst[:], T[w][:, dch, :], T["xt"][:, b, dch, :],
                                 start=(dch == 0), stop=False)
                    ins = t.matmul(pdst[:], T["rs_bf"][0:1, rsi, :], T["row_negm_bf"][0:1, :],
                                   start=False, stop=True)
                return ins
            idx = stages.add("tensor", projs)
            last_xt_reader[st] = idx

            if kv:
                def dve_k(v):
                    env.step("vector", v.tensor_copy(T["rstd_bc"][:], pbc[:]))
                    env.fence(v, "vector")
                    emit_theta(v, pk[:], DC_TBKC, DC_TBK)
                    env.fence(v, "vector")
                    return v.tensor_tensor(T["th"][:], pv[:], T["rstd_bc"][:], ALU.mult)
                stages.add("vector", dve_k)

                def act_k(sc, st=st):
                    s0 = st * STRIPE
                    sc.activation(T["cosK"][:, s0:s0 + STRIPE], T["si_c"][:], AF.Sin,
                                  bias=T["consts"][:, C_ZERO:C_ZERO + 1], scale=1.0)
                    sc.activation(T["sinK"][:, s0:s0 + STRIPE], T["si_s"][:], AF.Sin,
                                  bias=T["consts"][:, C_ZERO:C_ZERO + 1], scale=1.0)
                    return sc.activation(T["vT_bf"][:], T["th"][:], AF.Identity,
                                         bias=T["cols0"][:, 4:5],
                                         scale=T["consts"][:, C_ONE:C_ONE + 1])
                stages.add("scalar", act_k)

                def vtrans(t):
                    for j in range(4):
                        ins = t.transpose(ptb[:, j, :], T["vT_bf"][:, j * P:(j + 1) * P],
                                          T["id_bf"][:])
                    return ins
                stages.add("tensor", vtrans)

                def vevac(v, st=st):
                    for j in range(4):
                        ins = v.tensor_copy(T["v_tok"][:, st * 4 + j, :], ptb[:, j, :])
                    return ins
                stages.add("vector", vevac)
            else:
                def dve_q(v):
                    env.step("vector", v.tensor_copy(T["rstd_bc"][:], pbc[:]))
                    env.fence(v, "vector")
                    return emit_theta(v, pq[:], DC_TBQC, DC_TBQ)
                stages.add("vector", dve_q)

                def act_q(sc, st=st):
                    q0 = (st - NST) * STRIPE
                    sc.activation(T["cosQ"][:, q0:q0 + STRIPE], T["si_c"][:], AF.Sin,
                                  bias=T["consts"][:, C_ZERO:C_ZERO + 1], scale=1.0)
                    return sc.activation(T["sinQ"][:, q0:q0 + STRIPE], T["si_s"][:], AF.Sin,
                                         bias=T["consts"][:, C_ZERO:C_ZERO + 1], scale=1.0)
                stages.add("scalar", act_q)

        with nc.Block("prepass") as blk:
            @blk.gpsimd
            def _(g):
                A2 = env.a
                for s_ in range(NST):
                    g.dma_start(A2["xb16"][s_ * STRIPE:(s_ + 1) * STRIPE, :],
                                A2["xb"][s_ * STRIPE:(s_ + 1) * STRIPE, :]).then_inc(gx, 16)
                for s_ in range(4):
                    g.dma_start(A2["xq16"][s_ * STRIPE:(s_ + 1) * STRIPE, :],
                                A2["xq"][s_ * STRIPE:(s_ + 1) * STRIPE, :]).then_inc(gx, 16)

            @blk.sync
            def _(s):
                for st in range(NST + DBG_NQS):
                    s.wait_ge(gx, (st + 1) * 16)
                    if st >= 2:
                        s.wait_ge(stg, stages.stage_end_count(last_xt_reader[st - 2]))
                    if st < NST:
                        srcd = A["xb16"][st * STRIPE:(st + 1) * STRIPE, :]
                    else:
                        srcd = A["xq16"][(st - NST) * STRIPE:(st - NST + 1) * STRIPE, :]
                    for dch in range(DCH):
                        s.dma_start(
                            T["xt"][:, st % 2, dch, :],
                            srcd[:, dch * P:(dch + 1) * P],
                            transpose=True).then_inc(dt_, 16)

            stages.attach(blk, ("tensor", "scalar", "vector"))


def ph_l0_scores(env):
    nc = env.nc
    T = env.t
    psc = env.sem("psc")
    se2 = env.sem("se2")
    pvb = env.sem("pvb")
    va = env.sem("va")
    G = 4
    NG = 64 // G

    with ExitStack() as pctx:
        S = pctx.enter_context(nc.psum_tensor("S", [P, 2, STRIPE], F32))
        Aps = pctx.enter_context(nc.psum_tensor("Aps", [P, 4, STRIPE], F32))
        D1 = pctx.enter_context(nc.psum_tensor("D1", [P, STRIPE], F32))
        D2 = pctx.enter_context(nc.psum_tensor("D2", [32, STRIPE], F32))

        with nc.Block("scores") as blk:
            @blk.tensor
            def _(t):
                for g in range(NG):
                    for j in range(G * 4):
                        kt, qc = g * G + j // 4, j % 4
                        idx = g * G * 4 + j + 1
                        if idx >= 3:
                            t.wait_ge(se2, idx - 2)
                        k0 = kt * P
                        t.matmul(S[:, (idx - 1) % 2, :], T["cosK"][:, k0:k0 + P],
                                 T["cosQ"][:, qc * STRIPE:(qc + 1) * STRIPE],
                                 start=True, stop=False)
                        t.matmul(S[:, (idx - 1) % 2, :], T["sinK"][:, k0:k0 + P],
                                 T["sinQ"][:, qc * STRIPE:(qc + 1) * STRIPE],
                                 start=False, stop=True).then_inc(psc, 1)
                    t.wait_ge(se2, (g + 1) * G * 4)
                    if g >= 1:
                        t.wait_ge(va, g)
                    for qc in range(4):
                        for kk in range(G):
                            t.matmul(Aps[:, qc, :], T["v_tok"][:, g * G + kk, :],
                                     T["w_buf"][:, kk, qc * STRIPE:(qc + 1) * STRIPE],
                                     start=(kk == 0), stop=(kk == G - 1))
                    for qc in range(4):
                        dtile = D1[qc * 32:(qc + 1) * 32, :] if qc < 3 else D2[:, :]
                        for kk in range(G):
                            ins = t.matmul(dtile, T["ones32"][:],
                                           T["w_buf"][:, kk, qc * STRIPE:(qc + 1) * STRIPE],
                                           start=(kk == 0), stop=(kk == G - 1))
                    ins.then_inc(pvb, 1)

            @blk.scalar
            def _(sc):
                for g in range(NG):
                    for j in range(G * 4):
                        kt, qc = g * G + j // 4, j % 4
                        idx = g * G * 4 + j + 1
                        sc.wait_ge(psc, idx)
                        if g >= 1 and j == 0:
                            sc.wait_ge(pvb, g)
                        sc.activation(T["w_buf"][:, kt % G, qc * STRIPE:(qc + 1) * STRIPE],
                                      S[:, (idx - 1) % 2, :], AF.Exp,
                                      bias=T["consts"][:, C_NEGSQRT:C_NEGSQRT + 1],
                                      scale=T["consts"][:, C_INVSQRT:C_INVSQRT + 1]
                                      ).then_inc(se2, 1)

            @blk.vector
            def _(v):
                for g in range(NG):
                    v.wait_ge(pvb, g + 1)
                    for qc in range(4):
                        v.tensor_add(T["attn_acc"][:, qc * STRIPE:(qc + 1) * STRIPE],
                                     T["attn_acc"][:, qc * STRIPE:(qc + 1) * STRIPE],
                                     Aps[:, qc, :])
                    for qc in range(4):
                        srcp = D1[qc * 32:qc * 32 + 1, :] if qc < 3 else D2[0:1, :]
                        ins = v.tensor_add(T["denom_acc"][0:1, qc * STRIPE:(qc + 1) * STRIPE],
                                           T["denom_acc"][0:1, qc * STRIPE:(qc + 1) * STRIPE],
                                           srcp)
                    ins.then_inc(va, 1)


def ph_l0_epilogue(env):
    nc = env.nc
    T = env.t
    stg = env.sem("stg3")
    stages = Stages(stg)

    with ExitStack() as pctx:
        pbc2 = pctx.enter_context(nc.psum_tensor("pbc2", [P, STRIPE], F32))
        py = pctx.enter_context(nc.psum_tensor("py", [P, 2, STRIPE], F32))

        for qc in range(4):
            def recip(v, qc=qc):
                return v.reciprocal(T["row_var"][0:1, :],
                                    T["denom_acc"][0:1, qc * STRIPE:(qc + 1) * STRIPE])
            stages.add("vector", recip)

            def recip2(v):
                return v.tensor_copy(T["dinv_bf"][0:1, :], T["row_var"][0:1, :])
            stages.add("vector", recip2)

            def bc(t, qc=qc):
                return t.matmul(pbc2[:], T["ones_row"][:], T["dinv_bf"][0:1, :],
                                start=True, stop=True)
            stages.add("tensor", bc)

            def norm(v, qc=qc):
                return v.tensor_tensor(T["w_buf"][:, 0, qc * STRIPE:(qc + 1) * STRIPE],
                                       T["attn_acc"][:, qc * STRIPE:(qc + 1) * STRIPE],
                                       pbc2[:], ALU.mult)
            stages.add("vector", norm)

        for och in range(DCH):
            for qc in range(4):
                def proj(t, och=och, qc=qc):
                    pslot = py[:, (och * 4 + qc) % 2, :]
                    t.matmul(pslot, T["wo_sb"][:, och * P:(och + 1) * P],
                             T["w_buf"][:, 0, qc * STRIPE:(qc + 1) * STRIPE],
                             start=True, stop=False)
                    return t.matmul(pslot, T["bo_bf"][0:1, och, :], T["ones_row512"][0:1, :],
                                    start=False, stop=True)
                stages.add("tensor", proj)

                def resadd(v, och=och, qc=qc):
                    dst = T["x_res"][:, och, qc * STRIPE:(qc + 1) * STRIPE]
                    return v.tensor_add(dst, dst, py[:, (och * 4 + qc) % 2, :])
                stages.add("vector", resadd)

        with nc.Block("l0epi") as blk:
            stages.attach(blk, ("tensor", "vector"))


def ph_dump_xres(env):
    nc = env.nc
    T = env.t
    A = env.a
    do = env.sem("do")
    n = [0]
    with nc.Block("dump") as blk:
        @blk.sync
        def _(s):
            def d(dst, srcap):
                s.dma_start(dst, srcap).then_inc(do, 16)
                n[0] += 16
            d(A["xres_out"].rearrange("(c p) t -> p c t", p=P), T["x_res"][:])
            if DBG_L0_DUMPS:
                d(A["d_cosK"], T["cosK"][:])
                d(A["d_sinK"], T["sinK"][:])
                d(A["d_cosQ"], T["cosQ"][:])
                d(A["d_sinQ"], T["sinQ"][:])
                d(A["d_vtok"].rearrange("p (a b) -> p a b", b=P), T["v_tok"][:])
                d(A["d_attn"], T["attn_acc"][:])
                d(A["d_den"], T["denom_acc"][:])
                d(A["d_dcols"], T["dcols"][:])
        @blk.gpsimd
        def _(g):
            if DBG_L0_DUMPS:
                g.dma_start(A["d_rs"], T["rs_bf"][0:1, :, :].rearrange("a b c -> a (b c)")).then_inc(do, 16)
                n[0] += 16
            g.wait_ge(do, n[0])
        @blk.vector
        def _(v):
            pass
    with nc.Block("dumpwait") as blk:
        @blk.sync
        def _(s):
            s.wait_ge(do, n[0])


def build_l0attn(core, upto=6):
    nc = bass.Bass("TRN2", target_bir_lowering=False)
    with ExitStack() as ctx:
        env = Env(nc, ctx)
        env.core = core
        env.qoff = (core % 4) * TOK
        declare_l0attn_io(nc, env)
        env.mkchains()
        alloc_common_sbuf(env)
        alloc_l0attn_sbuf(env)
        phases = [ph_l0_setup, ph_l0_cols, ph_l0_xres_init, ph_l0_prepass, ph_l0_scores, ph_l0_epilogue]
        for i, ph in enumerate(phases):
            if i < upto:
                ph(env)
        ph_dump_xres(env)
    return nc


def prep_l0attn_inputs(I, core):
    f32 = np.float32
    b = core // 4
    x = np.asarray(I["L0_attn_x"], f32)
    cols = np.zeros((P, 8), f32)
    cols[:, 0] = np.asarray(I["L0_attn_wavelengths"], f32)
    cols[:, 1] = np.asarray(I["L0_attn_phase_bias"], f32)
    cols[:, 2] = np.asarray(I["L0_attn_bq"], f32)
    cols[:, 3] = np.asarray(I["L0_attn_bk"], f32)
    cols[:, 4] = np.asarray(I["L0_attn_bv"], f32)
    consts = np.zeros((P, 8), f32)
    consts[:, C_ONE] = 1.0
    consts[:, C_NEGSQRT] = -SQRT_HD
    consts[:, C_INVSQRT] = 1.0 / SQRT_HD
    consts[:, C_LNEPS] = 1e-5
    consts[:, C_NEGHALF] = -0.5
    consts[:, C_HALF_PI] = math.pi / 2.0
    consts[:, C_EIGHTH] = 0.125
    qoff = (core % 4) * TOK
    return {
        "xb": np.ascontiguousarray(x[b]),
        "xq": np.ascontiguousarray(x[b][qoff:qoff + TOK]),
        "wqT": np.ascontiguousarray(np.asarray(I["L0_attn_Wq"], f32).T),
        "wkT": np.ascontiguousarray(np.asarray(I["L0_attn_Wk"], f32).T),
        "wvT": np.ascontiguousarray(np.asarray(I["L0_attn_Wv"], f32).T),
        "woT": np.ascontiguousarray(np.asarray(I["L0_attn_Wo"], f32).T),
        "cols0": cols,
        "bo_row": np.ascontiguousarray(np.asarray(I["L0_attn_bo"], f32).reshape(1, D)),
        "consts": consts,
        "ident": np.eye(P, dtype=f32),
    }





def declare_l0ffn_io(nc, env, io_xres=True):
    A = env.a
    A["bwT"] = nc.dram_tensor("bwT", [D, D], F32, kind="ExternalInput").ap()
    A["sAT"] = nc.dram_tensor("sAT", [D, 32], F32, kind="ExternalInput").ap()
    A["sBT"] = nc.dram_tensor("sBT", [16, D], F32, kind="ExternalInput").ap()
    A["rAT"] = nc.dram_tensor("rAT", [D, 96], F32, kind="ExternalInput").ap()
    A["rBT"] = nc.dram_tensor("rBT", [96, D], F32, kind="ExternalInput").ap()
    A["rWT"] = nc.dram_tensor("rWT", [D, 32], F32, kind="ExternalInput").ap()
    A["rb_row"] = nc.dram_tensor("rb_row", [1, 32], F32, kind="ExternalInput").ap()
    A["bb_row"] = nc.dram_tensor("bb_row", [1, D], F32, kind="ExternalInput").ap()
    if io_xres:
        A["xres_in"] = nc.dram_tensor("xres_in", [D, TOK], F32, kind="ExternalInput").ap()
        A["consts"] = nc.dram_tensor("consts", [P, 8], F32, kind="ExternalInput").ap()
        A["ident"] = nc.dram_tensor("ident", [P, P], F32, kind="ExternalInput").ap()
        A["xres_out"] = nc.dram_tensor("xres_out", [D, TOK], F32, kind="ExternalOutput").ap()


def alloc_l0ffn_sbuf(env):
    sb = env.sb
    sb("f_bwT", [P, DCH, D], BF16)
    sb("f_sAT", [P, DCH, 32], BF16)
    sb("f_sBT", [16, D], BF16)
    sb("f_rAT", [P, DCH, 96], BF16)
    sb("f_rBT", [96, D], BF16)
    sb("f_rWT", [P, DCH, 32], BF16)
    sb("f_rb", [1, 32], BF16)
    sb("f_rstd_bf_row", [1, STRIPE], BF16)
    sb("f_logsb", [8, STRIPE], F32)
    sb("f_mbc2", [P, STRIPE], F32)
    sb("f_bb", [1, DCH, P], BF16)
    sb("f_xn", [P, DCH, STRIPE], BF16)
    sb("f_sq", [P, 4, STRIPE], BF16)
    sb("f_s1", [16, STRIPE], BF16)
    sb("f_glog", [P, 4, 8], F32)
    sb("f_rsc", [P, 4, 8], F32)
    sb("f_rsc2", [P, 4, 8], F32)
    sb("f_rsc3", [P, 4, 8], F32)
    sb("f_rcol", [P, 16], F32)
    sb("f_h2", [P, 4, 96], BF16)
    sb("f_h2T", [P, 4, P], BF16)
    sb("f_htmp", [P, 96], F32)
    sb("f_rstdtok", [P, 8], F32)
    sb("f_mbc", [P, STRIPE], F32)
    sb("f_row_negm", [1, STRIPE], F32)
    sb("f_row_var", [1, STRIPE], F32)
    sb("f_row_rstd", [1, STRIPE], F32)
    sb("f_negm_bf", [1, STRIPE], BF16)
    sb("f_rstd_bc", [P, STRIPE], F32)


def ph_l0ffn_setup(env):
    nc = env.nc
    T = env.t
    A = env.a
    dg = env.sem("fdg")
    with nc.Block("ffn0_setup") as blk:
        @blk.gpsimd
        def _(g):
            n = 0
            for name, ap, pat in (
                ("f_bwT", A["bwT"], "(c p) o -> p c o"),
                ("f_sAT", A["sAT"], "(c p) r -> p c r"),
                ("f_rAT", A["rAT"], "(c p) r -> p c r"),
                ("f_rWT", A["rWT"], "(c p) r -> p c r"),
            ):
                g.dma_start(T[name][:], ap.rearrange(pat, p=P)).then_inc(dg, 16)
                n += 16
            g.dma_start(T["f_sBT"][:], A["sBT"]).then_inc(dg, 16)
            g.dma_start(T["f_rBT"][:], A["rBT"]).then_inc(dg, 16)
            g.dma_start(T["f_rb"][:], A["rb_row"]).then_inc(dg, 16)
            g.dma_start(T["f_bb"][:], A["bb_row"].rearrange("one (c p) -> one c p", p=P)).then_inc(dg, 16)
            n += 64
            g.wait_ge(dg, n)


def ph_l0ffn(env):
    nc = env.nc
    T = env.t
    stg = env.sem("fstg")
    stages = Stages(stg)

    with ExitStack() as pctx:
        B = [pctx.enter_context(nc.psum_tensor(f"fb{i}", [P, STRIPE], F32))
             for i in range(8)]
        Bb = [b.bitcast(BF16) for b in B]

        for s in range(4):
            t0s = s * STRIPE

            def cast(g, t0s=t0s):
                for dch in range(DCH):
                    ins = g.tensor_copy(T["f_xn"][:, dch, :],
                                        T["x_res"][:, dch, t0s:t0s + STRIPE])
                return ins
            stages.add("gpsimd", cast)

            def statsx(t):
                for dch in range(DCH):
                    ins = t.matmul(B[0][0:32, :], T["ones32"][:], T["f_xn"][:, dch, :],
                                   start=(dch == 0), stop=(dch == DCH - 1))
                return ins
            stages.add("tensor", statsx)
            for half, bi in ((0, 1), (1, 2)):
                def sq(sc, half=half):
                    for j in range(4):
                        ins = sc.activation(T["f_sq"][:, j, :], T["f_xn"][:, half * 4 + j, :],
                                            AF.Square)
                    return ins
                stages.add("scalar", sq)

                def stats(t, bi=bi):
                    for j in range(4):
                        ins = t.matmul(B[bi][0:32, :], T["ones32"][:], T["f_sq"][:, j, :],
                                       start=(j == 0), stop=(j == 3))
                    return ins
                stages.add("tensor", stats)

            def rows1(v):
                env.step("vector", v.tensor_copy(T["f_row_var"][0:1, :], B[1][0:1, :]))
                env.fence(v, "vector")
                env.step("vector", v.tensor_add(T["f_row_var"][0:1, :], T["f_row_var"][0:1, :], B[2][0:1, :]))
                env.step("vector", v.tensor_scalar(T["f_row_negm"][0:1, :], B[0][0:1, :], -1.0 / D, None, ALU.mult))
                env.fence(v, "vector")
                env.step("vector", v.tensor_tensor(T["f_row_rstd"][0:1, :], T["f_row_negm"][0:1, :],
                                                   T["f_row_negm"][0:1, :], ALU.mult))
                env.step("vector", v.tensor_scalar(T["f_row_var"][0:1, :], T["f_row_var"][0:1, :], 1.0 / D, None, ALU.mult))
                env.fence(v, "vector")
                env.step("vector", v.tensor_sub(T["f_row_var"][0:1, :], T["f_row_var"][0:1, :], T["f_row_rstd"][0:1, :]))
                return v.tensor_copy(T["f_negm_bf"][0:1, :], T["f_row_negm"][0:1, :])
            stages.add("vector", rows1)

            def rstd(sc):
                env.step("scalar", sc.activation(T["f_row_rstd"][0:1, :], T["f_row_var"][0:1, :], AF.Ln,
                              bias=T["consts"][0:1, C_LNEPS:C_LNEPS + 1], scale=1.0))
                env.fence(sc, "scalar")
                return sc.activation(T["f_row_var"][0:1, :], T["f_row_rstd"][0:1, :], AF.Exp,
                                     bias=T["consts"][0:1, C_ZERO:C_ZERO + 1],
                                     scale=T["consts"][0:1, C_NEGHALF:C_NEGHALF + 1])
            stages.add("scalar", rstd)

            def rows2b(v):
                return v.tensor_copy(T["f_rstd_bf_row"][0:1, :], T["f_row_var"][0:1, :])
            stages.add("vector", rows2b)

            def bc(t):
                t.matmul(B[3][:], T["ones_row"][:], T["f_rstd_bf_row"][0:1, :], start=True, stop=True)
                return t.matmul(B[4][:], T["ones_row"][:], T["f_negm_bf"][0:1, :], start=True, stop=True)
            stages.add("tensor", bc)

            def bcevac(v):
                v.tensor_copy(T["f_rstd_bc"][:], B[3][:])
                return v.tensor_copy(T["f_mbc"][:], B[4][:])
            stages.add("vector", bcevac)

            def xn(v, t0s=t0s):
                for dch in range(DCH):
                    env.fence(v, "vector")
                    env.step("vector", v.tensor_add(T["f_mbc2"][:],
                                                    T["x_res"][:, dch, t0s:t0s + STRIPE],
                                                    T["f_mbc"][:]))
                    env.fence(v, "vector")
                    ins = v.tensor_mul(T["f_xn"][:, dch, :], T["f_mbc2"][:], T["f_rstd_bc"][:])
                    if dch < DCH - 1:
                        env.step("vector", ins)
                return ins
            stages.add("vector", xn)

            def s1mm(t):
                for dch in range(DCH):
                    ins = t.matmul(B[0][0:32, :], T["f_sAT"][:, dch, :], T["f_xn"][:, dch, :],
                                   start=(dch == 0), stop=(dch == DCH - 1))
                return ins
            stages.add("tensor", s1mm)

            def logmm(t):
                for dch in range(DCH):
                    t.matmul(B[1][0:32, :], T["f_rWT"][:, dch, :], T["f_xn"][:, dch, :],
                             start=(dch == 0), stop=False)
                return t.matmul(B[1][0:32, :], T["f_rb"][0:1, :], T["ones_row512"][0:1, :],
                                start=False, stop=True)
            stages.add("tensor", logmm)

            def s1evac(v):
                v.tensor_copy(T["f_s1"][0:16, :], B[0][0:16, :])
                return v.tensor_copy(T["f_logsb"][0:8, :], B[1][0:8, :])
            stages.add("vector", s1evac)

            def gtr(t):
                for tt in range(4):
                    t.transpose(B[2][:, tt * 8:tt * 8 + 6],
                                T["f_logsb"][0:6, tt * P:(tt + 1) * P],
                                T["id_f32"][0:6, 0:6])
                for tt in range(4):
                    ins = t.transpose(B[5][:, 2 * tt:2 * tt + 1],
                                      T["f_row_var"][0:1, tt * P:(tt + 1) * P],
                                      T["id_f32"][0:1, 0:1])
                return ins
            stages.add("tensor", gtr)

            def gevac(v):
                for tt in range(4):
                    v.tensor_copy(T["f_glog"][:, tt, 0:6], B[2][:, tt * 8:tt * 8 + 6])
                return v.tensor_copy(T["f_rstdtok"][:, 0:8], B[5][:, 0:8])
            stages.add("vector", gevac)

            def phmm(t):
                for tt in range(4):
                    bsel = B[5] if tt % 2 == 0 else B[4]
                    off = 256 + (tt // 2) * 96
                    for dch in range(DCH):
                        ins = t.matmul(bsel[:, off:off + 96],
                                       T["f_xn"][:, dch, tt * P:(tt + 1) * P],
                                       T["f_rAT"][:, dch, :],
                                       start=(dch == 0), stop=(dch == DCH - 1))
                return ins
            stages.add("tensor", phmm)

            for tt in range(4):
                def r1(v, tt=tt):
                    return v.reduce_max(T["f_rcol"][:, 0:1], T["f_glog"][:, tt, 0:6], axis=AX.X)
                stages.add("vector", r1)

                def r2(v, tt=tt):
                    return v.tensor_scalar(T["f_rcol"][:, 1:2], T["f_rcol"][:, 0:1], -1.0,
                                           None, ALU.mult)
                stages.add("vector", r2)

                def r3(sc, tt=tt):
                    return sc.activation(T["f_rsc"][:, tt, 0:6], T["f_glog"][:, tt, 0:6],
                                         AF.Exp, bias=T["f_rcol"][:, 1:2], scale=1.0)
                stages.add("scalar", r3)

                def r4a(v, tt=tt):
                    v.reduce_sum(T["f_rcol"][:, 2:3], T["f_rsc"][:, tt, 0:6], axis=AX.X)
                    env.step("vector", v.tensor_single_scalar(T["f_rsc2"][:, tt, 0:6],
                                                              T["f_glog"][:, tt, 0:6],
                                                              T["f_rcol"][:, 0:1], ALU.is_equal))
                    env.fence(v, "vector")
                    return v.scalar_tensor_tensor(T["f_rsc3"][:, tt, 0:6], T["f_rsc2"][:, tt, 0:6],
                                                  -1e30, T["f_glog"][:, tt, 0:6], ALU.mult, ALU.add)
                stages.add("vector", r4a)

                def r4c(v, tt=tt):
                    return v.reduce_max(T["f_rcol"][:, 3:4], T["f_rsc3"][:, tt, 0:6], axis=AX.X)
                stages.add("vector", r4c)

                def r5a(v, tt=tt):
                    env.step("vector", v.tensor_single_scalar(T["f_rsc3"][:, tt, 0:6],
                                                              T["f_glog"][:, tt, 0:6],
                                                              T["f_rcol"][:, 3:4], ALU.is_ge))
                    env.fence(v, "vector")
                    return v.tensor_mul(T["f_rsc2"][:, tt, 0:6], T["f_rsc"][:, tt, 0:6],
                                        T["f_rsc3"][:, tt, 0:6])
                stages.add("vector", r5a)

                def r5b(v, tt=tt):
                    return v.reduce_sum(T["f_rcol"][:, 4:5], T["f_rsc2"][:, tt, 0:6], axis=AX.X)
                stages.add("vector", r5b)

                def r5c(v, tt=tt):
                    return v.scalar_tensor_tensor(T["f_rcol"][:, 5:6], T["f_rcol"][:, 2:3],
                                                  1e-6, T["f_rcol"][:, 4:5], ALU.mult, ALU.add)
                stages.add("vector", r5c)

                def r6(v, tt=tt):
                    return v.reciprocal(T["f_rcol"][:, 6:7], T["f_rcol"][:, 5:6])
                stages.add("vector", r6)

                def r7(v, tt=tt):
                    return v.tensor_single_scalar(T["f_rsc3"][:, tt, 0:6],
                                                  T["f_rsc2"][:, tt, 0:6],
                                                  T["f_rcol"][:, 6:7], ALU.mult)
                stages.add("vector", r7)

            for tt in range(4):
                def h2(v, tt=tt):
                    bsel = B[5] if tt % 2 == 0 else B[4]
                    off = 256 + (tt // 2) * 96
                    env.step("vector", v.tensor_single_scalar(T["f_htmp"][:, 0:96],
                                                              bsel[:, off:off + 96],
                                                              T["f_rstdtok"][:, 2 * tt:2 * tt + 1],
                                                              ALU.mult))
                    env.fence(v, "vector")
                    for e in range(6):
                        ins = v.tensor_single_scalar(T["f_h2"][:, tt, e * 16:(e + 1) * 16],
                                                     T["f_htmp"][:, e * 16:(e + 1) * 16],
                                                     T["f_rsc3"][:, tt, e:e + 1], ALU.mult)
                    return ins
                stages.add("vector", h2)

            def h2tr(t):
                for tt in range(4):
                    ins = t.transpose(Bb[6][0:96, tt * P:(tt + 1) * P],
                                      T["f_h2"][:, tt, :], T["id_bf"][:])
                return ins
            stages.add("tensor", h2tr)

            def h2evac(v):
                for tt in range(4):
                    ins = v.tensor_copy(T["f_h2T"][0:96, tt, :], Bb[6][0:96, tt * P:(tt + 1) * P])
                return ins
            stages.add("vector", h2evac)

            for och in range(DCH):
                def ymm(t, och=och):
                    yb = B[7] if och % 2 == 0 else B[3]
                    for dch in range(DCH):
                        t.matmul(yb[:], T["f_bwT"][:, dch, och * P:(och + 1) * P],
                                 T["f_xn"][:, dch, :], start=(dch == 0), stop=False)
                    t.matmul(yb[:], T["f_sBT"][0:16, och * P:(och + 1) * P],
                             T["f_s1"][0:16, :], start=False, stop=False)
                    t.matmul(yb[:], T["f_rBT"][0:96, och * P:(och + 1) * P],
                             T["f_h2T"][0:96, :, :].rearrange("p a b -> p (a b)"),
                             start=False, stop=False)
                    return t.matmul(yb[:], T["f_bb"][0:1, och, :], T["ones_row512"][0:1, :],
                                    start=False, stop=True)
                stages.add("tensor", ymm)

                def yadd(v, och=och, t0s=t0s):
                    yb = B[7] if och % 2 == 0 else B[3]
                    dst = T["x_res"][:, och, t0s:t0s + STRIPE]
                    return v.tensor_add(dst, dst, yb[:])
                stages.add("vector", yadd)

        with nc.Block("ffn0") as blk:
            stages.attach(blk, ("tensor", "scalar", "vector", "gpsimd"))


def ph_load_xres(env):
    nc = env.nc
    T = env.t
    A = env.a
    dl = env.sem("dlx")
    with nc.Block("loadx") as blk:
        @blk.sync
        def _(s):
            s.dma_start(T["x_res"][:],
                        A["xres_in"].rearrange("(c p) t -> p c t", p=P)).then_inc(dl, 16)
            s.wait_ge(dl, 16)
        @blk.gpsimd
        def _(g):
            g.dma_start(T["id_bf"][:], A["ident"]).then_inc(dl, 16)
            g.wait_ge(dl, 32)
        @blk.scalar
        def _(sc):
            pass


def build_l0ffn(core):
    nc = bass.Bass("TRN2", target_bir_lowering=False)
    with ExitStack() as ctx:
        env = Env(nc, ctx)
        declare_l0ffn_io(nc, env)
        env.mkchains()
        if DBG_F0_DUMPS:
            A = env.a
            for nm, shp, dt_ in (("g_xn", [P, DCH * STRIPE], BF16), ("g_glog", [P, 32], F32),
                                 ("g_wn", [P, 32], F32), ("g_s1", [16, STRIPE], BF16),
                                 ("g_h2T", [P, 4 * P], BF16), ("g_rstdtok", [P, 8], F32),
                                 ("g_rows", [1, 3 * STRIPE], F32), ("g_rcol", [P, 16], F32), ("g_e", [P, 32], F32)):
                A[nm] = nc.dram_tensor(nm, shp, dt_, kind="ExternalOutput").ap()
        alloc_common_sbuf(env)
        alloc_l0ffn_sbuf(env)
        dsc = env.sem("dsc")
        with nc.Block("c0") as blk:
            @blk.sync
            def _(s):
                s.dma_start(env.t["consts"][:], env.a["consts"]).then_inc(dsc, 16)
                s.dma_start(env.t["id_f32"][:], env.a["ident"]).then_inc(dsc, 16)
                s.wait_ge(dsc, 32)
            @blk.gpsimd
            def _(g):
                g.memset(env.t["ones_row"][:], 1.0)
                g.memset(env.t["ones_col"][:], 1.0)
                g.memset(env.t["ones32"][:], 1.0)
                g.memset(env.t["ones_row512"][:], 1.0)
        ph_load_xres(env)
        ph_l0ffn_setup(env)
        ph_l0ffn(env)
        if DBG_F0_DUMPS:
            T = env.t
            A = env.a
            dd = env.sem("ddf")
            with nc.Block("fdump") as blk:
                @blk.sync
                def _(s):
                    s.dma_start(A["g_xn"], T["f_xn"][:].rearrange("p a b -> p (a b)")).then_inc(dd, 16)
                    s.dma_start(A["g_glog"], T["f_glog"][:].rearrange("p a b -> p (a b)")).then_inc(dd, 16)
                    s.dma_start(A["g_wn"], T["f_rsc3"][:].rearrange("p a b -> p (a b)")).then_inc(dd, 16)
                    s.dma_start(A["g_h2T"], T["f_h2T"][:].rearrange("p a b -> p (a b)")).then_inc(dd, 16)
                    s.dma_start(A["g_rstdtok"], T["f_rstdtok"][:]).then_inc(dd, 16)
                    s.dma_start(A["g_rcol"], T["f_rcol"][:]).then_inc(dd, 16)
                    s.dma_start(A["g_e"], T["f_rsc"][:].rearrange("p a b -> p (a b)")).then_inc(dd, 16)
                    s.wait_ge(dd, 112)
                @blk.gpsimd
                def _(g):
                    g.dma_start(A["g_s1"], T["f_s1"][:]).then_inc(dd, 16)
                    rows = [T["f_row_negm"], T["f_row_var"], T["f_row_rstd"]]
                    for i, r in enumerate(rows):
                        g.dma_start(A["g_rows"][0:1, i * STRIPE:(i + 1) * STRIPE], r[0:1, :]).then_inc(dd, 16)
                    g.wait_ge(dd, 176)
        ph_dump_xres(env)
    return nc





def declare_l1attn_io(nc, env, io_xres=True):
    A = env.a
    A["wqc"] = nc.dram_tensor("wqc", [D, D], F32, kind="ExternalInput").ap()
    A["wkc"] = nc.dram_tensor("wkc", [D, D], F32, kind="ExternalInput").ap()
    A["wvc"] = nc.dram_tensor("wvc", [D, D], F32, kind="ExternalInput").ap()
    A["woc"] = nc.dram_tensor("woc", [D, D], F32, kind="ExternalInput").ap()
    A["bq_row"] = nc.dram_tensor("bq_row", [1, D], F32, kind="ExternalInput").ap()
    A["bk_row"] = nc.dram_tensor("bk_row", [1, D], F32, kind="ExternalInput").ap()
    A["bv_row"] = nc.dram_tensor("bv_row", [1, D], F32, kind="ExternalInput").ap()
    A["boc_row"] = nc.dram_tensor("boc_row", [1, D], F32, kind="ExternalInput").ap()
    A["mrows"] = nc.dram_tensor("mrows", [P, 4 * P], F32, kind="ExternalInput").ap()
    if io_xres:
        A["xres_in"] = nc.dram_tensor("xres_in", [D, TOK], F32, kind="ExternalInput").ap()
        A["consts"] = nc.dram_tensor("consts", [P, 8], F32, kind="ExternalInput").ap()
        A["ident"] = nc.dram_tensor("ident", [P, P], F32, kind="ExternalInput").ap()
        A["xres_out"] = nc.dram_tensor("xres_out", [D, TOK], F32, kind="ExternalOutput").ap()


def alloc_l1attn_sbuf(env):
    sb = env.sb
    sb("a_wq", [P, DCH, D], BF16)
    sb("a_wk", [P, DCH, D], BF16)
    sb("a_wv", [P, DCH, D], BF16)
    sb("a_wo", [P, DCH, D], BF16)
    sb("a_bq", [1, DCH, P], BF16)
    sb("a_bk", [1, DCH, P], BF16)
    sb("a_bv", [1, D], BF16)
    sb("a_bo", [1, DCH, P], BF16)
    sb("a_bvbc", [P, D], BF16)
    sb("a_xn", [P, DCH, STRIPE], BF16)
    sb("a_sq", [P, 4, STRIPE], BF16)
    sb("a_qT", [P, DCH, STRIPE], BF16)
    sb("a_kT", [P, DCH, STRIPE], BF16)
    sb("a_vtok", [P, 4, 16 * 65], BF16)
    sb("a_vals", [P, 4, D], BF16)
    sb("a_valsT", [P, DCH, STRIPE], BF16)
    sb("a_wexp", [P, 2, P], BF16)
    sb("a_mrows", [P, 4, P], BF16)
    sb("a_iv", [P, 1], F32)
    sb("a_mbc", [P, STRIPE], F32)
    sb("a_mbc2", [P, STRIPE], F32)
    sb("a_rstd_bc", [P, STRIPE], F32)
    sb("a_row_negm", [1, STRIPE], F32)
    sb("a_row_var", [1, STRIPE], F32)
    sb("a_row_rstd", [1, STRIPE], F32)
    sb("a_negm_bf", [1, STRIPE], BF16)
    sb("a_rstd_bf", [1, STRIPE], BF16)


def ph_l1attn_setup(env):
    nc = env.nc
    T = env.t
    A = env.a
    dg = env.sem("adg")
    cs = env.sem("acs")
    with nc.Block("attn1_setup") as blk:
        @blk.gpsimd
        def _(g):
            n = 0
            for name, ap in (("a_wq", A["wqc"]), ("a_wk", A["wkc"]),
                             ("a_wv", A["wvc"]), ("a_wo", A["woc"])):
                g.dma_start(T[name][:], ap.rearrange("(c p) o -> p c o", p=P)).then_inc(dg, 16)
                n += 16
            for name, ap in (("a_bq", A["bq_row"]), ("a_bk", A["bk_row"]),
                             ("a_bo", A["boc_row"])):
                g.dma_start(T[name][:], ap.rearrange("one (c p) -> one c p", p=P)).then_inc(dg, 16)
                n += 16
            g.dma_start(T["a_bv"][:], A["bv_row"]).then_inc(dg, 16)
            g.dma_start(T["a_mrows"][:].rearrange("p a b -> p (a b)"), A["mrows"]).then_inc(dg, 16)
            n += 32
            g.wait_ge(dg, n)
            g.memset(T["a_vtok"][:].rearrange("p a (h c) -> p a h c", c=65)[:, :, :, 64:65], 1.0)



def ph_l1attn(env):
    nc = env.nc
    T = env.t
    stg = env.sem("astg")
    stages = Stages(stg)
    ps3 = env.sem("aps3")
    pv3 = env.sem("apv3")
    se3 = env.sem("ase3")
    ve3 = env.sem("ave3")

    with ExitStack() as pctx:
        B = [pctx.enter_context(nc.psum_tensor(f"ab{i}", [P, STRIPE], F32))
             for i in range(7)]
        Bb6 = B[6].bitcast(BF16)

        def bvb(t):
            t.matmul(B[0][:], T["ones_row"][:], T["a_bv"][0:1, 0:STRIPE], start=True, stop=True)
            return t.matmul(B[1][:], T["ones_row"][:], T["a_bv"][0:1, STRIPE:D],
                            start=True, stop=True)
        stages.add("tensor", bvb)

        def bvbe(v):
            env.step("vector", v.tensor_copy(T["a_bvbc"][:, 0:STRIPE], B[0][:]))
            return v.tensor_copy(T["a_bvbc"][:, STRIPE:D], B[1][:])
        stages.add("vector", bvbe)

        for s in range(4):
            t0s = s * STRIPE

            def cast(g, t0s=t0s):
                for dch in range(DCH):
                    ins = g.tensor_copy(T["a_xn"][:, dch, :],
                                        T["x_res"][:, dch, t0s:t0s + STRIPE])
                return ins
            stages.add("gpsimd", cast)

            def statsx(t):
                for dch in range(DCH):
                    ins = t.matmul(B[0][0:32, :], T["ones32"][:], T["a_xn"][:, dch, :],
                                   start=(dch == 0), stop=(dch == DCH - 1))
                return ins
            stages.add("tensor", statsx)
            for half, bi in ((0, 1), (1, 2)):
                def sq(sc, half=half):
                    for j in range(4):
                        ins = sc.activation(T["a_sq"][:, j, :], T["a_xn"][:, half * 4 + j, :],
                                            AF.Square)
                    return ins
                stages.add("scalar", sq)

                def stats(t, bi=bi):
                    for j in range(4):
                        ins = t.matmul(B[bi][0:32, :], T["ones32"][:], T["a_sq"][:, j, :],
                                       start=(j == 0), stop=(j == 3))
                    return ins
                stages.add("tensor", stats)

            def rows1(v):
                env.step("vector", v.tensor_copy(T["a_row_var"][0:1, :], B[1][0:1, :]))
                env.fence(v, "vector")
                env.step("vector", v.tensor_add(T["a_row_var"][0:1, :], T["a_row_var"][0:1, :], B[2][0:1, :]))
                env.step("vector", v.tensor_scalar(T["a_row_negm"][0:1, :], B[0][0:1, :], -1.0 / D, None, ALU.mult))
                env.fence(v, "vector")
                env.step("vector", v.tensor_tensor(T["a_row_rstd"][0:1, :], T["a_row_negm"][0:1, :],
                                                   T["a_row_negm"][0:1, :], ALU.mult))
                env.step("vector", v.tensor_scalar(T["a_row_var"][0:1, :], T["a_row_var"][0:1, :], 1.0 / D, None, ALU.mult))
                env.fence(v, "vector")
                env.step("vector", v.tensor_sub(T["a_row_var"][0:1, :], T["a_row_var"][0:1, :], T["a_row_rstd"][0:1, :]))
                return v.tensor_copy(T["a_negm_bf"][0:1, :], T["a_row_negm"][0:1, :])
            stages.add("vector", rows1)

            def rstd(sc):
                env.step("scalar", sc.activation(T["a_row_rstd"][0:1, :], T["a_row_var"][0:1, :], AF.Ln,
                                                 bias=T["consts"][0:1, C_LNEPS:C_LNEPS + 1], scale=1.0))
                env.fence(sc, "scalar")
                return sc.activation(T["a_row_var"][0:1, :], T["a_row_rstd"][0:1, :], AF.Exp,
                                     bias=T["consts"][0:1, C_ZERO:C_ZERO + 1],
                                     scale=T["consts"][0:1, C_NEGHALF:C_NEGHALF + 1])
            stages.add("scalar", rstd)

            def rows2b(v):
                return v.tensor_copy(T["a_rstd_bf"][0:1, :], T["a_row_var"][0:1, :])
            stages.add("vector", rows2b)

            def bc(t):
                t.matmul(B[3][:], T["ones_row"][:], T["a_rstd_bf"][0:1, :], start=True, stop=True)
                return t.matmul(B[4][:], T["ones_row"][:], T["a_negm_bf"][0:1, :], start=True, stop=True)
            stages.add("tensor", bc)

            def bcevac(v):
                env.step("vector", v.tensor_copy(T["a_rstd_bc"][:], B[3][:]))
                return v.tensor_copy(T["a_mbc"][:], B[4][:])
            stages.add("vector", bcevac)

            def xn(v, t0s=t0s):
                for dch in range(DCH):
                    env.fence(v, "vector")
                    env.step("vector", v.tensor_add(T["a_mbc2"][:],
                                                    T["x_res"][:, dch, t0s:t0s + STRIPE],
                                                    T["a_mbc"][:]))
                    env.fence(v, "vector")
                    ins = v.tensor_mul(T["a_xn"][:, dch, :], T["a_mbc2"][:], T["a_rstd_bc"][:])
                    if dch < DCH - 1:
                        env.step("vector", ins)
                return ins
            stages.add("vector", xn)

            for wname, bname, dst in (("a_wq", "a_bq", "a_qT"), ("a_wk", "a_bk", "a_kT")):
                for j in range(DCH):
                    def proj(t, wname=wname, bname=bname, j=j):
                        bsel = B[j % 2]
                        for dch in range(DCH):
                            t.matmul(bsel[:], T[wname][:, dch, j * P:(j + 1) * P],
                                     T["a_xn"][:, dch, :], start=(dch == 0), stop=False)
                        return t.matmul(bsel[:], T[bname][0:1, j, :], T["ones_row512"][0:1, :],
                                        start=False, stop=True)
                    stages.add("tensor", proj)

                    def pevac(v, dst=dst, j=j):
                        return v.tensor_copy(T[dst][:, j, :], B[j % 2][:])
                    stages.add("vector", pevac)

            for tt in range(4):
                for half in range(2):
                    def vproj(t, tt=tt, half=half, t0s=t0s):
                        bsel = B[(tt * 2 + half) % 2]
                        for dch in range(DCH):
                            ins = t.matmul(bsel[:], T["a_xn"][:, dch, tt * P:(tt + 1) * P],
                                           T["a_wv"][:, dch, half * STRIPE:(half + 1) * STRIPE],
                                           start=(dch == 0), stop=(dch == DCH - 1))
                        return ins
                    stages.add("tensor", vproj)

                    def vevac(v, tt=tt, half=half):
                        bsel = B[(tt * 2 + half) % 2]
                        env.step("vector", v.tensor_add(T["a_mbc2"][:], bsel[:],
                                                        T["a_bvbc"][:, half * STRIPE:(half + 1) * STRIPE]))
                        env.fence(v, "vector")
                        dstv = T["a_vtok"][:, tt, :].rearrange("p (h c) -> p h c", c=65)
                        h0 = half * 8
                        return v.tensor_copy(dstv[:, h0:h0 + 8, 0:64],
                                             T["a_mbc2"][:].rearrange("p (h c) -> p h c", c=64))
                    stages.add("vector", vevac)

            def attn_pe(t, s=s):
                for u in range(64):
                    U = s * 64 + u + 1
                    tt, h = u // 16, u % 16
                    hb = (h % 2) * 64
                    sbank = B[5] if u % 2 == 0 else B[6]
                    pbank = B[4] if u % 2 == 0 else B[3]
                    if U > 2:
                        t.wait_ge(se3, U - 2)
                    ksub = T["a_kT"][hb:hb + 64, h // 2, tt * P:(tt + 1) * P]
                    qsub = T["a_qT"][hb:hb + 64, h // 2, tt * P:(tt + 1) * P]
                    t.matmul(sbank[:, 0:P], ksub, qsub, start=True, stop=False,
                             tile_position=(hb, 0))
                    t.matmul(sbank[:, 0:P], T["a_mrows"][hb:hb + 1, 0, :],
                             T["a_mrows"][hb:hb + 1, 1, :], start=False, stop=False)
                    t.matmul(sbank[:, 0:P], T["a_mrows"][hb:hb + 1, 2, :],
                             T["a_mrows"][hb:hb + 1, 3, :], start=False,
                             stop=True).then_inc(ps3, 1)
                    t.wait_ge(se3, U)
                    if U > 2:
                        t.wait_ge(ve3, U - 2)
                    vsub = T["a_vtok"][:, tt, h * 65:h * 65 + 65]
                    t.matmul(pbank[:, 0:65], T["a_wexp"][:, u % 2, :], vsub,
                             start=True, stop=True).then_inc(pv3, 1)
                return t.nop()

            def attn_act(sc, s=s):
                for u in range(64):
                    U = s * 64 + u + 1
                    sbank = B[5] if u % 2 == 0 else B[6]
                    sc.wait_ge(ps3, U)
                    if U > 2:
                        sc.wait_ge(pv3, U - 2)
                    sc.activation(T["a_wexp"][:, u % 2, :], sbank[:, 0:P], AF.Exp,
                                  bias=T["consts"][:, C_ZERO:C_ZERO + 1],
                                  scale=T["consts"][:, C_EIGHTH:C_EIGHTH + 1]
                                  ).then_inc(se3, 1)
                return sc.nop()

            def attn_dve(v, s=s):
                for u in range(64):
                    U = s * 64 + u + 1
                    tt, h = u // 16, u % 16
                    pbank = B[4] if u % 2 == 0 else B[3]
                    v.wait_ge(pv3, U)
                    env.step("vector", v.reciprocal(T["a_iv"][:, 0:1], pbank[:, 64:65]))
                    env.fence(v, "vector")
                    v.tensor_single_scalar(
                        T["a_vals"][:, tt, h * 64:(h + 1) * 64],
                        pbank[:, 0:64],
                        T["a_iv"][:, 0:1], ALU.mult).then_inc(ve3, 1)
                return v.nop()

            stages.add_parallel([("tensor", attn_pe), ("scalar", attn_act),
                                 ("vector", attn_dve)])

            for tt in range(4):
                def vtr(t, tt=tt, s=s):
                    t.wait_ge(ve3, (s + 1) * 64)
                    for dch in range(DCH):
                        ins = t.transpose(Bb6[:, dch * P:(dch + 1) * P],
                                          T["a_vals"][:, tt, dch * P:(dch + 1) * P],
                                          T["id_bf"][:])
                    return ins
                stages.add("tensor", vtr)

                def vtre(v, tt=tt):
                    for dch in range(DCH):
                        ins = v.tensor_copy(T["a_valsT"][:, dch, tt * P:(tt + 1) * P],
                                            Bb6[:, dch * P:(dch + 1) * P])
                    return ins
                stages.add("vector", vtre)

            for och in range(DCH):
                def ymm(t, och=och):
                    yb = B[0] if och % 2 == 0 else B[1]
                    for dch in range(DCH):
                        t.matmul(yb[:], T["a_wo"][:, dch, och * P:(och + 1) * P],
                                 T["a_valsT"][:, dch, :], start=(dch == 0), stop=False)
                    return t.matmul(yb[:], T["a_bo"][0:1, och, :], T["ones_row512"][0:1, :],
                                    start=False, stop=True)
                stages.add("tensor", ymm)

                def yadd(v, och=och, t0s=t0s):
                    yb = B[0] if och % 2 == 0 else B[1]
                    dst = T["x_res"][:, och, t0s:t0s + STRIPE]
                    return v.tensor_add(dst, dst, yb[:])
                stages.add("vector", yadd)

        with nc.Block("attn1") as blk:
            stages.attach(blk, ("tensor", "scalar", "vector", "gpsimd"))


def build_l1attn(core):
    nc = bass.Bass("TRN2", target_bir_lowering=False)
    with ExitStack() as ctx:
        env = Env(nc, ctx)
        declare_l1attn_io(nc, env)
        env.mkchains()
        alloc_common_sbuf(env)
        alloc_l1attn_sbuf(env)
        dsc = env.sem("dsc")
        with nc.Block("c0") as blk:
            @blk.sync
            def _(s):
                s.dma_start(env.t["consts"][:], env.a["consts"]).then_inc(dsc, 16)
                s.dma_start(env.t["id_f32"][:], env.a["ident"]).then_inc(dsc, 16)
                s.wait_ge(dsc, 32)
            @blk.gpsimd
            def _(g):
                g.dma_start(env.t["id_bf"][:], env.a["ident"]).then_inc(dsc, 16)
                g.wait_ge(dsc, 48)
                g.memset(env.t["ones_row"][:], 1.0)
                g.memset(env.t["ones_col"][:], 1.0)
                g.memset(env.t["ones32"][:], 1.0)
                g.memset(env.t["ones_row512"][:], 1.0)
        ph_load_xres(env)
        ph_l1attn_setup(env)
        ph_l1attn(env)
        ph_dump_xres(env)
    return nc





NE1, NFG1, NFC1 = 8, 4, 8


def declare_l1ffn_io(nc, env, io_xres=True):
    A = env.a
    A["Wg"] = nc.dram_tensor("Wg", [NE1, D, 4096], F32, kind="ExternalInput").ap()
    A["Wu"] = nc.dram_tensor("Wu", [NE1, D, 4096], F32, kind="ExternalInput").ap()
    A["Wd"] = nc.dram_tensor("Wd", [NE1, 4096, D], F32, kind="ExternalInput").ap()
    if io_xres:
        A["xres_in"] = nc.dram_tensor("xres_in", [D, TOK], F32, kind="ExternalInput").ap()
        A["consts"] = nc.dram_tensor("consts", [P, 8], F32, kind="ExternalInput").ap()
        A["ident"] = nc.dram_tensor("ident", [P, P], F32, kind="ExternalInput").ap()
        A["xres_out"] = nc.dram_tensor("xres_out", [D, TOK], F32, kind="ExternalOutput").ap()


def alloc_l1ffn_sbuf(env):
    sb = env.sb
    sb("z_xn", [P, DCH, TOK], BF16)
    sb("z_hT", [P, NFC1, STRIPE], BF16)
    sb("z_wgu", [P, 2, 2, DCH, 1024], BF16)
    sb("z_wd", [P, DCH, D], BF16)
    sb("z_tmp", [P, 2, STRIPE], BF16)
    sb("z_sq", [P, 4, STRIPE], BF16)
    sb("z_mbc", [P, STRIPE], F32)
    sb("z_mbc2", [P, STRIPE], F32)
    sb("z_rstd_bc", [P, STRIPE], F32)
    sb("z_row_negm", [1, STRIPE], F32)
    sb("z_row_var", [1, STRIPE], F32)
    sb("z_row_rstd", [1, STRIPE], F32)
    sb("z_negm_bf", [1, STRIPE], BF16)
    sb("z_rstd_bf", [1, STRIPE], BF16)


def ph_l1ffn_xn(env):
    nc = env.nc
    T = env.t
    stg = env.sem("zstg")
    stages = Stages(stg)
    with ExitStack() as pctx:
        B = [pctx.enter_context(nc.psum_tensor(f"zb{i}", [P, STRIPE], F32))
             for i in range(5)]
        for s in range(4):
            t0s = s * STRIPE

            def cast(g, t0s=t0s):
                for dch in range(DCH):
                    ins = g.tensor_copy(T["z_xn"][:, dch, t0s:t0s + STRIPE],
                                        T["x_res"][:, dch, t0s:t0s + STRIPE])
                return ins
            stages.add("gpsimd", cast)

            def statsx(t, t0s=t0s):
                for dch in range(DCH):
                    ins = t.matmul(B[0][0:32, :], T["ones32"][:],
                                   T["z_xn"][:, dch, t0s:t0s + STRIPE],
                                   start=(dch == 0), stop=(dch == DCH - 1))
                return ins
            stages.add("tensor", statsx)
            for half, bi in ((0, 1), (1, 2)):
                def sq(sc, half=half, t0s=t0s):
                    for j in range(4):
                        ins = sc.activation(T["z_sq"][:, j, :],
                                            T["z_xn"][:, half * 4 + j, t0s:t0s + STRIPE],
                                            AF.Square)
                    return ins
                stages.add("scalar", sq)

                def stats(t, bi=bi):
                    for j in range(4):
                        ins = t.matmul(B[bi][0:32, :], T["ones32"][:], T["z_sq"][:, j, :],
                                       start=(j == 0), stop=(j == 3))
                    return ins
                stages.add("tensor", stats)

            def rows1(v):
                env.step("vector", v.tensor_copy(T["z_row_var"][0:1, :], B[1][0:1, :]))
                env.fence(v, "vector")
                env.step("vector", v.tensor_add(T["z_row_var"][0:1, :], T["z_row_var"][0:1, :], B[2][0:1, :]))
                env.step("vector", v.tensor_scalar(T["z_row_negm"][0:1, :], B[0][0:1, :], -1.0 / D, None, ALU.mult))
                env.fence(v, "vector")
                env.step("vector", v.tensor_tensor(T["z_row_rstd"][0:1, :], T["z_row_negm"][0:1, :],
                                                   T["z_row_negm"][0:1, :], ALU.mult))
                env.step("vector", v.tensor_scalar(T["z_row_var"][0:1, :], T["z_row_var"][0:1, :], 1.0 / D, None, ALU.mult))
                env.fence(v, "vector")
                env.step("vector", v.tensor_sub(T["z_row_var"][0:1, :], T["z_row_var"][0:1, :], T["z_row_rstd"][0:1, :]))
                return v.tensor_copy(T["z_negm_bf"][0:1, :], T["z_row_negm"][0:1, :])
            stages.add("vector", rows1)

            def rstd(sc):
                env.step("scalar", sc.activation(T["z_row_rstd"][0:1, :], T["z_row_var"][0:1, :], AF.Ln,
                                                 bias=T["consts"][0:1, C_LNEPS:C_LNEPS + 1], scale=1.0))
                env.fence(sc, "scalar")
                return sc.activation(T["z_row_var"][0:1, :], T["z_row_rstd"][0:1, :], AF.Exp,
                                     bias=T["consts"][0:1, C_ZERO:C_ZERO + 1],
                                     scale=T["consts"][0:1, C_NEGHALF:C_NEGHALF + 1])
            stages.add("scalar", rstd)

            def rows2b(v):
                return v.tensor_copy(T["z_rstd_bf"][0:1, :], T["z_row_var"][0:1, :])
            stages.add("vector", rows2b)

            def bc(t):
                t.matmul(B[3][:], T["ones_row"][:], T["z_rstd_bf"][0:1, :], start=True, stop=True)
                return t.matmul(B[4][:], T["ones_row"][:], T["z_negm_bf"][0:1, :], start=True, stop=True)
            stages.add("tensor", bc)

            def bcevac(v):
                env.step("vector", v.tensor_copy(T["z_rstd_bc"][:], B[3][:]))
                return v.tensor_copy(T["z_mbc"][:], B[4][:])
            stages.add("vector", bcevac)

            def xn(v, t0s=t0s):
                for dch in range(DCH):
                    env.fence(v, "vector")
                    env.step("vector", v.tensor_add(T["z_mbc2"][:],
                                                    T["x_res"][:, dch, t0s:t0s + STRIPE],
                                                    T["z_mbc"][:]))
                    env.fence(v, "vector")
                    ins = v.tensor_mul(T["z_xn"][:, dch, t0s:t0s + STRIPE], T["z_mbc2"][:],
                                       T["z_rstd_bc"][:])
                    if dch < DCH - 1:
                        env.step("vector", ins)
                return ins
            stages.add("vector", xn)

        with nc.Block("l1ffn_xn") as blk:
            stages.attach(blk, ("tensor", "scalar", "vector", "gpsimd"))


def ph_l1ffn(env):
    nc = env.nc
    T = env.t
    A = env.a
    wl = env.sem("zwl")
    peg = env.sem("zpeg")
    pey = env.sem("zpey")
    sse = env.sem("zsse")
    vh = env.sem("zvh")
    vy = env.sem("zvy")

    NG = NE1 * NFG1

    with ExitStack() as pctx:
        pg = [pctx.enter_context(nc.psum_tensor(f"zpg{i}", [P, STRIPE], F32)) for i in range(2)]
        pu = [pctx.enter_context(nc.psum_tensor(f"zpu{i}", [P, STRIPE], F32)) for i in range(2)]
        py = [pctx.enter_context(nc.psum_tensor(f"zpy{i}", [P, STRIPE], F32)) for i in range(2)]

        with nc.Block("l1ffn") as blk:
            @blk.gpsimd
            def _(g):
                for G in range(NG):
                    e, fg = G // NFG1, G % NFG1
                    if G >= 2:
                        g.wait_ge(peg, (G - 1) * 32)
                    g.dma_start(T["z_wgu"][:, G % 2, 0, :, :],
                                A["Wg"][e, :, fg * 1024:(fg + 1) * 1024]
                                .rearrange("(c p) f -> p c f", p=P)).then_inc(wl, 16)
                    g.dma_start(T["z_wgu"][:, G % 2, 1, :, :],
                                A["Wu"][e, :, fg * 1024:(fg + 1) * 1024]
                                .rearrange("(c p) f -> p c f", p=P)).then_inc(wl, 16)
                    if G >= 1:
                        g.wait_ge(pey, G * 32)
                    g.dma_start(T["z_wd"][:],
                                A["Wd"][e, fg * 1024:(fg + 1) * 1024, :]
                                .rearrange("(c p) o -> p c o", p=P)).then_inc(wl, 16)

            @blk.tensor
            def _(t):
                ngu = 0
                ny = 0
                for G in range(NG):
                    t.wait_ge(wl, G * 48 + 32)
                    for s in range(4):
                        t0s = s * STRIPE
                        for fc in range(NFC1):
                            par = ngu % 2
                            if ngu >= 2:
                                t.wait_ge(sse, ngu - 1)
                                t.wait_ge(vh, ngu - 1)
                            for dch in range(DCH):
                                t.matmul(pg[par][:], T["z_wgu"][:, G % 2, 0, dch, fc * P:(fc + 1) * P],
                                         T["z_xn"][:, dch, t0s:t0s + STRIPE],
                                         start=(dch == 0), stop=(dch == DCH - 1))
                            for dch in range(DCH):
                                ins = t.matmul(pu[par][:], T["z_wgu"][:, G % 2, 1, dch, fc * P:(fc + 1) * P],
                                               T["z_xn"][:, dch, t0s:t0s + STRIPE],
                                               start=(dch == 0), stop=(dch == DCH - 1))
                            ins.then_inc(peg, 1)
                            ngu += 1
                        if s == 0:
                            t.wait_ge(wl, 48 * (G + 1))
                        t.wait_ge(vh, ngu)
                        for och in range(DCH):
                            ypar = ny % 2
                            if ny >= 2:
                                t.wait_ge(vy, ny - 1)
                            for fc in range(NFC1):
                                ins = t.matmul(py[ypar][:], T["z_wd"][:, fc, och * P:(och + 1) * P],
                                               T["z_hT"][:, fc, :],
                                               start=(fc == 0), stop=(fc == NFC1 - 1))
                            ins.then_inc(pey, 1)
                            ny += 1

            @blk.scalar
            def _(sc):
                ngu = 0
                for G in range(NG):
                    for s in range(4):
                        for fc in range(NFC1):
                            ngu += 1
                            sc.wait_ge(peg, ngu)
                            if ngu >= 3:
                                sc.wait_ge(vh, ngu - 2)
                            sc.activation(T["z_tmp"][:, (ngu - 1) % 2, :], pg[(ngu - 1) % 2][:],
                                          AF.Silu).then_inc(sse, 1)

            @blk.vector
            def _(v):
                ngu = 0
                ny = 0
                for G in range(NG):
                    for s in range(4):
                        t0s = s * STRIPE
                        for fc in range(NFC1):
                            ngu += 1
                            v.wait_ge(sse, ngu)
                            v.tensor_tensor(T["z_hT"][:, fc, :], T["z_tmp"][:, (ngu - 1) % 2, :],
                                            pu[(ngu - 1) % 2][:], ALU.mult).then_inc(vh, 1)
                        for och in range(DCH):
                            ny += 1
                            v.wait_ge(pey, ny)
                            dst = T["x_res"][:, och, t0s:t0s + STRIPE]
                            v.tensor_add(dst, dst, py[(ny - 1) % 2][:]).then_inc(vy, 1)


def build_l1ffn(core):
    nc = bass.Bass("TRN2", target_bir_lowering=False)
    with ExitStack() as ctx:
        env = Env(nc, ctx)
        declare_l1ffn_io(nc, env)
        env.mkchains()
        alloc_common_sbuf(env)
        alloc_l1ffn_sbuf(env)
        dsc = env.sem("dsc")
        with nc.Block("c0") as blk:
            @blk.sync
            def _(s):
                s.dma_start(env.t["consts"][:], env.a["consts"]).then_inc(dsc, 16)
                s.dma_start(env.t["id_f32"][:], env.a["ident"]).then_inc(dsc, 16)
                s.wait_ge(dsc, 32)
            @blk.gpsimd
            def _(g):
                g.memset(env.t["ones_row"][:], 1.0)
                g.memset(env.t["ones_col"][:], 1.0)
                g.memset(env.t["ones32"][:], 1.0)
                g.memset(env.t["ones_row512"][:], 1.0)
        ph_load_xres_nold(env)
        ph_l1ffn_xn(env)
        ph_l1ffn(env)
        ph_dump_xres(env)
    return nc


def ph_load_xres_nold(env):
    nc = env.nc
    T = env.t
    A = env.a
    dl = env.sem("dlx")
    with nc.Block("loadx") as blk:
        @blk.sync
        def _(s):
            s.dma_start(T["x_res"][:],
                        A["xres_in"].rearrange("(c p) t -> p c t", p=P)).then_inc(dl, 16)
            s.wait_ge(dl, 16)


def _consts_arr():
    f32 = np.float32
    consts = np.zeros((P, 8), f32)
    consts[:, C_ONE] = 1.0
    consts[:, C_NEGSQRT] = -SQRT_HD
    consts[:, C_INVSQRT] = 1.0 / SQRT_HD
    consts[:, C_LNEPS] = 1e-5
    consts[:, C_NEGHALF] = -0.5
    consts[:, C_HALF_PI] = math.pi / 2.0
    consts[:, C_EIGHTH] = 0.125
    return consts


def prep_l0ffn_inputs(I, xres_in):
    f32 = np.float32
    sAT = np.zeros((D, 32), f32)
    sAT[:, :16] = np.asarray(I["L0_ffn_shared_A"], f32).T
    rWT = np.zeros((D, 32), f32)
    rWT[:, :6] = np.asarray(I["L0_ffn_router_W"], f32).T
    rb = np.zeros((1, 32), f32)
    rb[0, :6] = np.asarray(I["L0_ffn_router_b"], f32)
    return {
        "bwT": np.ascontiguousarray(np.asarray(I["L0_ffn_base_W"], f32).T),
        "sAT": sAT,
        "sBT": np.ascontiguousarray(np.asarray(I["L0_ffn_shared_B"], f32).T),
        "rAT": np.ascontiguousarray(np.asarray(I["L0_ffn_routed_A"], f32).transpose(2, 0, 1).reshape(D, 96)),
        "rBT": np.ascontiguousarray(np.asarray(I["L0_ffn_routed_B"], f32).transpose(0, 2, 1).reshape(96, D)),
        "rWT": rWT,
        "rb_row": rb,
        "bb_row": np.ascontiguousarray(np.asarray(I["L0_ffn_base_b"], f32).reshape(1, D)),
        "xres_in": xres_in,
        "consts": _consts_arr(),
        "ident": np.eye(P, dtype=f32),
    }


def _mask_rows():
    f32 = np.float32
    m = np.zeros((P, 4, P), f32)
    m[:, 0, 64:128] = -1e9
    m[:, 1, 0:64] = 1.0
    m[:, 2, 0:64] = -1e9
    m[:, 3, 64:128] = 1.0
    return np.ascontiguousarray(m.reshape(P, 4 * P))


def prep_l1attn_inputs(I, xres_in):
    f32 = np.float32
    w = np.asarray(I["L1_attn_w_qkv"], f32).reshape(D, 16, 192)
    b = np.asarray(I["L1_attn_b_qkv"], f32).reshape(16, 192)
    return {
        "wqc": np.ascontiguousarray(w[:, :, 0:64].reshape(D, D)),
        "wkc": np.ascontiguousarray(w[:, :, 64:128].reshape(D, D)),
        "wvc": np.ascontiguousarray(w[:, :, 128:192].reshape(D, D)),
        "woc": np.ascontiguousarray(np.asarray(I["L1_attn_w_o"], f32)),
        "bq_row": np.ascontiguousarray(b[:, 0:64].reshape(1, D)),
        "bk_row": np.ascontiguousarray(b[:, 64:128].reshape(1, D)),
        "bv_row": np.ascontiguousarray(b[:, 128:192].reshape(1, D)),
        "boc_row": np.ascontiguousarray(np.asarray(I["L1_attn_b_o"], f32).reshape(1, D)),
        "mrows": _mask_rows(),
        "xres_in": xres_in,
        "consts": _consts_arr(),
        "ident": np.eye(P, dtype=f32),
    }


def prep_l1ffn_inputs(I, xres_in):
    f32 = np.float32
    return {
        "Wg": np.ascontiguousarray(np.asarray(I["L1_ffn_Wg"], f32)),
        "Wu": np.ascontiguousarray(np.asarray(I["L1_ffn_Wu"], f32)),
        "Wd": np.ascontiguousarray(np.asarray(I["L1_ffn_Wd"], f32)),
        "xres_in": xres_in,
        "consts": _consts_arr(),
        "ident": np.eye(P, dtype=f32),
    }


def prep_full(I, core):
    m = prep_l0attn_inputs(I, core)
    for d in (prep_l0ffn_inputs(I, None), prep_l1attn_inputs(I, None),
              prep_l1ffn_inputs(I, None)):
        d.pop("xres_in")
        d.pop("consts", None)
        d.pop("ident", None)
        m.update(d)
    return m


def build_full():
    nc = bass.Bass("TRN2", target_bir_lowering=False)
    with ExitStack() as ctx:
        env = Env(nc, ctx)
        declare_l0attn_io(nc, env)
        declare_l0ffn_io(nc, env, io_xres=False)
        declare_l1attn_io(nc, env, io_xres=False)
        declare_l1ffn_io(nc, env, io_xres=False)
        env.mkchains()
        alloc_common_sbuf(env)
        with env.scope():
            alloc_l0attn_sbuf(env)
            ph_l0_setup(env)
            ph_l0_cols(env)
            ph_l0_xres_init(env)
            ph_l0_prepass(env)
            ph_l0_scores(env)
            ph_l0_epilogue(env)
        with env.scope():
            alloc_l0ffn_sbuf(env)
            ph_l0ffn_setup(env)
            ph_l0ffn(env)
        with env.scope():
            alloc_l1attn_sbuf(env)
            ph_l1attn_setup(env)
            ph_l1attn(env)
        with env.scope():
            alloc_l1ffn_sbuf(env)
            ph_l1ffn_xn(env)
            ph_l1ffn(env)
        ph_dump_xres(env)
    return nc


def kernel(**inputs):
    cores = list(range(8))
    nc = build_full()
    r = run_bass_kernel_spmd(nc, [prep_full(inputs, c) for c in cores], core_ids=cores)
    out = np.empty((2, SEQ, D), np.float32)
    for c in cores:
        b, qoff = c // 4, (c % 4) * TOK
        out[b, qoff:qoff + TOK, :] = np.asarray(r.results[c]["xres_out"], np.float32).T
    return out


# version 06
# speedup vs baseline: 1.1019x; 1.1019x over previous
import math
import numpy as np
import ml_dtypes
import concourse.bass as bass
import concourse.mybir as mybir
from concourse.bass_utils import run_bass_kernel_spmd
from contextlib import ExitStack

F32 = mybir.dt.float32
BF16 = mybir.dt.bfloat16
AF = mybir.ActivationFunctionType
ALU = mybir.AluOpType
AX = mybir.AxisListType

P = 128
D = 1024
DCH = 8
SEQ = 8192
TOK = 2048
NST = 16
STRIPE = 512
MAGIC = 12582912.0
TWO_PI = 2.0 * math.pi
INV_2PI = 1.0 / TWO_PI
SQRT_HD = math.sqrt(128.0)

DBG_SKIP_ROWSUMS = False
DBG_L0_DUMPS = False
DBG_NQS = 4
DBG_F0_DUMPS = False
DBG_SKIP_COLSV = False
C_ZERO, C_ONE, C_NEGSQRT, C_INVSQRT, C_LNEPS, C_NEGHALF, C_HALF_PI, C_EIGHTH = range(8)
DC_INVW, DC_TBK, DC_TBKC, DC_MCK, DC_MCKC, DC_TBQ, DC_TBQC, DC_MCQ, DC_MCQC, DC_TMP = range(10)



class Stages:

    def __init__(self, sem):
        self.sem = sem
        self.items = []

    def add(self, eng, fn):
        self.items.append([(eng, fn)])
        return len(self.items) - 1

    def add_parallel(self, members):
        self.items.append(list(members))
        return len(self.items) - 1

    def run(self, section, name):
        base = 0
        for grp in self.items:
            for eng, fn in grp:
                if eng == name:
                    section.wait_ge(self.sem, base)
                    last = fn(section)
                    last.then_inc(self.sem, 1)
            base += len(grp)

    def stage_end_count(self, idx):
        return sum(len(g) for g in self.items[:idx + 1])

    def attach(self, blk, engines=("tensor", "scalar", "vector", "gpsimd", "sync")):
        st = self
        if "tensor" in engines:
            @blk.tensor
            def _(t):
                st.run(t, "tensor")
        if "scalar" in engines:
            @blk.scalar
            def _(sc):
                st.run(sc, "scalar")
        if "vector" in engines:
            @blk.vector
            def _(v):
                st.run(v, "vector")
        if "gpsimd" in engines:
            @blk.gpsimd
            def _(g):
                st.run(g, "gpsimd")
        if "sync" in engines:
            @blk.sync
            def _(s):
                st.run(s, "sync")


class Env:
    def __init__(self, nc, ctx):
        self.nc = nc
        self.ctx = ctx
        self.scopes = [ctx]
        self.t = {}
        self.a = {}
        self.core = 0
        self.qoff = 0

    def scope(self):
        from contextlib import contextmanager

        @contextmanager
        def _s():
            es = ExitStack()
            self.scopes.append(es)
            try:
                with es:
                    yield es
            finally:
                self.scopes.pop()
        return _s()

    def sb(self, name, shape, dtype):
        self.t[name] = self.scopes[-1].enter_context(self.nc.sbuf_tensor(name + "_s", shape, dtype))
        return self.t[name]

    def sem(self, name):
        self.t[name] = self.ctx.enter_context(self.nc.semaphore(name))
        return self.t[name]

    def mkchains(self):
        self.chn = {}
        for nm in ("vector", "scalar", "gpsimd"):
            self.chn[nm] = [self.sem("ch_" + nm), 0]

    def fence(self, sec, nm):
        sem, n = self.chn[nm]
        if n:
            sec.wait_ge(sem, n)

    def step(self, nm, ins):
        e = self.chn[nm]
        ins.then_inc(e[0], 1)
        e[1] += 1
        return ins



def declare_l0attn_io(nc, env):
    A = env.a
    A["xb"] = nc.dram_tensor("xb", [SEQ, D], F32, kind="ExternalInput").ap()
    A["wqT"] = nc.dram_tensor("wqT", [D, P], F32, kind="ExternalInput").ap()
    A["wkT"] = nc.dram_tensor("wkT", [D, P], F32, kind="ExternalInput").ap()
    A["wvT"] = nc.dram_tensor("wvT", [D, P], F32, kind="ExternalInput").ap()
    A["woT"] = nc.dram_tensor("woT", [P, D], F32, kind="ExternalInput").ap()
    A["cols0"] = nc.dram_tensor("cols0", [P, 8], F32, kind="ExternalInput").ap()
    A["bo_row"] = nc.dram_tensor("bo_row", [1, D], F32, kind="ExternalInput").ap()
    A["consts"] = nc.dram_tensor("consts", [P, 8], F32, kind="ExternalInput").ap()
    A["ident"] = nc.dram_tensor("ident", [P, P], F32, kind="ExternalInput").ap()
    A["xq"] = nc.dram_tensor("xq", [TOK, D], F32, kind="ExternalInput").ap()
    A["xb16"] = nc.dram_tensor("xb16", [SEQ, D], BF16, kind="Internal").ap()
    A["xq16"] = nc.dram_tensor("xq16", [TOK, D], BF16, kind="Internal").ap()
    A["xres_out"] = nc.dram_tensor("xres_out", [D, TOK], F32, kind="ExternalOutput").ap()
    if DBG_L0_DUMPS:
        A["d_cosK"] = nc.dram_tensor("d_cosK", [P, SEQ], BF16, kind="ExternalOutput").ap()
        A["d_sinK"] = nc.dram_tensor("d_sinK", [P, SEQ], BF16, kind="ExternalOutput").ap()
        A["d_cosQ"] = nc.dram_tensor("d_cosQ", [P, TOK], BF16, kind="ExternalOutput").ap()
        A["d_sinQ"] = nc.dram_tensor("d_sinQ", [P, TOK], BF16, kind="ExternalOutput").ap()
        A["d_vtok"] = nc.dram_tensor("d_vtok", [P, 64 * P], BF16, kind="ExternalOutput").ap()
        A["d_attn"] = nc.dram_tensor("d_attn", [P, TOK], F32, kind="ExternalOutput").ap()
        A["d_den"] = nc.dram_tensor("d_den", [1, TOK], F32, kind="ExternalOutput").ap()
        A["d_dcols"] = nc.dram_tensor("d_dcols", [P, 12], F32, kind="ExternalOutput").ap()
        A["d_rs"] = nc.dram_tensor("d_rs", [1, 3 * P], BF16, kind="ExternalOutput").ap()


def alloc_common_sbuf(env):
    sb = env.sb
    sb("x_res", [P, DCH, TOK], F32)
    sb("consts", [P, 8], F32)
    sb("id_f32", [P, P], F32)
    sb("id_bf", [P, P], BF16)
    sb("ones_row", [1, P], BF16)
    sb("ones_col", [P, 1], BF16)
    sb("ones32", [P, 32], BF16)
    sb("ones_row512", [1, STRIPE], BF16)


def alloc_l0attn_sbuf(env):
    sb = env.sb
    sb("cosK", [P, SEQ], BF16)
    sb("sinK", [P, SEQ], BF16)
    sb("v_tok", [P, 64, P], BF16)
    sb("cosQ", [P, TOK], BF16)
    sb("sinQ", [P, TOK], BF16)
    sb("wq_sb", [P, DCH, P], BF16)
    sb("wk_sb", [P, DCH, P], BF16)
    sb("wv_sb", [P, DCH, P], BF16)
    sb("wo_sb", [P, D], BF16)
    sb("bo_bf", [1, DCH, P], BF16)
    sb("cols0", [P, 8], F32)
    sb("dcols", [P, 12], F32)
    sb("rs_bf", [1, 3, P], BF16)
    sb("attn_acc", [P, TOK], F32)
    sb("denom_acc", [1, TOK], F32)
    sb("w_buf", [P, 4, TOK], BF16)
    sb("xt", [P, 2, DCH, STRIPE], BF16)
    sb("sq_bf", [P, 4, STRIPE], BF16)
    sb("row_negm", [1, STRIPE], F32)
    sb("row_var", [1, STRIPE], F32)
    sb("row_rstd", [1, STRIPE], F32)
    sb("row_negm_bf", [1, STRIPE], BF16)
    sb("row_rstd_bf", [1, STRIPE], BF16)
    sb("rstd_bc", [P, STRIPE], F32)
    sb("th", [P, STRIPE], F32)
    sb("uu", [P, STRIPE], F32)
    sb("si_c", [P, STRIPE], F32)
    sb("si_s", [P, STRIPE], F32)
    sb("vT_bf", [P, STRIPE], BF16)
    sb("dinv_bf", [1, STRIPE], BF16)


def ph_l0_setup(env):
    nc = env.nc
    T = env.t
    A = env.a
    ds = env.sem("ds0")
    dg = env.sem("dg0")
    gx = env.sem("gx")
    cs = env.sem("cs0")
    T = env.t

    with nc.Block("setup") as blk:
        @blk.sync
        def _(s):
            s.dma_start(T["cols0"][:], A["cols0"]).then_inc(ds, 16)
            s.dma_start(T["consts"][:], A["consts"]).then_inc(ds, 16)
            s.dma_start(T["id_f32"][:], A["ident"]).then_inc(ds, 16)
            s.wait_ge(ds, 48)

        @blk.gpsimd
        def _(g):
            g.dma_start(T["wq_sb"][:], A["wqT"].rearrange("(c p) h -> p c h", p=P)).then_inc(dg, 16)
            g.dma_start(T["wk_sb"][:], A["wkT"].rearrange("(c p) h -> p c h", p=P)).then_inc(dg, 16)
            g.dma_start(T["wv_sb"][:], A["wvT"].rearrange("(c p) h -> p c h", p=P)).then_inc(dg, 16)
            g.dma_start(T["wo_sb"][:], A["woT"]).then_inc(dg, 16)
            g.dma_start(T["bo_bf"][:], A["bo_row"].rearrange("one (c p) -> one c p", p=P)).then_inc(dg, 16)
            g.dma_start(T["id_bf"][:], A["ident"]).then_inc(dg, 16)
            g.wait_ge(dg, 96)
            g.memset(T["ones_row"][:], 1.0)
            g.memset(T["ones_col"][:], 1.0)
            g.memset(T["ones32"][:], 1.0)
            g.memset(T["ones_row512"][:], 1.0)
            g.memset(T["attn_acc"][:], 0.0)
            g.memset(T["denom_acc"][:], 0.0)

    return


def ph_l0_cols(env):
    nc = env.nc
    T = env.t
    A = env.a
    gx = T["gx"]
    cs = T["cs0"]
    with nc.Block("cols") as blk:
        @blk.vector
        def _(v):
            if DBG_SKIP_COLSV:
                return
            dc = T["dcols"]
            c0 = T["cols0"]
            v.tensor_scalar(dc[:, DC_TMP:DC_TMP + 1], c0[:, 0:1], 1e-8, None, ALU.add).then_inc(cs, 1)
            v.wait_ge(cs, 1)
            v.reciprocal(dc[:, DC_INVW:DC_INVW + 1], dc[:, DC_TMP:DC_TMP + 1]).then_inc(cs, 1)
            v.wait_ge(cs, 2)
            v.tensor_tensor(dc[:, DC_TBK:DC_TBK + 1], dc[:, DC_INVW:DC_INVW + 1], c0[:, 3:4], ALU.mult)
            v.tensor_tensor(dc[:, DC_TBQ:DC_TBQ + 1], dc[:, DC_INVW:DC_INVW + 1], c0[:, 2:3], ALU.mult).then_inc(cs, 1)
            v.wait_ge(cs, 3)
            v.tensor_tensor(dc[:, DC_TBK:DC_TBK + 1], dc[:, DC_TBK:DC_TBK + 1], c0[:, 1:2], ALU.add)
            v.tensor_tensor(dc[:, DC_TBQ:DC_TBQ + 1], dc[:, DC_TBQ:DC_TBQ + 1], c0[:, 1:2], ALU.add).then_inc(cs, 1)
            v.wait_ge(cs, 4)
            v.tensor_scalar(dc[:, DC_TBKC:DC_TBKC + 1], dc[:, DC_TBK:DC_TBK + 1], math.pi / 2, None, ALU.add)
            v.tensor_scalar(dc[:, DC_MCK:DC_MCK + 1], dc[:, DC_TBK:DC_TBK + 1], INV_2PI, MAGIC, ALU.mult, ALU.add)
            v.tensor_scalar(dc[:, DC_TBQC:DC_TBQC + 1], dc[:, DC_TBQ:DC_TBQ + 1], math.pi / 2, None, ALU.add)
            v.tensor_scalar(dc[:, DC_MCQ:DC_MCQ + 1], dc[:, DC_TBQ:DC_TBQ + 1], INV_2PI, MAGIC, ALU.mult, ALU.add).then_inc(cs, 1)
            v.wait_ge(cs, 5)
            v.tensor_scalar(dc[:, DC_MCKC:DC_MCKC + 1], dc[:, DC_TBKC:DC_TBKC + 1], INV_2PI, MAGIC, ALU.mult, ALU.add)
            v.tensor_scalar(dc[:, DC_MCQC:DC_MCQC + 1], dc[:, DC_TBQC:DC_TBQC + 1], INV_2PI, MAGIC, ALU.mult, ALU.add)

    if DBG_SKIP_ROWSUMS:
        return
    with ExitStack() as pctx:
        prows = [pctx.enter_context(nc.psum_tensor(f"prow{i}", [32, STRIPE], F32))
                 for i in range(3)]
        with nc.Block("rowsums") as blk:
            @blk.tensor
            def _(t):
                for i, w in enumerate(("wq_sb", "wk_sb", "wv_sb")):
                    for dch in range(DCH):
                        ins = t.matmul(prows[i][0:32, 0:P], T["ones32"][:], T[w][:, dch, :],
                                       start=(dch == 0), stop=(dch == DCH - 1))
                ins.then_inc(cs, 1)

            @blk.vector
            def _(v):
                v.wait_ge(cs, 6)
                for i in range(3):
                    v.tensor_copy(T["rs_bf"][0:1, i, :], prows[i][0:1, 0:P])


def ph_l0_xres_init(env):
    nc = env.nc
    T = env.t
    A = env.a
    dsx = env.sem("dsx")
    stg = env.sem("stg1")
    stages = Stages(stg)
    stag = T["w_buf"].bitcast(F32)

    with ExitStack() as pctx:
        ptrf = pctx.enter_context(nc.psum_tensor("ptrf", [P, 8, P], F32))

        for st in range(8):
            def load(s, st=st):
                src = A["xq"][st * 256:(st + 1) * 256, :]
                s.dma_start(stag[:, 0:2, :], src.rearrange("(tt p) d -> p tt d", p=P)).then_inc(dsx, 16)
                s.wait_ge(dsx, (st + 1) * 16)
                return s.nop()
            stages.add("sync", load)
            for tt in range(2):
                def trans(t, tt=tt):
                    for dch in range(DCH):
                        ins = t.transpose(ptrf[:, dch, :],
                                          stag[:, tt, dch * P:(dch + 1) * P],
                                          T["id_f32"][:])
                    return ins
                stages.add("tensor", trans)

                def evac(v, st=st, tt=tt):
                    for dch in range(DCH):
                        t0 = st * 256 + tt * P
                        ins = v.tensor_copy(T["x_res"][:, dch, t0:t0 + P], ptrf[:, dch, :])
                    return ins
                stages.add("vector", evac)

        with nc.Block("xres_init") as blk:
            stages.attach(blk, ("sync", "tensor", "vector"))


def ph_l0_prepass(env):
    nc = env.nc
    T = env.t
    A = env.a
    gx = T["gx"]
    dt_ = env.sem("dt")
    stg = env.sem("stg2")
    stages = Stages(stg)

    with ExitStack() as pctx:
        psx = pctx.enter_context(nc.psum_tensor("psx", [32, STRIPE], F32))
        psx2 = pctx.enter_context(nc.psum_tensor("psx2", [32, STRIPE], F32))
        pbc = pctx.enter_context(nc.psum_tensor("pbc", [P, STRIPE], F32))
        pk = pctx.enter_context(nc.psum_tensor("pk", [P, STRIPE], F32))
        pv = pctx.enter_context(nc.psum_tensor("pv", [P, STRIPE], F32))
        pq = pctx.enter_context(nc.psum_tensor("pq", [P, STRIPE], F32))
        ptb = pctx.enter_context(nc.psum_tensor("ptb", [P, 8, P], BF16))

        def sqv(j):
            return T["w_buf"][:, j % 4, (j // 4) * STRIPE:(j // 4 + 1) * STRIPE]

        def emit_theta(v, psrc, tb_c, tb_s):
            env.step("vector", v.scalar_tensor_tensor(T["th"][:], psrc,
                                                      T["dcols"][:, DC_INVW:DC_INVW + 1],
                                                      T["rstd_bc"][:], ALU.mult, ALU.mult))
            for tb, si in ((tb_c, "si_c"), (tb_s, "si_s")):
                env.fence(v, "vector")
                env.step("vector", v.tensor_scalar(T[si][:], T["th"][:], 1.0,
                                                   T["dcols"][:, tb:tb + 1], ALU.mult, ALU.add))
                env.fence(v, "vector")
                env.step("vector", v.tensor_scalar(T["uu"][:], T[si][:], INV_2PI, MAGIC,
                                                   ALU.mult, ALU.add))
                env.step("vector", v.tensor_scalar(T["uu"][:], T["uu"][:], MAGIC, TWO_PI,
                                                   ALU.subtract, ALU.mult))
                env.fence(v, "vector")
                ins = v.tensor_sub(T[si][:], T[si][:], T["uu"][:])
                if si == "si_c":
                    env.step("vector", ins)
            return ins

        SKEW = 5
        slots = {}

        def put(t, eng, fn):
            slots.setdefault(t, []).append((eng, fn))

        proj_stage = {}
        for st in range(NST + DBG_NQS):
            kv = st < NST
            b = st % 2
            t0 = SKEW * st

            def sq(sc, st=st, b=b):
                sc.wait_ge(dt_, (st + 1) * 128)
                for j in range(DCH):
                    ins = sc.activation(sqv(j), T["xt"][:, b, j, :], AF.Square)
                return ins
            put(t0 + 1, "scalar", sq)

            def statsx(t, st=st, b=b):
                t.wait_ge(dt_, (st + 1) * 128)
                for dch in range(DCH):
                    t.matmul(psx[0:32, :], T["ones32"][:], T["xt"][:, b, dch, :],
                             start=(dch == 0), stop=(dch == DCH - 1))
                for j in range(DCH):
                    ins = t.matmul(psx2[0:32, :], T["ones32"][:], sqv(j),
                                   start=(j == 0), stop=(j == DCH - 1))
                return ins
            put(t0 + 2, "tensor", statsx)

            def rows1(v):
                env.step("vector", v.tensor_scalar(T["row_var"][0:1, :], psx2[0:1, :],
                                                   1.0 / D, None, ALU.mult))
                env.step("vector", v.tensor_scalar(T["row_negm"][0:1, :], psx[0:1, :],
                                                   -1.0 / D, None, ALU.mult))
                env.fence(v, "vector")
                env.step("vector", v.tensor_tensor(T["row_rstd"][0:1, :], T["row_negm"][0:1, :],
                                                   T["row_negm"][0:1, :], ALU.mult))
                env.fence(v, "vector")
                env.step("vector", v.tensor_sub(T["row_var"][0:1, :], T["row_var"][0:1, :],
                                                T["row_rstd"][0:1, :]))
                return v.tensor_copy(T["row_negm_bf"][0:1, :], T["row_negm"][0:1, :])
            put(t0 + 3, "vector", rows1)

            def rstd(sc):
                env.step("scalar", sc.activation(T["row_rstd"][0:1, :], T["row_var"][0:1, :], AF.Ln,
                              bias=T["consts"][0:1, C_LNEPS:C_LNEPS + 1], scale=1.0))
                env.fence(sc, "scalar")
                return sc.activation(T["row_var"][0:1, :], T["row_rstd"][0:1, :], AF.Exp,
                                     bias=T["consts"][0:1, C_ZERO:C_ZERO + 1],
                                     scale=T["consts"][0:1, C_NEGHALF:C_NEGHALF + 1])
            put(t0 + 4, "scalar", rstd)

            def rows2(v):
                return v.tensor_copy(T["row_rstd_bf"][0:1, :], T["row_var"][0:1, :])
            put(t0 + 5, "vector", rows2)

            def projs(t, b=b, kv=kv):
                t.matmul(pbc[:], T["ones_row"][:], T["row_rstd_bf"][0:1, :], start=True, stop=True)
                wl_ = (("wk_sb", pk, 1), ("wv_sb", pv, 2)) if kv else (("wq_sb", pq, 0),)
                for w, pdst, rsi in wl_:
                    for dch in range(DCH):
                        t.matmul(pdst[:], T[w][:, dch, :], T["xt"][:, b, dch, :],
                                 start=(dch == 0), stop=False)
                    ins = t.matmul(pdst[:], T["rs_bf"][0:1, rsi, :], T["row_negm_bf"][0:1, :],
                                   start=False, stop=True)
                return ins
            put(t0 + 6, "tensor", projs)
            proj_stage[st] = t0 + 6

            if kv:
                def dve_k(v):
                    env.step("vector", v.tensor_copy(T["rstd_bc"][:], pbc[:]))
                    env.fence(v, "vector")
                    emit_theta(v, pk[:], DC_TBKC, DC_TBK)
                    env.fence(v, "vector")
                    return v.tensor_tensor(T["th"][:], pv[:], T["rstd_bc"][:], ALU.mult)
                put(t0 + 7, "vector", dve_k)

                def act_k(sc, st=st):
                    s0 = st * STRIPE
                    sc.activation(T["cosK"][:, s0:s0 + STRIPE], T["si_c"][:], AF.Sin,
                                  bias=T["consts"][:, C_ZERO:C_ZERO + 1], scale=1.0)
                    sc.activation(T["sinK"][:, s0:s0 + STRIPE], T["si_s"][:], AF.Sin,
                                  bias=T["consts"][:, C_ZERO:C_ZERO + 1], scale=1.0)
                    return sc.activation(T["vT_bf"][:], T["th"][:], AF.Identity,
                                         bias=T["cols0"][:, 4:5],
                                         scale=T["consts"][:, C_ONE:C_ONE + 1])
                put(t0 + 8, "scalar", act_k)

                def vtrans(t):
                    for j in range(4):
                        ins = t.transpose(ptb[:, j, :], T["vT_bf"][:, j * P:(j + 1) * P],
                                          T["id_bf"][:])
                    return ins
                put(t0 + 9, "tensor", vtrans)

                def vevac(v, st=st):
                    for j in range(4):
                        ins = v.tensor_copy(T["v_tok"][:, st * 4 + j, :], ptb[:, j, :])
                    return ins
                put(t0 + 11, "vector", vevac)
            else:
                def dve_q(v):
                    env.step("vector", v.tensor_copy(T["rstd_bc"][:], pbc[:]))
                    env.fence(v, "vector")
                    return emit_theta(v, pq[:], DC_TBQC, DC_TBQ)
                put(t0 + 7, "vector", dve_q)

                def act_q(sc, st=st):
                    q0 = (st - NST) * STRIPE
                    sc.activation(T["cosQ"][:, q0:q0 + STRIPE], T["si_c"][:], AF.Sin,
                                  bias=T["consts"][:, C_ZERO:C_ZERO + 1], scale=1.0)
                    return sc.activation(T["sinQ"][:, q0:q0 + STRIPE], T["si_s"][:], AF.Sin,
                                         bias=T["consts"][:, C_ZERO:C_ZERO + 1], scale=1.0)
                put(t0 + 8, "scalar", act_q)

        last_xt_reader = {}
        for t in sorted(slots):
            idx = stages.add_parallel(slots[t])
            for st, ts in proj_stage.items():
                if ts == t:
                    last_xt_reader[st] = idx

        with nc.Block("prepass") as blk:
            @blk.gpsimd
            def _(g):
                A2 = env.a
                for s_ in range(NST):
                    g.dma_start(A2["xb16"][s_ * STRIPE:(s_ + 1) * STRIPE, :],
                                A2["xb"][s_ * STRIPE:(s_ + 1) * STRIPE, :]).then_inc(gx, 16)
                for s_ in range(4):
                    g.dma_start(A2["xq16"][s_ * STRIPE:(s_ + 1) * STRIPE, :],
                                A2["xq"][s_ * STRIPE:(s_ + 1) * STRIPE, :]).then_inc(gx, 16)

            @blk.sync
            def _(s):
                for st in range(NST + DBG_NQS):
                    s.wait_ge(gx, (st + 1) * 16)
                    if st >= 2:
                        s.wait_ge(stg, stages.stage_end_count(last_xt_reader[st - 2]))
                    if st < NST:
                        srcd = A["xb16"][st * STRIPE:(st + 1) * STRIPE, :]
                    else:
                        srcd = A["xq16"][(st - NST) * STRIPE:(st - NST + 1) * STRIPE, :]
                    for dch in range(DCH):
                        s.dma_start(
                            T["xt"][:, st % 2, dch, :],
                            srcd[:, dch * P:(dch + 1) * P],
                            transpose=True).then_inc(dt_, 16)

            stages.attach(blk, ("tensor", "scalar", "vector"))


def ph_l0_scores(env):
    nc = env.nc
    T = env.t
    psc = env.sem("psc")
    se2 = env.sem("se2")
    pvb = env.sem("pvb")
    va = env.sem("va")
    G = 4
    NG = 64 // G

    with ExitStack() as pctx:
        S = pctx.enter_context(nc.psum_tensor("S", [P, 2, STRIPE], F32))
        Aps = pctx.enter_context(nc.psum_tensor("Aps", [P, 4, STRIPE], F32))
        D1 = pctx.enter_context(nc.psum_tensor("D1", [P, STRIPE], F32))
        D2 = pctx.enter_context(nc.psum_tensor("D2", [32, STRIPE], F32))

        with nc.Block("scores") as blk:
            @blk.tensor
            def _(t):
                for g in range(NG):
                    for j in range(G * 4):
                        kt, qc = g * G + j // 4, j % 4
                        idx = g * G * 4 + j + 1
                        if idx >= 3:
                            t.wait_ge(se2, idx - 2)
                        k0 = kt * P
                        t.matmul(S[:, (idx - 1) % 2, :], T["cosK"][:, k0:k0 + P],
                                 T["cosQ"][:, qc * STRIPE:(qc + 1) * STRIPE],
                                 start=True, stop=False)
                        t.matmul(S[:, (idx - 1) % 2, :], T["sinK"][:, k0:k0 + P],
                                 T["sinQ"][:, qc * STRIPE:(qc + 1) * STRIPE],
                                 start=False, stop=True).then_inc(psc, 1)
                    t.wait_ge(se2, (g + 1) * G * 4)
                    if g >= 1:
                        t.wait_ge(va, g)
                    for qc in range(4):
                        for kk in range(G):
                            t.matmul(Aps[:, qc, :], T["v_tok"][:, g * G + kk, :],
                                     T["w_buf"][:, kk, qc * STRIPE:(qc + 1) * STRIPE],
                                     start=(kk == 0), stop=(kk == G - 1))
                    for qc in range(4):
                        dtile = D1[qc * 32:(qc + 1) * 32, :] if qc < 3 else D2[:, :]
                        for kk in range(G):
                            ins = t.matmul(dtile, T["ones32"][:],
                                           T["w_buf"][:, kk, qc * STRIPE:(qc + 1) * STRIPE],
                                           start=(kk == 0), stop=(kk == G - 1))
                    ins.then_inc(pvb, 1)

            @blk.scalar
            def _(sc):
                for g in range(NG):
                    for j in range(G * 4):
                        kt, qc = g * G + j // 4, j % 4
                        idx = g * G * 4 + j + 1
                        sc.wait_ge(psc, idx)
                        if g >= 1 and j == 0:
                            sc.wait_ge(pvb, g)
                        sc.activation(T["w_buf"][:, kt % G, qc * STRIPE:(qc + 1) * STRIPE],
                                      S[:, (idx - 1) % 2, :], AF.Exp,
                                      bias=T["consts"][:, C_NEGSQRT:C_NEGSQRT + 1],
                                      scale=T["consts"][:, C_INVSQRT:C_INVSQRT + 1]
                                      ).then_inc(se2, 1)

            @blk.vector
            def _(v):
                for g in range(NG):
                    v.wait_ge(pvb, g + 1)
                    for qc in range(4):
                        v.tensor_add(T["attn_acc"][:, qc * STRIPE:(qc + 1) * STRIPE],
                                     T["attn_acc"][:, qc * STRIPE:(qc + 1) * STRIPE],
                                     Aps[:, qc, :])
                    for qc in range(4):
                        srcp = D1[qc * 32:qc * 32 + 1, :] if qc < 3 else D2[0:1, :]
                        ins = v.tensor_add(T["denom_acc"][0:1, qc * STRIPE:(qc + 1) * STRIPE],
                                           T["denom_acc"][0:1, qc * STRIPE:(qc + 1) * STRIPE],
                                           srcp)
                    ins.then_inc(va, 1)


def ph_l0_epilogue(env):
    nc = env.nc
    T = env.t
    stg = env.sem("stg3")
    stages = Stages(stg)

    with ExitStack() as pctx:
        pbc2 = pctx.enter_context(nc.psum_tensor("pbc2", [P, STRIPE], F32))
        py = pctx.enter_context(nc.psum_tensor("py", [P, 2, STRIPE], F32))

        for qc in range(4):
            def recip(v, qc=qc):
                return v.reciprocal(T["row_var"][0:1, :],
                                    T["denom_acc"][0:1, qc * STRIPE:(qc + 1) * STRIPE])
            stages.add("vector", recip)

            def recip2(v):
                return v.tensor_copy(T["dinv_bf"][0:1, :], T["row_var"][0:1, :])
            stages.add("vector", recip2)

            def bc(t, qc=qc):
                return t.matmul(pbc2[:], T["ones_row"][:], T["dinv_bf"][0:1, :],
                                start=True, stop=True)
            stages.add("tensor", bc)

            def norm(v, qc=qc):
                return v.tensor_tensor(T["w_buf"][:, 0, qc * STRIPE:(qc + 1) * STRIPE],
                                       T["attn_acc"][:, qc * STRIPE:(qc + 1) * STRIPE],
                                       pbc2[:], ALU.mult)
            stages.add("vector", norm)

        for och in range(DCH):
            for qc in range(4):
                def proj(t, och=och, qc=qc):
                    pslot = py[:, (och * 4 + qc) % 2, :]
                    t.matmul(pslot, T["wo_sb"][:, och * P:(och + 1) * P],
                             T["w_buf"][:, 0, qc * STRIPE:(qc + 1) * STRIPE],
                             start=True, stop=False)
                    return t.matmul(pslot, T["bo_bf"][0:1, och, :], T["ones_row512"][0:1, :],
                                    start=False, stop=True)
                stages.add("tensor", proj)

                def resadd(v, och=och, qc=qc):
                    dst = T["x_res"][:, och, qc * STRIPE:(qc + 1) * STRIPE]
                    return v.tensor_add(dst, dst, py[:, (och * 4 + qc) % 2, :])
                stages.add("vector", resadd)

        with nc.Block("l0epi") as blk:
            stages.attach(blk, ("tensor", "vector"))


def ph_dump_xres(env):
    nc = env.nc
    T = env.t
    A = env.a
    do = env.sem("do")
    n = [0]
    with nc.Block("dump") as blk:
        @blk.sync
        def _(s):
            def d(dst, srcap):
                s.dma_start(dst, srcap).then_inc(do, 16)
                n[0] += 16
            d(A["xres_out"].rearrange("(c p) t -> p c t", p=P), T["x_res"][:])
            if DBG_L0_DUMPS:
                d(A["d_cosK"], T["cosK"][:])
                d(A["d_sinK"], T["sinK"][:])
                d(A["d_cosQ"], T["cosQ"][:])
                d(A["d_sinQ"], T["sinQ"][:])
                d(A["d_vtok"].rearrange("p (a b) -> p a b", b=P), T["v_tok"][:])
                d(A["d_attn"], T["attn_acc"][:])
                d(A["d_den"], T["denom_acc"][:])
                d(A["d_dcols"], T["dcols"][:])
        @blk.gpsimd
        def _(g):
            if DBG_L0_DUMPS:
                g.dma_start(A["d_rs"], T["rs_bf"][0:1, :, :].rearrange("a b c -> a (b c)")).then_inc(do, 16)
                n[0] += 16
            g.wait_ge(do, n[0])
        @blk.vector
        def _(v):
            pass
    with nc.Block("dumpwait") as blk:
        @blk.sync
        def _(s):
            s.wait_ge(do, n[0])


def build_l0attn(core, upto=6):
    nc = bass.Bass("TRN2", target_bir_lowering=False)
    with ExitStack() as ctx:
        env = Env(nc, ctx)
        env.core = core
        env.qoff = (core % 4) * TOK
        declare_l0attn_io(nc, env)
        env.mkchains()
        alloc_common_sbuf(env)
        alloc_l0attn_sbuf(env)
        phases = [ph_l0_setup, ph_l0_cols, ph_l0_xres_init, ph_l0_prepass, ph_l0_scores, ph_l0_epilogue]
        for i, ph in enumerate(phases):
            if i < upto:
                ph(env)
        ph_dump_xres(env)
    return nc


def prep_l0attn_inputs(I, core):
    f32 = np.float32
    b = core // 4
    x = np.asarray(I["L0_attn_x"], f32)
    cols = np.zeros((P, 8), f32)
    cols[:, 0] = np.asarray(I["L0_attn_wavelengths"], f32)
    cols[:, 1] = np.asarray(I["L0_attn_phase_bias"], f32)
    cols[:, 2] = np.asarray(I["L0_attn_bq"], f32)
    cols[:, 3] = np.asarray(I["L0_attn_bk"], f32)
    cols[:, 4] = np.asarray(I["L0_attn_bv"], f32)
    consts = np.zeros((P, 8), f32)
    consts[:, C_ONE] = 1.0
    consts[:, C_NEGSQRT] = -SQRT_HD
    consts[:, C_INVSQRT] = 1.0 / SQRT_HD
    consts[:, C_LNEPS] = 1e-5
    consts[:, C_NEGHALF] = -0.5
    consts[:, C_HALF_PI] = math.pi / 2.0
    consts[:, C_EIGHTH] = 0.125
    qoff = (core % 4) * TOK
    return {
        "xb": np.ascontiguousarray(x[b]),
        "xq": np.ascontiguousarray(x[b][qoff:qoff + TOK]),
        "wqT": np.ascontiguousarray(np.asarray(I["L0_attn_Wq"], f32).T),
        "wkT": np.ascontiguousarray(np.asarray(I["L0_attn_Wk"], f32).T),
        "wvT": np.ascontiguousarray(np.asarray(I["L0_attn_Wv"], f32).T),
        "woT": np.ascontiguousarray(np.asarray(I["L0_attn_Wo"], f32).T),
        "cols0": cols,
        "bo_row": np.ascontiguousarray(np.asarray(I["L0_attn_bo"], f32).reshape(1, D)),
        "consts": consts,
        "ident": np.eye(P, dtype=f32),
    }





def declare_l0ffn_io(nc, env, io_xres=True):
    A = env.a
    A["bwT"] = nc.dram_tensor("bwT", [D, D], F32, kind="ExternalInput").ap()
    A["sAT"] = nc.dram_tensor("sAT", [D, 32], F32, kind="ExternalInput").ap()
    A["sBT"] = nc.dram_tensor("sBT", [16, D], F32, kind="ExternalInput").ap()
    A["rAT"] = nc.dram_tensor("rAT", [D, 96], F32, kind="ExternalInput").ap()
    A["rBT"] = nc.dram_tensor("rBT", [96, D], F32, kind="ExternalInput").ap()
    A["rWT"] = nc.dram_tensor("rWT", [D, 32], F32, kind="ExternalInput").ap()
    A["rb_row"] = nc.dram_tensor("rb_row", [1, 32], F32, kind="ExternalInput").ap()
    A["bb_row"] = nc.dram_tensor("bb_row", [1, D], F32, kind="ExternalInput").ap()
    if io_xres:
        A["xres_in"] = nc.dram_tensor("xres_in", [D, TOK], F32, kind="ExternalInput").ap()
        A["consts"] = nc.dram_tensor("consts", [P, 8], F32, kind="ExternalInput").ap()
        A["ident"] = nc.dram_tensor("ident", [P, P], F32, kind="ExternalInput").ap()
        A["xres_out"] = nc.dram_tensor("xres_out", [D, TOK], F32, kind="ExternalOutput").ap()


def alloc_l0ffn_sbuf(env):
    sb = env.sb
    sb("f_bwT", [P, DCH, D], BF16)
    sb("f_sAT", [P, DCH, 32], BF16)
    sb("f_sBT", [16, D], BF16)
    sb("f_rAT", [P, DCH, 96], BF16)
    sb("f_rBT", [96, D], BF16)
    sb("f_rWT", [P, DCH, 32], BF16)
    sb("f_rb", [1, 32], BF16)
    sb("f_rstd_bf_row", [1, STRIPE], BF16)
    sb("f_logsb", [8, STRIPE], F32)
    sb("f_mbc2", [P, STRIPE], F32)
    sb("f_bb", [1, DCH, P], BF16)
    sb("f_xn", [P, DCH, STRIPE], BF16)
    sb("f_sq", [P, 4, STRIPE], BF16)
    sb("f_s1", [16, STRIPE], BF16)
    sb("f_glog", [P, 4, 8], F32)
    sb("f_rsc", [P, 4, 8], F32)
    sb("f_rsc2", [P, 4, 8], F32)
    sb("f_rsc3", [P, 4, 8], F32)
    sb("f_rcol", [P, 16], F32)
    sb("f_h2", [P, 4, 96], BF16)
    sb("f_h2T", [P, 4, P], BF16)
    sb("f_htmp", [P, 96], F32)
    sb("f_rstdtok", [P, 8], F32)
    sb("f_mbc", [P, STRIPE], F32)
    sb("f_row_negm", [1, STRIPE], F32)
    sb("f_row_var", [1, STRIPE], F32)
    sb("f_row_rstd", [1, STRIPE], F32)
    sb("f_negm_bf", [1, STRIPE], BF16)
    sb("f_rstd_bc", [P, STRIPE], F32)


def ph_l0ffn_setup(env):
    nc = env.nc
    T = env.t
    A = env.a
    dg = env.sem("fdg")
    with nc.Block("ffn0_setup") as blk:
        @blk.gpsimd
        def _(g):
            n = 0
            for name, ap, pat in (
                ("f_bwT", A["bwT"], "(c p) o -> p c o"),
                ("f_sAT", A["sAT"], "(c p) r -> p c r"),
                ("f_rAT", A["rAT"], "(c p) r -> p c r"),
                ("f_rWT", A["rWT"], "(c p) r -> p c r"),
            ):
                g.dma_start(T[name][:], ap.rearrange(pat, p=P)).then_inc(dg, 16)
                n += 16
            g.dma_start(T["f_sBT"][:], A["sBT"]).then_inc(dg, 16)
            g.dma_start(T["f_rBT"][:], A["rBT"]).then_inc(dg, 16)
            g.dma_start(T["f_rb"][:], A["rb_row"]).then_inc(dg, 16)
            g.dma_start(T["f_bb"][:], A["bb_row"].rearrange("one (c p) -> one c p", p=P)).then_inc(dg, 16)
            n += 64
            g.wait_ge(dg, n)


def ph_l0ffn(env):
    nc = env.nc
    T = env.t
    stg = env.sem("fstg")
    stages = Stages(stg)

    with ExitStack() as pctx:
        B = [pctx.enter_context(nc.psum_tensor(f"fb{i}", [P, STRIPE], F32))
             for i in range(8)]
        Bb = [b.bitcast(BF16) for b in B]

        for s in range(4):
            t0s = s * STRIPE

            def cast(g, t0s=t0s):
                for dch in range(DCH):
                    ins = g.tensor_copy(T["f_xn"][:, dch, :],
                                        T["x_res"][:, dch, t0s:t0s + STRIPE])
                return ins
            stages.add("gpsimd", cast)

            def statsx(t):
                for dch in range(DCH):
                    ins = t.matmul(B[0][0:32, :], T["ones32"][:], T["f_xn"][:, dch, :],
                                   start=(dch == 0), stop=(dch == DCH - 1))
                return ins
            stages.add("tensor", statsx)
            for half, bi in ((0, 1), (1, 2)):
                def sq(sc, half=half):
                    for j in range(4):
                        ins = sc.activation(T["f_sq"][:, j, :], T["f_xn"][:, half * 4 + j, :],
                                            AF.Square)
                    return ins
                stages.add("scalar", sq)

                def stats(t, bi=bi):
                    for j in range(4):
                        ins = t.matmul(B[bi][0:32, :], T["ones32"][:], T["f_sq"][:, j, :],
                                       start=(j == 0), stop=(j == 3))
                    return ins
                stages.add("tensor", stats)

            def rows1(v):
                env.step("vector", v.tensor_copy(T["f_row_var"][0:1, :], B[1][0:1, :]))
                env.fence(v, "vector")
                env.step("vector", v.tensor_add(T["f_row_var"][0:1, :], T["f_row_var"][0:1, :], B[2][0:1, :]))
                env.step("vector", v.tensor_scalar(T["f_row_negm"][0:1, :], B[0][0:1, :], -1.0 / D, None, ALU.mult))
                env.fence(v, "vector")
                env.step("vector", v.tensor_tensor(T["f_row_rstd"][0:1, :], T["f_row_negm"][0:1, :],
                                                   T["f_row_negm"][0:1, :], ALU.mult))
                env.step("vector", v.tensor_scalar(T["f_row_var"][0:1, :], T["f_row_var"][0:1, :], 1.0 / D, None, ALU.mult))
                env.fence(v, "vector")
                env.step("vector", v.tensor_sub(T["f_row_var"][0:1, :], T["f_row_var"][0:1, :], T["f_row_rstd"][0:1, :]))
                return v.tensor_copy(T["f_negm_bf"][0:1, :], T["f_row_negm"][0:1, :])
            stages.add("vector", rows1)

            def rstd(sc):
                env.step("scalar", sc.activation(T["f_row_rstd"][0:1, :], T["f_row_var"][0:1, :], AF.Ln,
                              bias=T["consts"][0:1, C_LNEPS:C_LNEPS + 1], scale=1.0))
                env.fence(sc, "scalar")
                return sc.activation(T["f_row_var"][0:1, :], T["f_row_rstd"][0:1, :], AF.Exp,
                                     bias=T["consts"][0:1, C_ZERO:C_ZERO + 1],
                                     scale=T["consts"][0:1, C_NEGHALF:C_NEGHALF + 1])
            stages.add("scalar", rstd)

            def rows2b(v):
                return v.tensor_copy(T["f_rstd_bf_row"][0:1, :], T["f_row_var"][0:1, :])
            stages.add("vector", rows2b)

            def bc(t):
                t.matmul(B[3][:], T["ones_row"][:], T["f_rstd_bf_row"][0:1, :], start=True, stop=True)
                return t.matmul(B[4][:], T["ones_row"][:], T["f_negm_bf"][0:1, :], start=True, stop=True)
            stages.add("tensor", bc)

            def bcevac(v):
                v.tensor_copy(T["f_rstd_bc"][:], B[3][:])
                return v.tensor_copy(T["f_mbc"][:], B[4][:])
            stages.add("vector", bcevac)

            def xn(v, t0s=t0s):
                for dch in range(DCH):
                    env.fence(v, "vector")
                    env.step("vector", v.tensor_add(T["f_mbc2"][:],
                                                    T["x_res"][:, dch, t0s:t0s + STRIPE],
                                                    T["f_mbc"][:]))
                    env.fence(v, "vector")
                    ins = v.tensor_mul(T["f_xn"][:, dch, :], T["f_mbc2"][:], T["f_rstd_bc"][:])
                    if dch < DCH - 1:
                        env.step("vector", ins)
                return ins
            stages.add("vector", xn)

            def s1mm(t):
                for dch in range(DCH):
                    ins = t.matmul(B[0][0:32, :], T["f_sAT"][:, dch, :], T["f_xn"][:, dch, :],
                                   start=(dch == 0), stop=(dch == DCH - 1))
                return ins
            stages.add("tensor", s1mm)

            def logmm(t):
                for dch in range(DCH):
                    t.matmul(B[1][0:32, :], T["f_rWT"][:, dch, :], T["f_xn"][:, dch, :],
                             start=(dch == 0), stop=False)
                return t.matmul(B[1][0:32, :], T["f_rb"][0:1, :], T["ones_row512"][0:1, :],
                                start=False, stop=True)
            stages.add("tensor", logmm)

            def s1evac(v):
                v.tensor_copy(T["f_s1"][0:16, :], B[0][0:16, :])
                return v.tensor_copy(T["f_logsb"][0:8, :], B[1][0:8, :])
            stages.add("vector", s1evac)

            def gtr(t):
                for tt in range(4):
                    t.transpose(B[2][:, tt * 8:tt * 8 + 6],
                                T["f_logsb"][0:6, tt * P:(tt + 1) * P],
                                T["id_f32"][0:6, 0:6])
                for tt in range(4):
                    ins = t.transpose(B[5][:, 2 * tt:2 * tt + 1],
                                      T["f_row_var"][0:1, tt * P:(tt + 1) * P],
                                      T["id_f32"][0:1, 0:1])
                return ins
            stages.add("tensor", gtr)

            def gevac(v):
                for tt in range(4):
                    v.tensor_copy(T["f_glog"][:, tt, 0:6], B[2][:, tt * 8:tt * 8 + 6])
                return v.tensor_copy(T["f_rstdtok"][:, 0:8], B[5][:, 0:8])
            stages.add("vector", gevac)

            def phmm(t):
                for tt in range(4):
                    bsel = B[5] if tt % 2 == 0 else B[4]
                    off = 256 + (tt // 2) * 96
                    for dch in range(DCH):
                        ins = t.matmul(bsel[:, off:off + 96],
                                       T["f_xn"][:, dch, tt * P:(tt + 1) * P],
                                       T["f_rAT"][:, dch, :],
                                       start=(dch == 0), stop=(dch == DCH - 1))
                return ins
            stages.add("tensor", phmm)

            for tt in range(4):
                def r1(v, tt=tt):
                    return v.reduce_max(T["f_rcol"][:, 0:1], T["f_glog"][:, tt, 0:6], axis=AX.X)
                stages.add("vector", r1)

                def r2(v, tt=tt):
                    return v.tensor_scalar(T["f_rcol"][:, 1:2], T["f_rcol"][:, 0:1], -1.0,
                                           None, ALU.mult)
                stages.add("vector", r2)

                def r3(sc, tt=tt):
                    return sc.activation(T["f_rsc"][:, tt, 0:6], T["f_glog"][:, tt, 0:6],
                                         AF.Exp, bias=T["f_rcol"][:, 1:2], scale=1.0)
                stages.add("scalar", r3)

                def r4a(v, tt=tt):
                    v.reduce_sum(T["f_rcol"][:, 2:3], T["f_rsc"][:, tt, 0:6], axis=AX.X)
                    env.step("vector", v.tensor_single_scalar(T["f_rsc2"][:, tt, 0:6],
                                                              T["f_glog"][:, tt, 0:6],
                                                              T["f_rcol"][:, 0:1], ALU.is_equal))
                    env.fence(v, "vector")
                    return v.scalar_tensor_tensor(T["f_rsc3"][:, tt, 0:6], T["f_rsc2"][:, tt, 0:6],
                                                  -1e30, T["f_glog"][:, tt, 0:6], ALU.mult, ALU.add)
                stages.add("vector", r4a)

                def r4c(v, tt=tt):
                    return v.reduce_max(T["f_rcol"][:, 3:4], T["f_rsc3"][:, tt, 0:6], axis=AX.X)
                stages.add("vector", r4c)

                def r5a(v, tt=tt):
                    env.step("vector", v.tensor_single_scalar(T["f_rsc3"][:, tt, 0:6],
                                                              T["f_glog"][:, tt, 0:6],
                                                              T["f_rcol"][:, 3:4], ALU.is_ge))
                    env.fence(v, "vector")
                    return v.tensor_mul(T["f_rsc2"][:, tt, 0:6], T["f_rsc"][:, tt, 0:6],
                                        T["f_rsc3"][:, tt, 0:6])
                stages.add("vector", r5a)

                def r5b(v, tt=tt):
                    return v.reduce_sum(T["f_rcol"][:, 4:5], T["f_rsc2"][:, tt, 0:6], axis=AX.X)
                stages.add("vector", r5b)

                def r5c(v, tt=tt):
                    return v.scalar_tensor_tensor(T["f_rcol"][:, 5:6], T["f_rcol"][:, 2:3],
                                                  1e-6, T["f_rcol"][:, 4:5], ALU.mult, ALU.add)
                stages.add("vector", r5c)

                def r6(v, tt=tt):
                    return v.reciprocal(T["f_rcol"][:, 6:7], T["f_rcol"][:, 5:6])
                stages.add("vector", r6)

                def r7(v, tt=tt):
                    return v.tensor_single_scalar(T["f_rsc3"][:, tt, 0:6],
                                                  T["f_rsc2"][:, tt, 0:6],
                                                  T["f_rcol"][:, 6:7], ALU.mult)
                stages.add("vector", r7)

            for tt in range(4):
                def h2(v, tt=tt):
                    bsel = B[5] if tt % 2 == 0 else B[4]
                    off = 256 + (tt // 2) * 96
                    env.step("vector", v.tensor_single_scalar(T["f_htmp"][:, 0:96],
                                                              bsel[:, off:off + 96],
                                                              T["f_rstdtok"][:, 2 * tt:2 * tt + 1],
                                                              ALU.mult))
                    env.fence(v, "vector")
                    for e in range(6):
                        ins = v.tensor_single_scalar(T["f_h2"][:, tt, e * 16:(e + 1) * 16],
                                                     T["f_htmp"][:, e * 16:(e + 1) * 16],
                                                     T["f_rsc3"][:, tt, e:e + 1], ALU.mult)
                    return ins
                stages.add("vector", h2)

            def h2tr(t):
                for tt in range(4):
                    ins = t.transpose(Bb[6][0:96, tt * P:(tt + 1) * P],
                                      T["f_h2"][:, tt, :], T["id_bf"][:])
                return ins
            stages.add("tensor", h2tr)

            def h2evac(v):
                for tt in range(4):
                    ins = v.tensor_copy(T["f_h2T"][0:96, tt, :], Bb[6][0:96, tt * P:(tt + 1) * P])
                return ins
            stages.add("vector", h2evac)

            for och in range(DCH):
                def ymm(t, och=och):
                    yb = B[7] if och % 2 == 0 else B[3]
                    for dch in range(DCH):
                        t.matmul(yb[:], T["f_bwT"][:, dch, och * P:(och + 1) * P],
                                 T["f_xn"][:, dch, :], start=(dch == 0), stop=False)
                    t.matmul(yb[:], T["f_sBT"][0:16, och * P:(och + 1) * P],
                             T["f_s1"][0:16, :], start=False, stop=False)
                    t.matmul(yb[:], T["f_rBT"][0:96, och * P:(och + 1) * P],
                             T["f_h2T"][0:96, :, :].rearrange("p a b -> p (a b)"),
                             start=False, stop=False)
                    return t.matmul(yb[:], T["f_bb"][0:1, och, :], T["ones_row512"][0:1, :],
                                    start=False, stop=True)
                stages.add("tensor", ymm)

                def yadd(v, och=och, t0s=t0s):
                    yb = B[7] if och % 2 == 0 else B[3]
                    dst = T["x_res"][:, och, t0s:t0s + STRIPE]
                    return v.tensor_add(dst, dst, yb[:])
                stages.add("vector", yadd)

        with nc.Block("ffn0") as blk:
            stages.attach(blk, ("tensor", "scalar", "vector", "gpsimd"))


def ph_load_xres(env):
    nc = env.nc
    T = env.t
    A = env.a
    dl = env.sem("dlx")
    with nc.Block("loadx") as blk:
        @blk.sync
        def _(s):
            s.dma_start(T["x_res"][:],
                        A["xres_in"].rearrange("(c p) t -> p c t", p=P)).then_inc(dl, 16)
            s.wait_ge(dl, 16)
        @blk.gpsimd
        def _(g):
            g.dma_start(T["id_bf"][:], A["ident"]).then_inc(dl, 16)
            g.wait_ge(dl, 32)
        @blk.scalar
        def _(sc):
            pass


def build_l0ffn(core):
    nc = bass.Bass("TRN2", target_bir_lowering=False)
    with ExitStack() as ctx:
        env = Env(nc, ctx)
        declare_l0ffn_io(nc, env)
        env.mkchains()
        if DBG_F0_DUMPS:
            A = env.a
            for nm, shp, dt_ in (("g_xn", [P, DCH * STRIPE], BF16), ("g_glog", [P, 32], F32),
                                 ("g_wn", [P, 32], F32), ("g_s1", [16, STRIPE], BF16),
                                 ("g_h2T", [P, 4 * P], BF16), ("g_rstdtok", [P, 8], F32),
                                 ("g_rows", [1, 3 * STRIPE], F32), ("g_rcol", [P, 16], F32), ("g_e", [P, 32], F32)):
                A[nm] = nc.dram_tensor(nm, shp, dt_, kind="ExternalOutput").ap()
        alloc_common_sbuf(env)
        alloc_l0ffn_sbuf(env)
        dsc = env.sem("dsc")
        with nc.Block("c0") as blk:
            @blk.sync
            def _(s):
                s.dma_start(env.t["consts"][:], env.a["consts"]).then_inc(dsc, 16)
                s.dma_start(env.t["id_f32"][:], env.a["ident"]).then_inc(dsc, 16)
                s.wait_ge(dsc, 32)
            @blk.gpsimd
            def _(g):
                g.memset(env.t["ones_row"][:], 1.0)
                g.memset(env.t["ones_col"][:], 1.0)
                g.memset(env.t["ones32"][:], 1.0)
                g.memset(env.t["ones_row512"][:], 1.0)
        ph_load_xres(env)
        ph_l0ffn_setup(env)
        ph_l0ffn(env)
        if DBG_F0_DUMPS:
            T = env.t
            A = env.a
            dd = env.sem("ddf")
            with nc.Block("fdump") as blk:
                @blk.sync
                def _(s):
                    s.dma_start(A["g_xn"], T["f_xn"][:].rearrange("p a b -> p (a b)")).then_inc(dd, 16)
                    s.dma_start(A["g_glog"], T["f_glog"][:].rearrange("p a b -> p (a b)")).then_inc(dd, 16)
                    s.dma_start(A["g_wn"], T["f_rsc3"][:].rearrange("p a b -> p (a b)")).then_inc(dd, 16)
                    s.dma_start(A["g_h2T"], T["f_h2T"][:].rearrange("p a b -> p (a b)")).then_inc(dd, 16)
                    s.dma_start(A["g_rstdtok"], T["f_rstdtok"][:]).then_inc(dd, 16)
                    s.dma_start(A["g_rcol"], T["f_rcol"][:]).then_inc(dd, 16)
                    s.dma_start(A["g_e"], T["f_rsc"][:].rearrange("p a b -> p (a b)")).then_inc(dd, 16)
                    s.wait_ge(dd, 112)
                @blk.gpsimd
                def _(g):
                    g.dma_start(A["g_s1"], T["f_s1"][:]).then_inc(dd, 16)
                    rows = [T["f_row_negm"], T["f_row_var"], T["f_row_rstd"]]
                    for i, r in enumerate(rows):
                        g.dma_start(A["g_rows"][0:1, i * STRIPE:(i + 1) * STRIPE], r[0:1, :]).then_inc(dd, 16)
                    g.wait_ge(dd, 176)
        ph_dump_xres(env)
    return nc





def declare_l1attn_io(nc, env, io_xres=True):
    A = env.a
    A["wqc"] = nc.dram_tensor("wqc", [D, D], F32, kind="ExternalInput").ap()
    A["wkc"] = nc.dram_tensor("wkc", [D, D], F32, kind="ExternalInput").ap()
    A["wvc"] = nc.dram_tensor("wvc", [D, D], F32, kind="ExternalInput").ap()
    A["woc"] = nc.dram_tensor("woc", [D, D], F32, kind="ExternalInput").ap()
    A["bq_row"] = nc.dram_tensor("bq_row", [1, D], F32, kind="ExternalInput").ap()
    A["bk_row"] = nc.dram_tensor("bk_row", [1, D], F32, kind="ExternalInput").ap()
    A["bv_row"] = nc.dram_tensor("bv_row", [1, D], F32, kind="ExternalInput").ap()
    A["boc_row"] = nc.dram_tensor("boc_row", [1, D], F32, kind="ExternalInput").ap()
    A["mrows"] = nc.dram_tensor("mrows", [P, 4 * P], F32, kind="ExternalInput").ap()
    if io_xres:
        A["xres_in"] = nc.dram_tensor("xres_in", [D, TOK], F32, kind="ExternalInput").ap()
        A["consts"] = nc.dram_tensor("consts", [P, 8], F32, kind="ExternalInput").ap()
        A["ident"] = nc.dram_tensor("ident", [P, P], F32, kind="ExternalInput").ap()
        A["xres_out"] = nc.dram_tensor("xres_out", [D, TOK], F32, kind="ExternalOutput").ap()


def alloc_l1attn_sbuf(env):
    sb = env.sb
    sb("a_wq", [P, DCH, D], BF16)
    sb("a_wk", [P, DCH, D], BF16)
    sb("a_wv", [P, DCH, D], BF16)
    sb("a_wo", [P, DCH, D], BF16)
    sb("a_bq", [1, DCH, P], BF16)
    sb("a_bk", [1, DCH, P], BF16)
    sb("a_bv", [1, D], BF16)
    sb("a_bo", [1, DCH, P], BF16)
    sb("a_bvbc", [P, D], BF16)
    sb("a_xn", [P, DCH, STRIPE], BF16)
    sb("a_sq", [P, 4, STRIPE], BF16)
    sb("a_qT", [P, DCH, STRIPE], BF16)
    sb("a_kT", [P, DCH, STRIPE], BF16)
    sb("a_vtok", [P, 4, 16 * 65], BF16)
    sb("a_vals", [P, 4, D], BF16)
    sb("a_valsT", [P, DCH, STRIPE], BF16)
    sb("a_wexp", [P, 2, P], BF16)
    sb("a_mrows", [P, 4, P], BF16)
    sb("a_iv", [P, 1], F32)
    sb("a_mbc", [P, STRIPE], F32)
    sb("a_mbc2", [P, STRIPE], F32)
    sb("a_rstd_bc", [P, STRIPE], F32)
    sb("a_row_negm", [1, STRIPE], F32)
    sb("a_row_var", [1, STRIPE], F32)
    sb("a_row_rstd", [1, STRIPE], F32)
    sb("a_negm_bf", [1, STRIPE], BF16)
    sb("a_rstd_bf", [1, STRIPE], BF16)


def ph_l1attn_setup(env):
    nc = env.nc
    T = env.t
    A = env.a
    dg = env.sem("adg")
    cs = env.sem("acs")
    with nc.Block("attn1_setup") as blk:
        @blk.gpsimd
        def _(g):
            n = 0
            for name, ap in (("a_wq", A["wqc"]), ("a_wk", A["wkc"]),
                             ("a_wv", A["wvc"]), ("a_wo", A["woc"])):
                g.dma_start(T[name][:], ap.rearrange("(c p) o -> p c o", p=P)).then_inc(dg, 16)
                n += 16
            for name, ap in (("a_bq", A["bq_row"]), ("a_bk", A["bk_row"]),
                             ("a_bo", A["boc_row"])):
                g.dma_start(T[name][:], ap.rearrange("one (c p) -> one c p", p=P)).then_inc(dg, 16)
                n += 16
            g.dma_start(T["a_bv"][:], A["bv_row"]).then_inc(dg, 16)
            g.dma_start(T["a_mrows"][:].rearrange("p a b -> p (a b)"), A["mrows"]).then_inc(dg, 16)
            n += 32
            g.wait_ge(dg, n)
            g.memset(T["a_vtok"][:].rearrange("p a (h c) -> p a h c", c=65)[:, :, :, 64:65], 1.0)



def ph_l1attn(env):
    nc = env.nc
    T = env.t
    stg = env.sem("astg")
    stages = Stages(stg)
    ps3 = env.sem("aps3")
    pv3 = env.sem("apv3")
    se3 = env.sem("ase3")
    ve3 = env.sem("ave3")

    with ExitStack() as pctx:
        B = [pctx.enter_context(nc.psum_tensor(f"ab{i}", [P, STRIPE], F32))
             for i in range(7)]
        Bb6 = B[6].bitcast(BF16)

        def bvb(t):
            t.matmul(B[0][:], T["ones_row"][:], T["a_bv"][0:1, 0:STRIPE], start=True, stop=True)
            return t.matmul(B[1][:], T["ones_row"][:], T["a_bv"][0:1, STRIPE:D],
                            start=True, stop=True)
        stages.add("tensor", bvb)

        def bvbe(v):
            env.step("vector", v.tensor_copy(T["a_bvbc"][:, 0:STRIPE], B[0][:]))
            return v.tensor_copy(T["a_bvbc"][:, STRIPE:D], B[1][:])
        stages.add("vector", bvbe)

        for s in range(4):
            t0s = s * STRIPE

            def cast(g, t0s=t0s):
                for dch in range(DCH):
                    ins = g.tensor_copy(T["a_xn"][:, dch, :],
                                        T["x_res"][:, dch, t0s:t0s + STRIPE])
                return ins
            stages.add("gpsimd", cast)

            def statsx(t):
                for dch in range(DCH):
                    ins = t.matmul(B[0][0:32, :], T["ones32"][:], T["a_xn"][:, dch, :],
                                   start=(dch == 0), stop=(dch == DCH - 1))
                return ins
            stages.add("tensor", statsx)
            for half, bi in ((0, 1), (1, 2)):
                def sq(sc, half=half):
                    for j in range(4):
                        ins = sc.activation(T["a_sq"][:, j, :], T["a_xn"][:, half * 4 + j, :],
                                            AF.Square)
                    return ins
                stages.add("scalar", sq)

                def stats(t, bi=bi):
                    for j in range(4):
                        ins = t.matmul(B[bi][0:32, :], T["ones32"][:], T["a_sq"][:, j, :],
                                       start=(j == 0), stop=(j == 3))
                    return ins
                stages.add("tensor", stats)

            def rows1(v):
                env.step("vector", v.tensor_copy(T["a_row_var"][0:1, :], B[1][0:1, :]))
                env.fence(v, "vector")
                env.step("vector", v.tensor_add(T["a_row_var"][0:1, :], T["a_row_var"][0:1, :], B[2][0:1, :]))
                env.step("vector", v.tensor_scalar(T["a_row_negm"][0:1, :], B[0][0:1, :], -1.0 / D, None, ALU.mult))
                env.fence(v, "vector")
                env.step("vector", v.tensor_tensor(T["a_row_rstd"][0:1, :], T["a_row_negm"][0:1, :],
                                                   T["a_row_negm"][0:1, :], ALU.mult))
                env.step("vector", v.tensor_scalar(T["a_row_var"][0:1, :], T["a_row_var"][0:1, :], 1.0 / D, None, ALU.mult))
                env.fence(v, "vector")
                env.step("vector", v.tensor_sub(T["a_row_var"][0:1, :], T["a_row_var"][0:1, :], T["a_row_rstd"][0:1, :]))
                return v.tensor_copy(T["a_negm_bf"][0:1, :], T["a_row_negm"][0:1, :])
            stages.add("vector", rows1)

            def rstd(sc):
                env.step("scalar", sc.activation(T["a_row_rstd"][0:1, :], T["a_row_var"][0:1, :], AF.Ln,
                                                 bias=T["consts"][0:1, C_LNEPS:C_LNEPS + 1], scale=1.0))
                env.fence(sc, "scalar")
                return sc.activation(T["a_row_var"][0:1, :], T["a_row_rstd"][0:1, :], AF.Exp,
                                     bias=T["consts"][0:1, C_ZERO:C_ZERO + 1],
                                     scale=T["consts"][0:1, C_NEGHALF:C_NEGHALF + 1])
            stages.add("scalar", rstd)

            def rows2b(v):
                return v.tensor_copy(T["a_rstd_bf"][0:1, :], T["a_row_var"][0:1, :])
            stages.add("vector", rows2b)

            def bc(t):
                t.matmul(B[3][:], T["ones_row"][:], T["a_rstd_bf"][0:1, :], start=True, stop=True)
                return t.matmul(B[4][:], T["ones_row"][:], T["a_negm_bf"][0:1, :], start=True, stop=True)
            stages.add("tensor", bc)

            def bcevac(v):
                env.step("vector", v.tensor_copy(T["a_rstd_bc"][:], B[3][:]))
                return v.tensor_copy(T["a_mbc"][:], B[4][:])
            stages.add("vector", bcevac)

            def xn(v, t0s=t0s):
                for dch in range(DCH):
                    env.fence(v, "vector")
                    env.step("vector", v.tensor_add(T["a_mbc2"][:],
                                                    T["x_res"][:, dch, t0s:t0s + STRIPE],
                                                    T["a_mbc"][:]))
                    env.fence(v, "vector")
                    ins = v.tensor_mul(T["a_xn"][:, dch, :], T["a_mbc2"][:], T["a_rstd_bc"][:])
                    if dch < DCH - 1:
                        env.step("vector", ins)
                return ins
            stages.add("vector", xn)

            for wname, bname, dst in (("a_wq", "a_bq", "a_qT"), ("a_wk", "a_bk", "a_kT")):
                for j in range(DCH):
                    def proj(t, wname=wname, bname=bname, j=j):
                        bsel = B[j % 2]
                        for dch in range(DCH):
                            t.matmul(bsel[:], T[wname][:, dch, j * P:(j + 1) * P],
                                     T["a_xn"][:, dch, :], start=(dch == 0), stop=False)
                        return t.matmul(bsel[:], T[bname][0:1, j, :], T["ones_row512"][0:1, :],
                                        start=False, stop=True)
                    stages.add("tensor", proj)

                    def pevac(v, dst=dst, j=j):
                        return v.tensor_copy(T[dst][:, j, :], B[j % 2][:])
                    stages.add("vector", pevac)

            for tt in range(4):
                for half in range(2):
                    def vproj(t, tt=tt, half=half, t0s=t0s):
                        bsel = B[(tt * 2 + half) % 2]
                        for dch in range(DCH):
                            ins = t.matmul(bsel[:], T["a_xn"][:, dch, tt * P:(tt + 1) * P],
                                           T["a_wv"][:, dch, half * STRIPE:(half + 1) * STRIPE],
                                           start=(dch == 0), stop=(dch == DCH - 1))
                        return ins
                    stages.add("tensor", vproj)

                    def vevac(v, tt=tt, half=half):
                        bsel = B[(tt * 2 + half) % 2]
                        env.step("vector", v.tensor_add(T["a_mbc2"][:], bsel[:],
                                                        T["a_bvbc"][:, half * STRIPE:(half + 1) * STRIPE]))
                        env.fence(v, "vector")
                        dstv = T["a_vtok"][:, tt, :].rearrange("p (h c) -> p h c", c=65)
                        h0 = half * 8
                        return v.tensor_copy(dstv[:, h0:h0 + 8, 0:64],
                                             T["a_mbc2"][:].rearrange("p (h c) -> p h c", c=64))
                    stages.add("vector", vevac)

            def attn_pe(t, s=s):
                for u in range(64):
                    U = s * 64 + u + 1
                    tt, h = u // 16, u % 16
                    hb = (h % 2) * 64
                    sbank = B[5] if u % 2 == 0 else B[6]
                    pbank = B[4] if u % 2 == 0 else B[3]
                    if U > 2:
                        t.wait_ge(se3, U - 2)
                    ksub = T["a_kT"][hb:hb + 64, h // 2, tt * P:(tt + 1) * P]
                    qsub = T["a_qT"][hb:hb + 64, h // 2, tt * P:(tt + 1) * P]
                    t.matmul(sbank[:, 0:P], ksub, qsub, start=True, stop=False,
                             tile_position=(hb, 0))
                    t.matmul(sbank[:, 0:P], T["a_mrows"][hb:hb + 1, 0, :],
                             T["a_mrows"][hb:hb + 1, 1, :], start=False, stop=False)
                    t.matmul(sbank[:, 0:P], T["a_mrows"][hb:hb + 1, 2, :],
                             T["a_mrows"][hb:hb + 1, 3, :], start=False,
                             stop=True).then_inc(ps3, 1)
                    t.wait_ge(se3, U)
                    if U > 2:
                        t.wait_ge(ve3, U - 2)
                    vsub = T["a_vtok"][:, tt, h * 65:h * 65 + 65]
                    t.matmul(pbank[:, 0:65], T["a_wexp"][:, u % 2, :], vsub,
                             start=True, stop=True).then_inc(pv3, 1)
                return t.nop()

            def attn_act(sc, s=s):
                for u in range(64):
                    U = s * 64 + u + 1
                    sbank = B[5] if u % 2 == 0 else B[6]
                    sc.wait_ge(ps3, U)
                    if U > 2:
                        sc.wait_ge(pv3, U - 2)
                    sc.activation(T["a_wexp"][:, u % 2, :], sbank[:, 0:P], AF.Exp,
                                  bias=T["consts"][:, C_ZERO:C_ZERO + 1],
                                  scale=T["consts"][:, C_EIGHTH:C_EIGHTH + 1]
                                  ).then_inc(se3, 1)
                return sc.nop()

            def attn_dve(v, s=s):
                for u in range(64):
                    U = s * 64 + u + 1
                    tt, h = u // 16, u % 16
                    pbank = B[4] if u % 2 == 0 else B[3]
                    v.wait_ge(pv3, U)
                    env.step("vector", v.reciprocal(T["a_iv"][:, 0:1], pbank[:, 64:65]))
                    env.fence(v, "vector")
                    v.tensor_single_scalar(
                        T["a_vals"][:, tt, h * 64:(h + 1) * 64],
                        pbank[:, 0:64],
                        T["a_iv"][:, 0:1], ALU.mult).then_inc(ve3, 1)
                return v.nop()

            stages.add_parallel([("tensor", attn_pe), ("scalar", attn_act),
                                 ("vector", attn_dve)])

            for tt in range(4):
                def vtr(t, tt=tt, s=s):
                    t.wait_ge(ve3, (s + 1) * 64)
                    for dch in range(DCH):
                        ins = t.transpose(Bb6[:, dch * P:(dch + 1) * P],
                                          T["a_vals"][:, tt, dch * P:(dch + 1) * P],
                                          T["id_bf"][:])
                    return ins
                stages.add("tensor", vtr)

                def vtre(v, tt=tt):
                    for dch in range(DCH):
                        ins = v.tensor_copy(T["a_valsT"][:, dch, tt * P:(tt + 1) * P],
                                            Bb6[:, dch * P:(dch + 1) * P])
                    return ins
                stages.add("vector", vtre)

            for och in range(DCH):
                def ymm(t, och=och):
                    yb = B[0] if och % 2 == 0 else B[1]
                    for dch in range(DCH):
                        t.matmul(yb[:], T["a_wo"][:, dch, och * P:(och + 1) * P],
                                 T["a_valsT"][:, dch, :], start=(dch == 0), stop=False)
                    return t.matmul(yb[:], T["a_bo"][0:1, och, :], T["ones_row512"][0:1, :],
                                    start=False, stop=True)
                stages.add("tensor", ymm)

                def yadd(v, och=och, t0s=t0s):
                    yb = B[0] if och % 2 == 0 else B[1]
                    dst = T["x_res"][:, och, t0s:t0s + STRIPE]
                    return v.tensor_add(dst, dst, yb[:])
                stages.add("vector", yadd)

        with nc.Block("attn1") as blk:
            stages.attach(blk, ("tensor", "scalar", "vector", "gpsimd"))


def build_l1attn(core):
    nc = bass.Bass("TRN2", target_bir_lowering=False)
    with ExitStack() as ctx:
        env = Env(nc, ctx)
        declare_l1attn_io(nc, env)
        env.mkchains()
        alloc_common_sbuf(env)
        alloc_l1attn_sbuf(env)
        dsc = env.sem("dsc")
        with nc.Block("c0") as blk:
            @blk.sync
            def _(s):
                s.dma_start(env.t["consts"][:], env.a["consts"]).then_inc(dsc, 16)
                s.dma_start(env.t["id_f32"][:], env.a["ident"]).then_inc(dsc, 16)
                s.wait_ge(dsc, 32)
            @blk.gpsimd
            def _(g):
                g.dma_start(env.t["id_bf"][:], env.a["ident"]).then_inc(dsc, 16)
                g.wait_ge(dsc, 48)
                g.memset(env.t["ones_row"][:], 1.0)
                g.memset(env.t["ones_col"][:], 1.0)
                g.memset(env.t["ones32"][:], 1.0)
                g.memset(env.t["ones_row512"][:], 1.0)
        ph_load_xres(env)
        ph_l1attn_setup(env)
        ph_l1attn(env)
        ph_dump_xres(env)
    return nc





NE1, NFG1, NFC1 = 8, 4, 8


def declare_l1ffn_io(nc, env, io_xres=True):
    A = env.a
    A["Wg"] = nc.dram_tensor("Wg", [NE1, D, 4096], F32, kind="ExternalInput").ap()
    A["Wu"] = nc.dram_tensor("Wu", [NE1, D, 4096], F32, kind="ExternalInput").ap()
    A["Wd"] = nc.dram_tensor("Wd", [NE1, 4096, D], F32, kind="ExternalInput").ap()
    if io_xres:
        A["xres_in"] = nc.dram_tensor("xres_in", [D, TOK], F32, kind="ExternalInput").ap()
        A["consts"] = nc.dram_tensor("consts", [P, 8], F32, kind="ExternalInput").ap()
        A["ident"] = nc.dram_tensor("ident", [P, P], F32, kind="ExternalInput").ap()
        A["xres_out"] = nc.dram_tensor("xres_out", [D, TOK], F32, kind="ExternalOutput").ap()


def alloc_l1ffn_sbuf(env):
    sb = env.sb
    sb("z_xn", [P, DCH, TOK], BF16)
    sb("z_hT", [P, NFC1, STRIPE], BF16)
    sb("z_wgu", [P, 2, 2, DCH, 1024], BF16)
    sb("z_wd", [P, DCH, D], BF16)
    sb("z_tmp", [P, 2, STRIPE], BF16)
    sb("z_sq", [P, 4, STRIPE], BF16)
    sb("z_mbc", [P, STRIPE], F32)
    sb("z_mbc2", [P, STRIPE], F32)
    sb("z_rstd_bc", [P, STRIPE], F32)
    sb("z_row_negm", [1, STRIPE], F32)
    sb("z_row_var", [1, STRIPE], F32)
    sb("z_row_rstd", [1, STRIPE], F32)
    sb("z_negm_bf", [1, STRIPE], BF16)
    sb("z_rstd_bf", [1, STRIPE], BF16)


def ph_l1ffn_xn(env):
    nc = env.nc
    T = env.t
    stg = env.sem("zstg")
    stages = Stages(stg)
    with ExitStack() as pctx:
        B = [pctx.enter_context(nc.psum_tensor(f"zb{i}", [P, STRIPE], F32))
             for i in range(5)]
        for s in range(4):
            t0s = s * STRIPE

            def cast(g, t0s=t0s):
                for dch in range(DCH):
                    ins = g.tensor_copy(T["z_xn"][:, dch, t0s:t0s + STRIPE],
                                        T["x_res"][:, dch, t0s:t0s + STRIPE])
                return ins
            stages.add("gpsimd", cast)

            def statsx(t, t0s=t0s):
                for dch in range(DCH):
                    ins = t.matmul(B[0][0:32, :], T["ones32"][:],
                                   T["z_xn"][:, dch, t0s:t0s + STRIPE],
                                   start=(dch == 0), stop=(dch == DCH - 1))
                return ins
            stages.add("tensor", statsx)
            for half, bi in ((0, 1), (1, 2)):
                def sq(sc, half=half, t0s=t0s):
                    for j in range(4):
                        ins = sc.activation(T["z_sq"][:, j, :],
                                            T["z_xn"][:, half * 4 + j, t0s:t0s + STRIPE],
                                            AF.Square)
                    return ins
                stages.add("scalar", sq)

                def stats(t, bi=bi):
                    for j in range(4):
                        ins = t.matmul(B[bi][0:32, :], T["ones32"][:], T["z_sq"][:, j, :],
                                       start=(j == 0), stop=(j == 3))
                    return ins
                stages.add("tensor", stats)

            def rows1(v):
                env.step("vector", v.tensor_copy(T["z_row_var"][0:1, :], B[1][0:1, :]))
                env.fence(v, "vector")
                env.step("vector", v.tensor_add(T["z_row_var"][0:1, :], T["z_row_var"][0:1, :], B[2][0:1, :]))
                env.step("vector", v.tensor_scalar(T["z_row_negm"][0:1, :], B[0][0:1, :], -1.0 / D, None, ALU.mult))
                env.fence(v, "vector")
                env.step("vector", v.tensor_tensor(T["z_row_rstd"][0:1, :], T["z_row_negm"][0:1, :],
                                                   T["z_row_negm"][0:1, :], ALU.mult))
                env.step("vector", v.tensor_scalar(T["z_row_var"][0:1, :], T["z_row_var"][0:1, :], 1.0 / D, None, ALU.mult))
                env.fence(v, "vector")
                env.step("vector", v.tensor_sub(T["z_row_var"][0:1, :], T["z_row_var"][0:1, :], T["z_row_rstd"][0:1, :]))
                return v.tensor_copy(T["z_negm_bf"][0:1, :], T["z_row_negm"][0:1, :])
            stages.add("vector", rows1)

            def rstd(sc):
                env.step("scalar", sc.activation(T["z_row_rstd"][0:1, :], T["z_row_var"][0:1, :], AF.Ln,
                                                 bias=T["consts"][0:1, C_LNEPS:C_LNEPS + 1], scale=1.0))
                env.fence(sc, "scalar")
                return sc.activation(T["z_row_var"][0:1, :], T["z_row_rstd"][0:1, :], AF.Exp,
                                     bias=T["consts"][0:1, C_ZERO:C_ZERO + 1],
                                     scale=T["consts"][0:1, C_NEGHALF:C_NEGHALF + 1])
            stages.add("scalar", rstd)

            def rows2b(v):
                return v.tensor_copy(T["z_rstd_bf"][0:1, :], T["z_row_var"][0:1, :])
            stages.add("vector", rows2b)

            def bc(t):
                t.matmul(B[3][:], T["ones_row"][:], T["z_rstd_bf"][0:1, :], start=True, stop=True)
                return t.matmul(B[4][:], T["ones_row"][:], T["z_negm_bf"][0:1, :], start=True, stop=True)
            stages.add("tensor", bc)

            def bcevac(v):
                env.step("vector", v.tensor_copy(T["z_rstd_bc"][:], B[3][:]))
                return v.tensor_copy(T["z_mbc"][:], B[4][:])
            stages.add("vector", bcevac)

            def xn(v, t0s=t0s):
                for dch in range(DCH):
                    env.fence(v, "vector")
                    env.step("vector", v.tensor_add(T["z_mbc2"][:],
                                                    T["x_res"][:, dch, t0s:t0s + STRIPE],
                                                    T["z_mbc"][:]))
                    env.fence(v, "vector")
                    ins = v.tensor_mul(T["z_xn"][:, dch, t0s:t0s + STRIPE], T["z_mbc2"][:],
                                       T["z_rstd_bc"][:])
                    if dch < DCH - 1:
                        env.step("vector", ins)
                return ins
            stages.add("vector", xn)

        with nc.Block("l1ffn_xn") as blk:
            stages.attach(blk, ("tensor", "scalar", "vector", "gpsimd"))


def ph_l1ffn(env):
    nc = env.nc
    T = env.t
    A = env.a
    wl = env.sem("zwl")
    peg = env.sem("zpeg")
    pey = env.sem("zpey")
    sse = env.sem("zsse")
    vh = env.sem("zvh")
    vy = env.sem("zvy")

    NG = NE1 * NFG1

    with ExitStack() as pctx:
        pg = [pctx.enter_context(nc.psum_tensor(f"zpg{i}", [P, STRIPE], F32)) for i in range(2)]
        pu = [pctx.enter_context(nc.psum_tensor(f"zpu{i}", [P, STRIPE], F32)) for i in range(2)]
        py = [pctx.enter_context(nc.psum_tensor(f"zpy{i}", [P, STRIPE], F32)) for i in range(2)]

        with nc.Block("l1ffn") as blk:
            @blk.gpsimd
            def _(g):
                for G in range(NG):
                    e, fg = G // NFG1, G % NFG1
                    if G >= 2:
                        g.wait_ge(peg, (G - 1) * 32)
                    g.dma_start(T["z_wgu"][:, G % 2, 0, :, :],
                                A["Wg"][e, :, fg * 1024:(fg + 1) * 1024]
                                .rearrange("(c p) f -> p c f", p=P)).then_inc(wl, 16)
                    g.dma_start(T["z_wgu"][:, G % 2, 1, :, :],
                                A["Wu"][e, :, fg * 1024:(fg + 1) * 1024]
                                .rearrange("(c p) f -> p c f", p=P)).then_inc(wl, 16)
                    if G >= 1:
                        g.wait_ge(pey, G * 32)
                    g.dma_start(T["z_wd"][:],
                                A["Wd"][e, fg * 1024:(fg + 1) * 1024, :]
                                .rearrange("(c p) o -> p c o", p=P)).then_inc(wl, 16)

            @blk.tensor
            def _(t):
                ngu = 0
                ny = 0
                for G in range(NG):
                    t.wait_ge(wl, G * 48 + 32)
                    for s in range(4):
                        t0s = s * STRIPE
                        for fc in range(NFC1):
                            par = ngu % 2
                            if ngu >= 2:
                                t.wait_ge(sse, ngu - 1)
                                t.wait_ge(vh, ngu - 1)
                            for dch in range(DCH):
                                t.matmul(pg[par][:], T["z_wgu"][:, G % 2, 0, dch, fc * P:(fc + 1) * P],
                                         T["z_xn"][:, dch, t0s:t0s + STRIPE],
                                         start=(dch == 0), stop=(dch == DCH - 1))
                            for dch in range(DCH):
                                ins = t.matmul(pu[par][:], T["z_wgu"][:, G % 2, 1, dch, fc * P:(fc + 1) * P],
                                               T["z_xn"][:, dch, t0s:t0s + STRIPE],
                                               start=(dch == 0), stop=(dch == DCH - 1))
                            ins.then_inc(peg, 1)
                            ngu += 1
                        if s == 0:
                            t.wait_ge(wl, 48 * (G + 1))
                        t.wait_ge(vh, ngu)
                        for och in range(DCH):
                            ypar = ny % 2
                            if ny >= 2:
                                t.wait_ge(vy, ny - 1)
                            for fc in range(NFC1):
                                ins = t.matmul(py[ypar][:], T["z_wd"][:, fc, och * P:(och + 1) * P],
                                               T["z_hT"][:, fc, :],
                                               start=(fc == 0), stop=(fc == NFC1 - 1))
                            ins.then_inc(pey, 1)
                            ny += 1

            @blk.scalar
            def _(sc):
                ngu = 0
                for G in range(NG):
                    for s in range(4):
                        for fc in range(NFC1):
                            ngu += 1
                            sc.wait_ge(peg, ngu)
                            if ngu >= 3:
                                sc.wait_ge(vh, ngu - 2)
                            sc.activation(T["z_tmp"][:, (ngu - 1) % 2, :], pg[(ngu - 1) % 2][:],
                                          AF.Silu).then_inc(sse, 1)

            @blk.vector
            def _(v):
                ngu = 0
                ny = 0
                for G in range(NG):
                    for s in range(4):
                        t0s = s * STRIPE
                        for fc in range(NFC1):
                            ngu += 1
                            v.wait_ge(sse, ngu)
                            v.tensor_tensor(T["z_hT"][:, fc, :], T["z_tmp"][:, (ngu - 1) % 2, :],
                                            pu[(ngu - 1) % 2][:], ALU.mult).then_inc(vh, 1)
                        for och in range(DCH):
                            ny += 1
                            v.wait_ge(pey, ny)
                            dst = T["x_res"][:, och, t0s:t0s + STRIPE]
                            v.tensor_add(dst, dst, py[(ny - 1) % 2][:]).then_inc(vy, 1)


def build_l1ffn(core):
    nc = bass.Bass("TRN2", target_bir_lowering=False)
    with ExitStack() as ctx:
        env = Env(nc, ctx)
        declare_l1ffn_io(nc, env)
        env.mkchains()
        alloc_common_sbuf(env)
        alloc_l1ffn_sbuf(env)
        dsc = env.sem("dsc")
        with nc.Block("c0") as blk:
            @blk.sync
            def _(s):
                s.dma_start(env.t["consts"][:], env.a["consts"]).then_inc(dsc, 16)
                s.dma_start(env.t["id_f32"][:], env.a["ident"]).then_inc(dsc, 16)
                s.wait_ge(dsc, 32)
            @blk.gpsimd
            def _(g):
                g.memset(env.t["ones_row"][:], 1.0)
                g.memset(env.t["ones_col"][:], 1.0)
                g.memset(env.t["ones32"][:], 1.0)
                g.memset(env.t["ones_row512"][:], 1.0)
        ph_load_xres_nold(env)
        ph_l1ffn_xn(env)
        ph_l1ffn(env)
        ph_dump_xres(env)
    return nc


def ph_load_xres_nold(env):
    nc = env.nc
    T = env.t
    A = env.a
    dl = env.sem("dlx")
    with nc.Block("loadx") as blk:
        @blk.sync
        def _(s):
            s.dma_start(T["x_res"][:],
                        A["xres_in"].rearrange("(c p) t -> p c t", p=P)).then_inc(dl, 16)
            s.wait_ge(dl, 16)


def _consts_arr():
    f32 = np.float32
    consts = np.zeros((P, 8), f32)
    consts[:, C_ONE] = 1.0
    consts[:, C_NEGSQRT] = -SQRT_HD
    consts[:, C_INVSQRT] = 1.0 / SQRT_HD
    consts[:, C_LNEPS] = 1e-5
    consts[:, C_NEGHALF] = -0.5
    consts[:, C_HALF_PI] = math.pi / 2.0
    consts[:, C_EIGHTH] = 0.125
    return consts


def prep_l0ffn_inputs(I, xres_in):
    f32 = np.float32
    sAT = np.zeros((D, 32), f32)
    sAT[:, :16] = np.asarray(I["L0_ffn_shared_A"], f32).T
    rWT = np.zeros((D, 32), f32)
    rWT[:, :6] = np.asarray(I["L0_ffn_router_W"], f32).T
    rb = np.zeros((1, 32), f32)
    rb[0, :6] = np.asarray(I["L0_ffn_router_b"], f32)
    return {
        "bwT": np.ascontiguousarray(np.asarray(I["L0_ffn_base_W"], f32).T),
        "sAT": sAT,
        "sBT": np.ascontiguousarray(np.asarray(I["L0_ffn_shared_B"], f32).T),
        "rAT": np.ascontiguousarray(np.asarray(I["L0_ffn_routed_A"], f32).transpose(2, 0, 1).reshape(D, 96)),
        "rBT": np.ascontiguousarray(np.asarray(I["L0_ffn_routed_B"], f32).transpose(0, 2, 1).reshape(96, D)),
        "rWT": rWT,
        "rb_row": rb,
        "bb_row": np.ascontiguousarray(np.asarray(I["L0_ffn_base_b"], f32).reshape(1, D)),
        "xres_in": xres_in,
        "consts": _consts_arr(),
        "ident": np.eye(P, dtype=f32),
    }


def _mask_rows():
    f32 = np.float32
    m = np.zeros((P, 4, P), f32)
    m[:, 0, 64:128] = -1e9
    m[:, 1, 0:64] = 1.0
    m[:, 2, 0:64] = -1e9
    m[:, 3, 64:128] = 1.0
    return np.ascontiguousarray(m.reshape(P, 4 * P))


def prep_l1attn_inputs(I, xres_in):
    f32 = np.float32
    w = np.asarray(I["L1_attn_w_qkv"], f32).reshape(D, 16, 192)
    b = np.asarray(I["L1_attn_b_qkv"], f32).reshape(16, 192)
    return {
        "wqc": np.ascontiguousarray(w[:, :, 0:64].reshape(D, D)),
        "wkc": np.ascontiguousarray(w[:, :, 64:128].reshape(D, D)),
        "wvc": np.ascontiguousarray(w[:, :, 128:192].reshape(D, D)),
        "woc": np.ascontiguousarray(np.asarray(I["L1_attn_w_o"], f32)),
        "bq_row": np.ascontiguousarray(b[:, 0:64].reshape(1, D)),
        "bk_row": np.ascontiguousarray(b[:, 64:128].reshape(1, D)),
        "bv_row": np.ascontiguousarray(b[:, 128:192].reshape(1, D)),
        "boc_row": np.ascontiguousarray(np.asarray(I["L1_attn_b_o"], f32).reshape(1, D)),
        "mrows": _mask_rows(),
        "xres_in": xres_in,
        "consts": _consts_arr(),
        "ident": np.eye(P, dtype=f32),
    }


def prep_l1ffn_inputs(I, xres_in):
    f32 = np.float32
    return {
        "Wg": np.ascontiguousarray(np.asarray(I["L1_ffn_Wg"], f32)),
        "Wu": np.ascontiguousarray(np.asarray(I["L1_ffn_Wu"], f32)),
        "Wd": np.ascontiguousarray(np.asarray(I["L1_ffn_Wd"], f32)),
        "xres_in": xres_in,
        "consts": _consts_arr(),
        "ident": np.eye(P, dtype=f32),
    }


def prep_full(I, core):
    m = prep_l0attn_inputs(I, core)
    for d in (prep_l0ffn_inputs(I, None), prep_l1attn_inputs(I, None),
              prep_l1ffn_inputs(I, None)):
        d.pop("xres_in")
        d.pop("consts", None)
        d.pop("ident", None)
        m.update(d)
    return m


def build_full():
    nc = bass.Bass("TRN2", target_bir_lowering=False)
    with ExitStack() as ctx:
        env = Env(nc, ctx)
        declare_l0attn_io(nc, env)
        declare_l0ffn_io(nc, env, io_xres=False)
        declare_l1attn_io(nc, env, io_xres=False)
        declare_l1ffn_io(nc, env, io_xres=False)
        env.mkchains()
        alloc_common_sbuf(env)
        with env.scope():
            alloc_l0attn_sbuf(env)
            ph_l0_setup(env)
            ph_l0_cols(env)
            ph_l0_xres_init(env)
            ph_l0_prepass(env)
            ph_l0_scores(env)
            ph_l0_epilogue(env)
        with env.scope():
            alloc_l0ffn_sbuf(env)
            ph_l0ffn_setup(env)
            ph_l0ffn(env)
        with env.scope():
            alloc_l1attn_sbuf(env)
            ph_l1attn_setup(env)
            ph_l1attn(env)
        with env.scope():
            alloc_l1ffn_sbuf(env)
            ph_l1ffn_xn(env)
            ph_l1ffn(env)
        ph_dump_xres(env)
    return nc


def kernel(**inputs):
    cores = list(range(8))
    nc = build_full()
    r = run_bass_kernel_spmd(nc, [prep_full(inputs, c) for c in cores], core_ids=cores)
    out = np.empty((2, SEQ, D), np.float32)
    for c in cores:
        b, qoff = c // 4, (c % 4) * TOK
        out[b, qoff:qoff + TOK, :] = np.asarray(r.results[c]["xres_out"], np.float32).T
    return out


# version 08 (final)
# speedup vs baseline: 1.1273x; 1.0230x over previous
import math
import numpy as np
import ml_dtypes
import concourse.bass as bass
import concourse.mybir as mybir
from concourse.bass_utils import run_bass_kernel_spmd
from contextlib import ExitStack

F32 = mybir.dt.float32
BF16 = mybir.dt.bfloat16
AF = mybir.ActivationFunctionType
ALU = mybir.AluOpType
AX = mybir.AxisListType

P = 128
D = 1024
DCH = 8
SEQ = 8192
TOK = 2048
NST = 16
STRIPE = 512
MAGIC = 12582912.0
TWO_PI = 2.0 * math.pi
INV_2PI = 1.0 / TWO_PI
SQRT_HD = math.sqrt(128.0)

DBG_SKIP_ROWSUMS = False
DBG_L0_DUMPS = False
DBG_NQS = 4
DBG_F0_DUMPS = False
DBG_SKIP_COLSV = False
C_ZERO, C_ONE, C_NEGSQRT, C_INVSQRT, C_LNEPS, C_NEGHALF, C_HALF_PI, C_EIGHTH = range(8)
DC_INVW, DC_TBK, DC_TBKC, DC_MCK, DC_MCKC, DC_TBQ, DC_TBQC, DC_MCQ, DC_MCQC, DC_TMP = range(10)



class Stages:

    def __init__(self, sem):
        self.sem = sem
        self.items = []

    def add(self, eng, fn):
        self.items.append([(eng, fn)])
        return len(self.items) - 1

    def add_parallel(self, members):
        self.items.append(list(members))
        return len(self.items) - 1

    def run(self, section, name):
        base = 0
        for grp in self.items:
            for eng, fn in grp:
                if eng == name:
                    section.wait_ge(self.sem, base)
                    last = fn(section)
                    last.then_inc(self.sem, 1)
            base += len(grp)

    def stage_end_count(self, idx):
        return sum(len(g) for g in self.items[:idx + 1])

    def attach(self, blk, engines=("tensor", "scalar", "vector", "gpsimd", "sync")):
        st = self
        if "tensor" in engines:
            @blk.tensor
            def _(t):
                st.run(t, "tensor")
        if "scalar" in engines:
            @blk.scalar
            def _(sc):
                st.run(sc, "scalar")
        if "vector" in engines:
            @blk.vector
            def _(v):
                st.run(v, "vector")
        if "gpsimd" in engines:
            @blk.gpsimd
            def _(g):
                st.run(g, "gpsimd")
        if "sync" in engines:
            @blk.sync
            def _(s):
                st.run(s, "sync")


class Env:
    def __init__(self, nc, ctx):
        self.nc = nc
        self.ctx = ctx
        self.scopes = [ctx]
        self.t = {}
        self.a = {}
        self.core = 0
        self.qoff = 0

    def scope(self):
        from contextlib import contextmanager

        @contextmanager
        def _s():
            es = ExitStack()
            self.scopes.append(es)
            try:
                with es:
                    yield es
            finally:
                self.scopes.pop()
        return _s()

    def sb(self, name, shape, dtype):
        self.t[name] = self.scopes[-1].enter_context(self.nc.sbuf_tensor(name + "_s", shape, dtype))
        return self.t[name]

    def sem(self, name):
        self.t[name] = self.ctx.enter_context(self.nc.semaphore(name))
        return self.t[name]

    def mkchains(self):
        self.chn = {}
        for nm in ("vector", "scalar", "gpsimd"):
            self.chn[nm] = [self.sem("ch_" + nm), 0]

    def fence(self, sec, nm):
        sem, n = self.chn[nm]
        if n:
            sec.wait_ge(sem, n)

    def step(self, nm, ins):
        e = self.chn[nm]
        ins.then_inc(e[0], 1)
        e[1] += 1
        return ins



def declare_l0attn_io(nc, env):
    A = env.a
    A["xb"] = nc.dram_tensor("xb", [SEQ, D], F32, kind="ExternalInput").ap()
    A["wqT"] = nc.dram_tensor("wqT", [D, P], F32, kind="ExternalInput").ap()
    A["wkT"] = nc.dram_tensor("wkT", [D, P], F32, kind="ExternalInput").ap()
    A["wvT"] = nc.dram_tensor("wvT", [D, P], F32, kind="ExternalInput").ap()
    A["woT"] = nc.dram_tensor("woT", [P, D], F32, kind="ExternalInput").ap()
    A["cols0"] = nc.dram_tensor("cols0", [P, 8], F32, kind="ExternalInput").ap()
    A["bo_row"] = nc.dram_tensor("bo_row", [1, D], F32, kind="ExternalInput").ap()
    A["consts"] = nc.dram_tensor("consts", [P, 8], F32, kind="ExternalInput").ap()
    A["ident"] = nc.dram_tensor("ident", [P, P], F32, kind="ExternalInput").ap()
    A["xq"] = nc.dram_tensor("xq", [TOK, D], F32, kind="ExternalInput").ap()
    A["xb16"] = nc.dram_tensor("xb16", [SEQ, D], BF16, kind="Internal").ap()
    A["xq16"] = nc.dram_tensor("xq16", [TOK, D], BF16, kind="Internal").ap()
    A["xres_out"] = nc.dram_tensor("xres_out", [D, TOK], F32, kind="ExternalOutput").ap()
    if DBG_L0_DUMPS:
        A["d_cosK"] = nc.dram_tensor("d_cosK", [P, SEQ], BF16, kind="ExternalOutput").ap()
        A["d_sinK"] = nc.dram_tensor("d_sinK", [P, SEQ], BF16, kind="ExternalOutput").ap()
        A["d_cosQ"] = nc.dram_tensor("d_cosQ", [P, TOK], BF16, kind="ExternalOutput").ap()
        A["d_sinQ"] = nc.dram_tensor("d_sinQ", [P, TOK], BF16, kind="ExternalOutput").ap()
        A["d_vtok"] = nc.dram_tensor("d_vtok", [P, 64 * P], BF16, kind="ExternalOutput").ap()
        A["d_attn"] = nc.dram_tensor("d_attn", [P, TOK], F32, kind="ExternalOutput").ap()
        A["d_den"] = nc.dram_tensor("d_den", [1, TOK], F32, kind="ExternalOutput").ap()
        A["d_dcols"] = nc.dram_tensor("d_dcols", [P, 12], F32, kind="ExternalOutput").ap()
        A["d_rs"] = nc.dram_tensor("d_rs", [1, 3 * P], BF16, kind="ExternalOutput").ap()


def alloc_common_sbuf(env):
    sb = env.sb
    sb("x_res", [P, DCH, TOK], F32)
    sb("consts", [P, 8], F32)
    sb("id_f32", [P, P], F32)
    sb("id_bf", [P, P], BF16)
    sb("ones_row", [1, P], BF16)
    sb("ones_col", [P, 1], BF16)
    sb("ones32", [P, 32], BF16)
    sb("ones_row512", [1, STRIPE], BF16)


def alloc_l0attn_sbuf(env):
    sb = env.sb
    sb("cosK", [P, SEQ], BF16)
    sb("sinK", [P, SEQ], BF16)
    sb("v_tok", [P, 64, P], BF16)
    sb("cosQ", [P, TOK], BF16)
    sb("sinQ", [P, TOK], BF16)
    sb("wq_sb", [P, DCH, P], BF16)
    sb("wk_sb", [P, DCH, P], BF16)
    sb("wv_sb", [P, DCH, P], BF16)
    sb("wo_sb", [P, D], BF16)
    sb("bo_bf", [1, DCH, P], BF16)
    sb("cols0", [P, 8], F32)
    sb("dcols", [P, 12], F32)
    sb("rs_bf", [1, 3, P], BF16)
    sb("attn_acc", [P, TOK], F32)
    sb("denom_acc", [1, TOK], F32)
    sb("w_buf", [P, 4, TOK], BF16)
    sb("xt", [P, 2, DCH, STRIPE], BF16)
    sb("sq_bf", [P, 4, STRIPE], BF16)
    sb("row_negm", [1, STRIPE], F32)
    sb("row_var", [1, STRIPE], F32)
    sb("row_rstd", [1, STRIPE], F32)
    sb("row_negm_bf", [1, STRIPE], BF16)
    sb("row_rstd_bf", [1, STRIPE], BF16)
    sb("rstd_bc", [P, STRIPE], F32)
    sb("th", [P, STRIPE], F32)
    sb("uu", [P, STRIPE], F32)
    sb("si_c", [P, STRIPE], F32)
    sb("si_s", [P, STRIPE], F32)
    sb("vT_bf", [P, STRIPE], BF16)
    sb("dinv_bf", [1, STRIPE], BF16)


def ph_l0_setup(env):
    nc = env.nc
    T = env.t
    A = env.a
    ds = env.sem("ds0")
    dg = env.sem("dg0")
    gx = env.sem("gx")
    cs = env.sem("cs0")
    T = env.t

    with nc.Block("setup") as blk:
        @blk.sync
        def _(s):
            s.dma_start(T["cols0"][:], A["cols0"]).then_inc(ds, 16)
            s.dma_start(T["consts"][:], A["consts"]).then_inc(ds, 16)
            s.dma_start(T["id_f32"][:], A["ident"]).then_inc(ds, 16)
            s.wait_ge(ds, 48)

        @blk.gpsimd
        def _(g):
            g.dma_start(T["wq_sb"][:], A["wqT"].rearrange("(c p) h -> p c h", p=P)).then_inc(dg, 16)
            g.dma_start(T["wk_sb"][:], A["wkT"].rearrange("(c p) h -> p c h", p=P)).then_inc(dg, 16)
            g.dma_start(T["wv_sb"][:], A["wvT"].rearrange("(c p) h -> p c h", p=P)).then_inc(dg, 16)
            g.dma_start(T["wo_sb"][:], A["woT"]).then_inc(dg, 16)
            g.dma_start(T["bo_bf"][:], A["bo_row"].rearrange("one (c p) -> one c p", p=P)).then_inc(dg, 16)
            g.dma_start(T["id_bf"][:], A["ident"]).then_inc(dg, 16)
            g.wait_ge(dg, 96)
            g.memset(T["ones_row"][:], 1.0)
            g.memset(T["ones_col"][:], 1.0)
            g.memset(T["ones32"][:], 1.0)
            g.memset(T["ones_row512"][:], 1.0)
            g.memset(T["attn_acc"][:], 0.0)
            g.memset(T["denom_acc"][:], 0.0)

    return


def ph_l0_cols(env):
    nc = env.nc
    T = env.t
    A = env.a
    gx = T["gx"]
    cs = T["cs0"]
    with nc.Block("cols") as blk:
        @blk.vector
        def _(v):
            if DBG_SKIP_COLSV:
                return
            dc = T["dcols"]
            c0 = T["cols0"]
            v.tensor_scalar(dc[:, DC_TMP:DC_TMP + 1], c0[:, 0:1], 1e-8, None, ALU.add).then_inc(cs, 1)
            v.wait_ge(cs, 1)
            v.reciprocal(dc[:, DC_INVW:DC_INVW + 1], dc[:, DC_TMP:DC_TMP + 1]).then_inc(cs, 1)
            v.wait_ge(cs, 2)
            v.tensor_tensor(dc[:, DC_TBK:DC_TBK + 1], dc[:, DC_INVW:DC_INVW + 1], c0[:, 3:4], ALU.mult)
            v.tensor_tensor(dc[:, DC_TBQ:DC_TBQ + 1], dc[:, DC_INVW:DC_INVW + 1], c0[:, 2:3], ALU.mult).then_inc(cs, 1)
            v.wait_ge(cs, 3)
            v.tensor_tensor(dc[:, DC_TBK:DC_TBK + 1], dc[:, DC_TBK:DC_TBK + 1], c0[:, 1:2], ALU.add)
            v.tensor_tensor(dc[:, DC_TBQ:DC_TBQ + 1], dc[:, DC_TBQ:DC_TBQ + 1], c0[:, 1:2], ALU.add).then_inc(cs, 1)
            v.wait_ge(cs, 4)
            v.tensor_scalar(dc[:, DC_TBKC:DC_TBKC + 1], dc[:, DC_TBK:DC_TBK + 1], math.pi / 2, None, ALU.add)
            v.tensor_scalar(dc[:, DC_MCK:DC_MCK + 1], dc[:, DC_TBK:DC_TBK + 1], INV_2PI, MAGIC, ALU.mult, ALU.add)
            v.tensor_scalar(dc[:, DC_TBQC:DC_TBQC + 1], dc[:, DC_TBQ:DC_TBQ + 1], math.pi / 2, None, ALU.add)
            v.tensor_scalar(dc[:, DC_MCQ:DC_MCQ + 1], dc[:, DC_TBQ:DC_TBQ + 1], INV_2PI, MAGIC, ALU.mult, ALU.add).then_inc(cs, 1)
            v.wait_ge(cs, 5)
            v.tensor_scalar(dc[:, DC_MCKC:DC_MCKC + 1], dc[:, DC_TBKC:DC_TBKC + 1], INV_2PI, MAGIC, ALU.mult, ALU.add)
            v.tensor_scalar(dc[:, DC_MCQC:DC_MCQC + 1], dc[:, DC_TBQC:DC_TBQC + 1], INV_2PI, MAGIC, ALU.mult, ALU.add)

    if DBG_SKIP_ROWSUMS:
        return
    with ExitStack() as pctx:
        prows = [pctx.enter_context(nc.psum_tensor(f"prow{i}", [32, STRIPE], F32))
                 for i in range(3)]
        with nc.Block("rowsums") as blk:
            @blk.tensor
            def _(t):
                for i, w in enumerate(("wq_sb", "wk_sb", "wv_sb")):
                    for dch in range(DCH):
                        ins = t.matmul(prows[i][0:32, 0:P], T["ones32"][:], T[w][:, dch, :],
                                       start=(dch == 0), stop=(dch == DCH - 1))
                ins.then_inc(cs, 1)

            @blk.vector
            def _(v):
                v.wait_ge(cs, 6)
                for i in range(3):
                    v.tensor_copy(T["rs_bf"][0:1, i, :], prows[i][0:1, 0:P])


def ph_l0_xres_init(env):
    nc = env.nc
    T = env.t
    A = env.a
    dsx = env.sem("dsx")
    stg = env.sem("stg1")
    stages = Stages(stg)
    stag = T["w_buf"].bitcast(F32)

    with ExitStack() as pctx:
        ptrf = pctx.enter_context(nc.psum_tensor("ptrf", [P, 8, P], F32))

        for st in range(8):
            def load(s, st=st):
                src = A["xq"][st * 256:(st + 1) * 256, :]
                s.dma_start(stag[:, 0:2, :], src.rearrange("(tt p) d -> p tt d", p=P)).then_inc(dsx, 16)
                s.wait_ge(dsx, (st + 1) * 16)
                return s.nop()
            stages.add("sync", load)
            for tt in range(2):
                def trans(t, tt=tt):
                    for dch in range(DCH):
                        ins = t.transpose(ptrf[:, dch, :],
                                          stag[:, tt, dch * P:(dch + 1) * P],
                                          T["id_f32"][:])
                    return ins
                stages.add("tensor", trans)

                def evac(v, st=st, tt=tt):
                    for dch in range(DCH):
                        t0 = st * 256 + tt * P
                        ins = v.tensor_copy(T["x_res"][:, dch, t0:t0 + P], ptrf[:, dch, :])
                    return ins
                stages.add("vector", evac)

        with nc.Block("xres_init") as blk:
            stages.attach(blk, ("sync", "tensor", "vector"))


def ph_l0_prepass(env):
    nc = env.nc
    T = env.t
    A = env.a
    gx = T["gx"]
    dt_ = env.sem("dt")
    stg = env.sem("stg2")
    stages = Stages(stg)

    with ExitStack() as pctx:
        psx = pctx.enter_context(nc.psum_tensor("psx", [32, STRIPE], F32))
        psx2 = pctx.enter_context(nc.psum_tensor("psx2", [32, STRIPE], F32))
        pbc = pctx.enter_context(nc.psum_tensor("pbc", [P, STRIPE], F32))
        pk = pctx.enter_context(nc.psum_tensor("pk", [P, STRIPE], F32))
        pv = pctx.enter_context(nc.psum_tensor("pv", [P, STRIPE], F32))
        pq = pctx.enter_context(nc.psum_tensor("pq", [P, STRIPE], F32))
        ptb = pctx.enter_context(nc.psum_tensor("ptb", [P, 8, P], BF16))

        def sqv(j):
            return T["w_buf"][:, j % 4, (j // 4) * STRIPE:(j // 4 + 1) * STRIPE]

        def emit_theta(v, psrc, tb_c, tb_s):
            env.step("vector", v.scalar_tensor_tensor(T["th"][:], psrc,
                                                      T["dcols"][:, DC_INVW:DC_INVW + 1],
                                                      T["rstd_bc"][:], ALU.mult, ALU.mult))
            for tb, si in ((tb_c, "si_c"), (tb_s, "si_s")):
                env.fence(v, "vector")
                env.step("vector", v.tensor_scalar(T[si][:], T["th"][:], 1.0,
                                                   T["dcols"][:, tb:tb + 1], ALU.mult, ALU.add))
                env.fence(v, "vector")
                env.step("vector", v.tensor_scalar(T["uu"][:], T[si][:], INV_2PI, MAGIC,
                                                   ALU.mult, ALU.add))
                env.step("vector", v.tensor_scalar(T["uu"][:], T["uu"][:], MAGIC, TWO_PI,
                                                   ALU.subtract, ALU.mult))
                env.fence(v, "vector")
                ins = v.tensor_sub(T[si][:], T[si][:], T["uu"][:])
                if si == "si_c":
                    env.step("vector", ins)
            return ins

        SKEW = 5
        slots = {}

        def put(t, eng, fn):
            slots.setdefault(t, []).append((eng, fn))

        proj_stage = {}
        for st in range(NST + DBG_NQS):
            kv = st < NST
            b = st % 2
            t0 = SKEW * st

            def sq(sc, st=st, b=b):
                sc.wait_ge(dt_, (st + 1) * 128)
                for j in range(DCH):
                    ins = sc.activation(sqv(j), T["xt"][:, b, j, :], AF.Square)
                return ins
            put(t0 + 1, "scalar", sq)

            def statsx(t, st=st, b=b):
                t.wait_ge(dt_, (st + 1) * 128)
                for dch in range(DCH):
                    t.matmul(psx[0:32, :], T["ones32"][:], T["xt"][:, b, dch, :],
                             start=(dch == 0), stop=(dch == DCH - 1))
                for j in range(DCH):
                    ins = t.matmul(psx2[0:32, :], T["ones32"][:], sqv(j),
                                   start=(j == 0), stop=(j == DCH - 1))
                return ins
            put(t0 + 2, "tensor", statsx)

            def rows1(v):
                env.step("vector", v.tensor_scalar(T["row_var"][0:1, :], psx2[0:1, :],
                                                   1.0 / D, None, ALU.mult))
                env.step("vector", v.tensor_scalar(T["row_negm"][0:1, :], psx[0:1, :],
                                                   -1.0 / D, None, ALU.mult))
                env.fence(v, "vector")
                env.step("vector", v.tensor_tensor(T["row_rstd"][0:1, :], T["row_negm"][0:1, :],
                                                   T["row_negm"][0:1, :], ALU.mult))
                env.fence(v, "vector")
                env.step("vector", v.tensor_sub(T["row_var"][0:1, :], T["row_var"][0:1, :],
                                                T["row_rstd"][0:1, :]))
                return v.tensor_copy(T["row_negm_bf"][0:1, :], T["row_negm"][0:1, :])
            put(t0 + 3, "vector", rows1)

            def rstd(sc):
                env.step("scalar", sc.activation(T["row_rstd"][0:1, :], T["row_var"][0:1, :], AF.Ln,
                              bias=T["consts"][0:1, C_LNEPS:C_LNEPS + 1], scale=1.0))
                env.fence(sc, "scalar")
                return sc.activation(T["row_var"][0:1, :], T["row_rstd"][0:1, :], AF.Exp,
                                     bias=T["consts"][0:1, C_ZERO:C_ZERO + 1],
                                     scale=T["consts"][0:1, C_NEGHALF:C_NEGHALF + 1])
            put(t0 + 4, "scalar", rstd)

            def rows2(v):
                return v.tensor_copy(T["row_rstd_bf"][0:1, :], T["row_var"][0:1, :])
            put(t0 + 5, "vector", rows2)

            def projs(t, b=b, kv=kv):
                t.matmul(pbc[:], T["ones_row"][:], T["row_rstd_bf"][0:1, :], start=True, stop=True)
                wl_ = (("wk_sb", pk, 1), ("wv_sb", pv, 2)) if kv else (("wq_sb", pq, 0),)
                for w, pdst, rsi in wl_:
                    for dch in range(DCH):
                        t.matmul(pdst[:], T[w][:, dch, :], T["xt"][:, b, dch, :],
                                 start=(dch == 0), stop=False)
                    ins = t.matmul(pdst[:], T["rs_bf"][0:1, rsi, :], T["row_negm_bf"][0:1, :],
                                   start=False, stop=True)
                return ins
            put(t0 + 6, "tensor", projs)
            proj_stage[st] = t0 + 6

            if kv:
                def dve_k(v):
                    env.step("vector", v.tensor_copy(T["rstd_bc"][:], pbc[:]))
                    env.fence(v, "vector")
                    emit_theta(v, pk[:], DC_TBKC, DC_TBK)
                    env.fence(v, "vector")
                    return v.tensor_tensor(T["th"][:], pv[:], T["rstd_bc"][:], ALU.mult)
                put(t0 + 7, "vector", dve_k)

                def act_k(sc, st=st):
                    s0 = st * STRIPE
                    sc.activation(T["cosK"][:, s0:s0 + STRIPE], T["si_c"][:], AF.Sin,
                                  bias=T["consts"][:, C_ZERO:C_ZERO + 1], scale=1.0)
                    sc.activation(T["sinK"][:, s0:s0 + STRIPE], T["si_s"][:], AF.Sin,
                                  bias=T["consts"][:, C_ZERO:C_ZERO + 1], scale=1.0)
                    return sc.activation(T["vT_bf"][:], T["th"][:], AF.Identity,
                                         bias=T["cols0"][:, 4:5],
                                         scale=T["consts"][:, C_ONE:C_ONE + 1])
                put(t0 + 8, "scalar", act_k)

                def vtrans(t):
                    for j in range(4):
                        ins = t.transpose(ptb[:, j, :], T["vT_bf"][:, j * P:(j + 1) * P],
                                          T["id_bf"][:])
                    return ins
                put(t0 + 9, "tensor", vtrans)

                def vevac(v, st=st):
                    for j in range(4):
                        ins = v.tensor_copy(T["v_tok"][:, st * 4 + j, :], ptb[:, j, :])
                    return ins
                put(t0 + 11, "vector", vevac)
            else:
                def dve_q(v):
                    env.step("vector", v.tensor_copy(T["rstd_bc"][:], pbc[:]))
                    env.fence(v, "vector")
                    return emit_theta(v, pq[:], DC_TBQC, DC_TBQ)
                put(t0 + 7, "vector", dve_q)

                def act_q(sc, st=st):
                    q0 = (st - NST) * STRIPE
                    sc.activation(T["cosQ"][:, q0:q0 + STRIPE], T["si_c"][:], AF.Sin,
                                  bias=T["consts"][:, C_ZERO:C_ZERO + 1], scale=1.0)
                    return sc.activation(T["sinQ"][:, q0:q0 + STRIPE], T["si_s"][:], AF.Sin,
                                         bias=T["consts"][:, C_ZERO:C_ZERO + 1], scale=1.0)
                put(t0 + 8, "scalar", act_q)

        last_xt_reader = {}
        for t in sorted(slots):
            idx = stages.add_parallel(slots[t])
            for st, ts in proj_stage.items():
                if ts == t:
                    last_xt_reader[st] = idx

        with nc.Block("prepass") as blk:
            @blk.gpsimd
            def _(g):
                A2 = env.a
                for s_ in range(NST):
                    g.dma_start(A2["xb16"][s_ * STRIPE:(s_ + 1) * STRIPE, :],
                                A2["xb"][s_ * STRIPE:(s_ + 1) * STRIPE, :]).then_inc(gx, 16)
                for s_ in range(4):
                    g.dma_start(A2["xq16"][s_ * STRIPE:(s_ + 1) * STRIPE, :],
                                A2["xq"][s_ * STRIPE:(s_ + 1) * STRIPE, :]).then_inc(gx, 16)

            @blk.sync
            def _(s):
                for st in range(NST + DBG_NQS):
                    s.wait_ge(gx, (st + 1) * 16)
                    if st >= 2:
                        s.wait_ge(stg, stages.stage_end_count(last_xt_reader[st - 2]))
                    if st < NST:
                        srcd = A["xb16"][st * STRIPE:(st + 1) * STRIPE, :]
                    else:
                        srcd = A["xq16"][(st - NST) * STRIPE:(st - NST + 1) * STRIPE, :]
                    for dch in range(DCH):
                        s.dma_start(
                            T["xt"][:, st % 2, dch, :],
                            srcd[:, dch * P:(dch + 1) * P],
                            transpose=True).then_inc(dt_, 16)

            stages.attach(blk, ("tensor", "scalar", "vector"))


def ph_l0_scores(env):
    nc = env.nc
    T = env.t
    psc = env.sem("psc")
    se2 = env.sem("se2")
    pvb = env.sem("pvb")
    va = env.sem("va")
    G = 4
    NG = 64 // G

    with ExitStack() as pctx:
        S = pctx.enter_context(nc.psum_tensor("S", [P, 2, STRIPE], F32))
        Aps = pctx.enter_context(nc.psum_tensor("Aps", [P, 4, STRIPE], F32))
        D1 = pctx.enter_context(nc.psum_tensor("D1", [P, STRIPE], F32))
        D2 = pctx.enter_context(nc.psum_tensor("D2", [32, STRIPE], F32))

        with nc.Block("scores") as blk:
            @blk.tensor
            def _(t):
                for g in range(NG):
                    for j in range(G * 4):
                        kt, qc = g * G + j // 4, j % 4
                        idx = g * G * 4 + j + 1
                        if idx >= 3:
                            t.wait_ge(se2, idx - 2)
                        k0 = kt * P
                        t.matmul(S[:, (idx - 1) % 2, :], T["cosK"][:, k0:k0 + P],
                                 T["cosQ"][:, qc * STRIPE:(qc + 1) * STRIPE],
                                 start=True, stop=False)
                        t.matmul(S[:, (idx - 1) % 2, :], T["sinK"][:, k0:k0 + P],
                                 T["sinQ"][:, qc * STRIPE:(qc + 1) * STRIPE],
                                 start=False, stop=True).then_inc(psc, 1)
                    t.wait_ge(se2, (g + 1) * G * 4)
                    if g >= 1:
                        t.wait_ge(va, g)
                    for qc in range(4):
                        for kk in range(G):
                            t.matmul(Aps[:, qc, :], T["v_tok"][:, g * G + kk, :],
                                     T["w_buf"][:, kk, qc * STRIPE:(qc + 1) * STRIPE],
                                     start=(kk == 0), stop=(kk == G - 1))
                    for qc in range(4):
                        dtile = D1[qc * 32:(qc + 1) * 32, :] if qc < 3 else D2[:, :]
                        for kk in range(G):
                            ins = t.matmul(dtile, T["ones32"][:],
                                           T["w_buf"][:, kk, qc * STRIPE:(qc + 1) * STRIPE],
                                           start=(kk == 0), stop=(kk == G - 1))
                    ins.then_inc(pvb, 1)

            @blk.scalar
            def _(sc):
                for g in range(NG):
                    for j in range(G * 4):
                        kt, qc = g * G + j // 4, j % 4
                        idx = g * G * 4 + j + 1
                        sc.wait_ge(psc, idx)
                        if g >= 1 and j == 0:
                            sc.wait_ge(pvb, g)
                        sc.activation(T["w_buf"][:, kt % G, qc * STRIPE:(qc + 1) * STRIPE],
                                      S[:, (idx - 1) % 2, :], AF.Exp,
                                      bias=T["consts"][:, C_NEGSQRT:C_NEGSQRT + 1],
                                      scale=T["consts"][:, C_INVSQRT:C_INVSQRT + 1]
                                      ).then_inc(se2, 1)

            @blk.vector
            def _(v):
                for g in range(NG):
                    v.wait_ge(pvb, g + 1)
                    for qc in range(4):
                        v.tensor_add(T["attn_acc"][:, qc * STRIPE:(qc + 1) * STRIPE],
                                     T["attn_acc"][:, qc * STRIPE:(qc + 1) * STRIPE],
                                     Aps[:, qc, :])
                    for qc in range(4):
                        srcp = D1[qc * 32:qc * 32 + 1, :] if qc < 3 else D2[0:1, :]
                        ins = v.tensor_add(T["denom_acc"][0:1, qc * STRIPE:(qc + 1) * STRIPE],
                                           T["denom_acc"][0:1, qc * STRIPE:(qc + 1) * STRIPE],
                                           srcp)
                    ins.then_inc(va, 1)


def ph_l0_epilogue(env):
    nc = env.nc
    T = env.t
    stg = env.sem("stg3")
    stages = Stages(stg)

    with ExitStack() as pctx:
        pbc2 = pctx.enter_context(nc.psum_tensor("pbc2", [P, STRIPE], F32))
        py = pctx.enter_context(nc.psum_tensor("py", [P, 2, STRIPE], F32))

        for qc in range(4):
            def recip(v, qc=qc):
                return v.reciprocal(T["row_var"][0:1, :],
                                    T["denom_acc"][0:1, qc * STRIPE:(qc + 1) * STRIPE])
            stages.add("vector", recip)

            def recip2(v):
                return v.tensor_copy(T["dinv_bf"][0:1, :], T["row_var"][0:1, :])
            stages.add("vector", recip2)

            def bc(t, qc=qc):
                return t.matmul(pbc2[:], T["ones_row"][:], T["dinv_bf"][0:1, :],
                                start=True, stop=True)
            stages.add("tensor", bc)

            def norm(v, qc=qc):
                return v.tensor_tensor(T["w_buf"][:, 0, qc * STRIPE:(qc + 1) * STRIPE],
                                       T["attn_acc"][:, qc * STRIPE:(qc + 1) * STRIPE],
                                       pbc2[:], ALU.mult)
            stages.add("vector", norm)

        for och in range(DCH):
            for qc in range(4):
                def proj(t, och=och, qc=qc):
                    pslot = py[:, (och * 4 + qc) % 2, :]
                    t.matmul(pslot, T["wo_sb"][:, och * P:(och + 1) * P],
                             T["w_buf"][:, 0, qc * STRIPE:(qc + 1) * STRIPE],
                             start=True, stop=False)
                    return t.matmul(pslot, T["bo_bf"][0:1, och, :], T["ones_row512"][0:1, :],
                                    start=False, stop=True)
                stages.add("tensor", proj)

                def resadd(v, och=och, qc=qc):
                    dst = T["x_res"][:, och, qc * STRIPE:(qc + 1) * STRIPE]
                    return v.tensor_add(dst, dst, py[:, (och * 4 + qc) % 2, :])
                stages.add("vector", resadd)

        with nc.Block("l0epi") as blk:
            stages.attach(blk, ("tensor", "vector"))


def ph_dump_xres(env):
    nc = env.nc
    T = env.t
    A = env.a
    do = env.sem("do")
    n = [0]
    with nc.Block("dump") as blk:
        @blk.sync
        def _(s):
            def d(dst, srcap):
                s.dma_start(dst, srcap).then_inc(do, 16)
                n[0] += 16
            d(A["xres_out"].rearrange("(c p) t -> p c t", p=P), T["x_res"][:])
            if DBG_L0_DUMPS:
                d(A["d_cosK"], T["cosK"][:])
                d(A["d_sinK"], T["sinK"][:])
                d(A["d_cosQ"], T["cosQ"][:])
                d(A["d_sinQ"], T["sinQ"][:])
                d(A["d_vtok"].rearrange("p (a b) -> p a b", b=P), T["v_tok"][:])
                d(A["d_attn"], T["attn_acc"][:])
                d(A["d_den"], T["denom_acc"][:])
                d(A["d_dcols"], T["dcols"][:])
        @blk.gpsimd
        def _(g):
            if DBG_L0_DUMPS:
                g.dma_start(A["d_rs"], T["rs_bf"][0:1, :, :].rearrange("a b c -> a (b c)")).then_inc(do, 16)
                n[0] += 16
            g.wait_ge(do, n[0])
        @blk.vector
        def _(v):
            pass
    with nc.Block("dumpwait") as blk:
        @blk.sync
        def _(s):
            s.wait_ge(do, n[0])


def build_l0attn(core, upto=6):
    nc = bass.Bass("TRN2", target_bir_lowering=False)
    with ExitStack() as ctx:
        env = Env(nc, ctx)
        env.core = core
        env.qoff = (core % 4) * TOK
        declare_l0attn_io(nc, env)
        env.mkchains()
        alloc_common_sbuf(env)
        alloc_l0attn_sbuf(env)
        phases = [ph_l0_setup, ph_l0_cols, ph_l0_xres_init, ph_l0_prepass, ph_l0_scores, ph_l0_epilogue]
        for i, ph in enumerate(phases):
            if i < upto:
                ph(env)
        ph_dump_xres(env)
    return nc


def prep_l0attn_inputs(I, core):
    f32 = np.float32
    b = core // 4
    x = np.asarray(I["L0_attn_x"], f32)
    cols = np.zeros((P, 8), f32)
    cols[:, 0] = np.asarray(I["L0_attn_wavelengths"], f32)
    cols[:, 1] = np.asarray(I["L0_attn_phase_bias"], f32)
    cols[:, 2] = np.asarray(I["L0_attn_bq"], f32)
    cols[:, 3] = np.asarray(I["L0_attn_bk"], f32)
    cols[:, 4] = np.asarray(I["L0_attn_bv"], f32)
    consts = np.zeros((P, 8), f32)
    consts[:, C_ONE] = 1.0
    consts[:, C_NEGSQRT] = -SQRT_HD
    consts[:, C_INVSQRT] = 1.0 / SQRT_HD
    consts[:, C_LNEPS] = 1e-5
    consts[:, C_NEGHALF] = -0.5
    consts[:, C_HALF_PI] = math.pi / 2.0
    consts[:, C_EIGHTH] = 0.125
    qoff = (core % 4) * TOK
    return {
        "xb": np.ascontiguousarray(x[b]),
        "xq": np.ascontiguousarray(x[b][qoff:qoff + TOK]),
        "wqT": np.ascontiguousarray(np.asarray(I["L0_attn_Wq"], f32).T),
        "wkT": np.ascontiguousarray(np.asarray(I["L0_attn_Wk"], f32).T),
        "wvT": np.ascontiguousarray(np.asarray(I["L0_attn_Wv"], f32).T),
        "woT": np.ascontiguousarray(np.asarray(I["L0_attn_Wo"], f32).T),
        "cols0": cols,
        "bo_row": np.ascontiguousarray(np.asarray(I["L0_attn_bo"], f32).reshape(1, D)),
        "consts": consts,
        "ident": np.eye(P, dtype=f32),
    }





def declare_l0ffn_io(nc, env, io_xres=True):
    A = env.a
    A["bwT"] = nc.dram_tensor("bwT", [D, D], F32, kind="ExternalInput").ap()
    A["sAT"] = nc.dram_tensor("sAT", [D, 32], F32, kind="ExternalInput").ap()
    A["sBT"] = nc.dram_tensor("sBT", [16, D], F32, kind="ExternalInput").ap()
    A["rAT"] = nc.dram_tensor("rAT", [D, 96], F32, kind="ExternalInput").ap()
    A["rBT"] = nc.dram_tensor("rBT", [96, D], F32, kind="ExternalInput").ap()
    A["rWT"] = nc.dram_tensor("rWT", [D, 32], F32, kind="ExternalInput").ap()
    A["rb_row"] = nc.dram_tensor("rb_row", [1, 32], F32, kind="ExternalInput").ap()
    A["bb_row"] = nc.dram_tensor("bb_row", [1, D], F32, kind="ExternalInput").ap()
    if io_xres:
        A["xres_in"] = nc.dram_tensor("xres_in", [D, TOK], F32, kind="ExternalInput").ap()
        A["consts"] = nc.dram_tensor("consts", [P, 8], F32, kind="ExternalInput").ap()
        A["ident"] = nc.dram_tensor("ident", [P, P], F32, kind="ExternalInput").ap()
        A["xres_out"] = nc.dram_tensor("xres_out", [D, TOK], F32, kind="ExternalOutput").ap()


def alloc_l0ffn_sbuf(env):
    sb = env.sb
    sb("f_bwT", [P, DCH, D], BF16)
    sb("f_sAT", [P, DCH, 32], BF16)
    sb("f_sBT", [16, D], BF16)
    sb("f_rAT", [P, DCH, 96], BF16)
    sb("f_rBT", [96, D], BF16)
    sb("f_rWT", [P, DCH, 32], BF16)
    sb("f_rb", [1, 32], BF16)
    sb("f_rstd_bf_row", [1, STRIPE], BF16)
    sb("f_logsb", [8, STRIPE], F32)
    sb("f_mbc2", [P, STRIPE], F32)
    sb("f_bb", [1, DCH, P], BF16)
    sb("f_xn", [P, DCH, STRIPE], BF16)
    sb("f_sq", [P, 4, STRIPE], BF16)
    sb("f_s1", [16, STRIPE], BF16)
    sb("f_glog", [P, 4, 8], F32)
    sb("f_rsc", [P, 4, 8], F32)
    sb("f_rsc2", [P, 4, 8], F32)
    sb("f_rsc3", [P, 4, 8], F32)
    sb("f_rcol", [P, 32], F32)
    sb("f_h2", [P, 4, 96], BF16)
    sb("f_h2T", [P, 4, P], BF16)
    sb("f_htmp", [P, 96], F32)
    sb("f_rstdtok", [P, 8], F32)
    sb("f_mbc", [P, STRIPE], F32)
    sb("f_row_negm", [1, STRIPE], F32)
    sb("f_row_var", [1, STRIPE], F32)
    sb("f_row_rstd", [1, STRIPE], F32)
    sb("f_negm_bf", [1, STRIPE], BF16)
    sb("f_rstd_bc", [P, STRIPE], F32)


def ph_l0ffn_setup(env):
    nc = env.nc
    T = env.t
    A = env.a
    dg = env.sem("fdg")
    with nc.Block("ffn0_setup") as blk:
        @blk.gpsimd
        def _(g):
            n = 0
            for name, ap, pat in (
                ("f_bwT", A["bwT"], "(c p) o -> p c o"),
                ("f_sAT", A["sAT"], "(c p) r -> p c r"),
                ("f_rAT", A["rAT"], "(c p) r -> p c r"),
                ("f_rWT", A["rWT"], "(c p) r -> p c r"),
            ):
                g.dma_start(T[name][:], ap.rearrange(pat, p=P)).then_inc(dg, 16)
                n += 16
            g.dma_start(T["f_sBT"][:], A["sBT"]).then_inc(dg, 16)
            g.dma_start(T["f_rBT"][:], A["rBT"]).then_inc(dg, 16)
            g.dma_start(T["f_rb"][:], A["rb_row"]).then_inc(dg, 16)
            g.dma_start(T["f_bb"][:], A["bb_row"].rearrange("one (c p) -> one c p", p=P)).then_inc(dg, 16)
            n += 64
            g.wait_ge(dg, n)


def ph_l0ffn(env):
    nc = env.nc
    T = env.t
    stg = env.sem("fstg")
    stages = Stages(stg)

    with ExitStack() as pctx:
        B = [pctx.enter_context(nc.psum_tensor(f"fb{i}", [P, STRIPE], F32))
             for i in range(8)]
        Bb = [b.bitcast(BF16) for b in B]

        pend_yadd = None
        for s in range(4):
            t0s = s * STRIPE

            def cast(g, t0s=t0s):
                for dch in range(DCH):
                    ins = g.tensor_copy(T["f_xn"][:, dch, :],
                                        T["x_res"][:, dch, t0s:t0s + STRIPE])
                return ins

            def cast_next(g, t0s=t0s + STRIPE):
                for dch in range(DCH):
                    ins = g.tensor_copy(T["f_xn"][:, dch, :],
                                        T["x_res"][:, dch, t0s:t0s + STRIPE])
                return ins

            def statsx(t):
                for dch in range(DCH):
                    ins = t.matmul(B[0][0:32, :], T["ones32"][:], T["f_xn"][:, dch, :],
                                   start=(dch == 0), stop=(dch == DCH - 1))
                return ins

            def mk_sq(half):
                def sq(sc, half=half):
                    for j in range(4):
                        ins = sc.activation(T["f_sq"][:, j, :], T["f_xn"][:, half * 4 + j, :],
                                            AF.Square)
                    return ins
                return sq

            def mk_stats(bi):
                def stats(t, bi=bi):
                    for j in range(4):
                        ins = t.matmul(B[bi][0:32, :], T["ones32"][:], T["f_sq"][:, j, :],
                                       start=(j == 0), stop=(j == 3))
                    return ins
                return stats

            if s == 0:
                stages.add("gpsimd", cast)
            stages.add_parallel([("tensor", statsx), ("scalar", mk_sq(0))])
            stages.add("tensor", mk_stats(1))
            stages.add("scalar", mk_sq(1))
            stages.add("tensor", mk_stats(2))

            def rows1(v):
                env.step("vector", v.tensor_copy(T["f_row_var"][0:1, :], B[1][0:1, :]))
                env.fence(v, "vector")
                env.step("vector", v.tensor_add(T["f_row_var"][0:1, :], T["f_row_var"][0:1, :], B[2][0:1, :]))
                env.step("vector", v.tensor_scalar(T["f_row_negm"][0:1, :], B[0][0:1, :], -1.0 / D, None, ALU.mult))
                env.fence(v, "vector")
                env.step("vector", v.tensor_tensor(T["f_row_rstd"][0:1, :], T["f_row_negm"][0:1, :],
                                                   T["f_row_negm"][0:1, :], ALU.mult))
                env.step("vector", v.tensor_scalar(T["f_row_var"][0:1, :], T["f_row_var"][0:1, :], 1.0 / D, None, ALU.mult))
                env.fence(v, "vector")
                env.step("vector", v.tensor_sub(T["f_row_var"][0:1, :], T["f_row_var"][0:1, :], T["f_row_rstd"][0:1, :]))
                return v.tensor_copy(T["f_negm_bf"][0:1, :], T["f_row_negm"][0:1, :])
            stages.add("vector", rows1)

            def rstd(sc):
                env.step("scalar", sc.activation(T["f_row_rstd"][0:1, :], T["f_row_var"][0:1, :], AF.Ln,
                              bias=T["consts"][0:1, C_LNEPS:C_LNEPS + 1], scale=1.0))
                env.fence(sc, "scalar")
                return sc.activation(T["f_row_var"][0:1, :], T["f_row_rstd"][0:1, :], AF.Exp,
                                     bias=T["consts"][0:1, C_ZERO:C_ZERO + 1],
                                     scale=T["consts"][0:1, C_NEGHALF:C_NEGHALF + 1])
            stages.add("scalar", rstd)

            def rows2b(v):
                return v.tensor_copy(T["f_rstd_bf_row"][0:1, :], T["f_row_var"][0:1, :])
            stages.add("vector", rows2b)

            def bc(t):
                t.matmul(B[3][:], T["ones_row"][:], T["f_rstd_bf_row"][0:1, :], start=True, stop=True)
                return t.matmul(B[4][:], T["ones_row"][:], T["f_negm_bf"][0:1, :], start=True, stop=True)
            stages.add("tensor", bc)

            def xn(v, t0s=t0s):
                env.step("vector", v.tensor_copy(T["f_rstd_bc"][:], B[3][:]))
                env.step("vector", v.tensor_copy(T["f_mbc"][:], B[4][:]))
                for dch in range(DCH):
                    env.fence(v, "vector")
                    env.step("vector", v.tensor_add(T["f_mbc2"][:],
                                                    T["x_res"][:, dch, t0s:t0s + STRIPE],
                                                    T["f_mbc"][:]))
                    env.fence(v, "vector")
                    ins = v.tensor_mul(T["f_xn"][:, dch, :], T["f_mbc2"][:], T["f_rstd_bc"][:])
                    if dch < DCH - 1:
                        env.step("vector", ins)
                return ins
            stages.add("vector", xn)

            def s1mm(t):
                for dch in range(DCH):
                    ins = t.matmul(B[0][0:32, :], T["f_sAT"][:, dch, :], T["f_xn"][:, dch, :],
                                   start=(dch == 0), stop=(dch == DCH - 1))
                return ins
            stages.add("tensor", s1mm)

            def logmm(t):
                for dch in range(DCH):
                    t.matmul(B[1][0:32, :], T["f_rWT"][:, dch, :], T["f_xn"][:, dch, :],
                             start=(dch == 0), stop=False)
                return t.matmul(B[1][0:32, :], T["f_rb"][0:1, :], T["ones_row512"][0:1, :],
                                start=False, stop=True)
            stages.add("tensor", logmm)

            def s1evac(v):
                v.tensor_copy(T["f_s1"][0:16, :], B[0][0:16, :])
                return v.tensor_copy(T["f_logsb"][0:8, :], B[1][0:8, :])
            stages.add("vector", s1evac)

            def gtr(t):
                for tt in range(4):
                    t.transpose(B[2][:, tt * 8:tt * 8 + 6],
                                T["f_logsb"][0:6, tt * P:(tt + 1) * P],
                                T["id_f32"][0:6, 0:6])
                for tt in range(4):
                    ins = t.transpose(B[5][:, 2 * tt:2 * tt + 1],
                                      T["f_row_var"][0:1, tt * P:(tt + 1) * P],
                                      T["id_f32"][0:1, 0:1])
                return ins
            stages.add("tensor", gtr)

            def gevac(v):
                for tt in range(4):
                    v.tensor_copy(T["f_glog"][:, tt, 0:6], B[2][:, tt * 8:tt * 8 + 6])
                return v.tensor_copy(T["f_rstdtok"][:, 0:8], B[5][:, 0:8])
            stages.add("vector", gevac)

            def phmm(t):
                for tt in range(4):
                    bsel = B[5] if tt % 2 == 0 else B[4]
                    off = 256 + (tt // 2) * 96
                    for dch in range(DCH):
                        ins = t.matmul(bsel[:, off:off + 96],
                                       T["f_xn"][:, dch, tt * P:(tt + 1) * P],
                                       T["f_rAT"][:, dch, :],
                                       start=(dch == 0), stop=(dch == DCH - 1))
                return ins
            stages.add("tensor", phmm)

            def r12(v):
                for tt in range(4):
                    c = tt * 8
                    env.step("vector", v.reduce_max(T["f_rcol"][:, c:c + 1],
                                                    T["f_glog"][:, tt, 0:6], axis=AX.X))
                    env.fence(v, "vector")
                    ins = v.tensor_scalar(T["f_rcol"][:, c + 1:c + 2], T["f_rcol"][:, c:c + 1],
                                          -1.0, None, ALU.mult)
                    if tt < 3:
                        env.step("vector", ins)
                return ins
            stages.add("vector", r12)

            def r3m(sc):
                for tt in range(4):
                    c = tt * 8
                    ins = sc.activation(T["f_rsc"][:, tt, 0:6], T["f_glog"][:, tt, 0:6],
                                        AF.Exp, bias=T["f_rcol"][:, c + 1:c + 2], scale=1.0)
                return ins
            stages.add("scalar", r3m)

            def r4567(v):
                for tt in range(4):
                    c = tt * 8
                    env.step("vector", v.reduce_sum(T["f_rcol"][:, c + 2:c + 3],
                                                    T["f_rsc"][:, tt, 0:6], axis=AX.X))
                    env.step("vector", v.tensor_single_scalar(T["f_rsc2"][:, tt, 0:6],
                                                              T["f_glog"][:, tt, 0:6],
                                                              T["f_rcol"][:, c:c + 1], ALU.is_equal))
                    env.fence(v, "vector")
                    env.step("vector", v.scalar_tensor_tensor(T["f_rsc3"][:, tt, 0:6],
                                                              T["f_rsc2"][:, tt, 0:6],
                                                              -1e30, T["f_glog"][:, tt, 0:6],
                                                              ALU.mult, ALU.add))
                    env.fence(v, "vector")
                    env.step("vector", v.reduce_max(T["f_rcol"][:, c + 3:c + 4],
                                                    T["f_rsc3"][:, tt, 0:6], axis=AX.X))
                    env.fence(v, "vector")
                    env.step("vector", v.tensor_single_scalar(T["f_rsc3"][:, tt, 0:6],
                                                              T["f_glog"][:, tt, 0:6],
                                                              T["f_rcol"][:, c + 3:c + 4], ALU.is_ge))
                    env.fence(v, "vector")
                    env.step("vector", v.tensor_mul(T["f_rsc2"][:, tt, 0:6], T["f_rsc"][:, tt, 0:6],
                                                    T["f_rsc3"][:, tt, 0:6]))
                    env.fence(v, "vector")
                    env.step("vector", v.reduce_sum(T["f_rcol"][:, c + 4:c + 5],
                                                    T["f_rsc2"][:, tt, 0:6], axis=AX.X))
                    env.fence(v, "vector")
                    env.step("vector", v.scalar_tensor_tensor(T["f_rcol"][:, c + 5:c + 6],
                                                              T["f_rcol"][:, c + 2:c + 3],
                                                              1e-6, T["f_rcol"][:, c + 4:c + 5],
                                                              ALU.mult, ALU.add))
                    env.fence(v, "vector")
                    env.step("vector", v.reciprocal(T["f_rcol"][:, c + 6:c + 7],
                                                    T["f_rcol"][:, c + 5:c + 6]))
                    env.fence(v, "vector")
                    ins = v.tensor_single_scalar(T["f_rsc3"][:, tt, 0:6],
                                                 T["f_rsc2"][:, tt, 0:6],
                                                 T["f_rcol"][:, c + 6:c + 7], ALU.mult)
                    if tt < 3:
                        env.step("vector", ins)
                return ins
            stages.add("vector", r4567)

            def h2m(v):
                for tt in range(4):
                    bsel = B[5] if tt % 2 == 0 else B[4]
                    off = 256 + (tt // 2) * 96
                    env.step("vector", v.tensor_single_scalar(T["f_htmp"][:, 0:96],
                                                              bsel[:, off:off + 96],
                                                              T["f_rstdtok"][:, 2 * tt:2 * tt + 1],
                                                              ALU.mult))
                    env.fence(v, "vector")
                    for e in range(6):
                        ins = v.tensor_single_scalar(T["f_h2"][:, tt, e * 16:(e + 1) * 16],
                                                     T["f_htmp"][:, e * 16:(e + 1) * 16],
                                                     T["f_rsc3"][:, tt, e:e + 1], ALU.mult)
                        if not (tt == 3 and e == 5):
                            env.step("vector", ins)
                return ins
            stages.add("vector", h2m)

            def h2tr(t):
                for tt in range(4):
                    ins = t.transpose(Bb[6][0:96, tt * P:(tt + 1) * P],
                                      T["f_h2"][:, tt, :], T["id_bf"][:])
                return ins
            stages.add("tensor", h2tr)

            def h2evac(v):
                for tt in range(4):
                    ins = v.tensor_copy(T["f_h2T"][0:96, tt, :], Bb[6][0:96, tt * P:(tt + 1) * P])
                return ins
            stages.add("vector", h2evac)

            for och in range(DCH):
                def ymm(t, och=och):
                    yb = B[7] if och % 2 == 0 else B[3]
                    for dch in range(DCH):
                        t.matmul(yb[:], T["f_bwT"][:, dch, och * P:(och + 1) * P],
                                 T["f_xn"][:, dch, :], start=(dch == 0), stop=False)
                    t.matmul(yb[:], T["f_sBT"][0:16, och * P:(och + 1) * P],
                             T["f_s1"][0:16, :], start=False, stop=False)
                    t.matmul(yb[:], T["f_rBT"][0:96, och * P:(och + 1) * P],
                             T["f_h2T"][0:96, :, :].rearrange("p a b -> p (a b)"),
                             start=False, stop=False)
                    return t.matmul(yb[:], T["f_bb"][0:1, och, :], T["ones_row512"][0:1, :],
                                    start=False, stop=True)
                def yadd(v, och=och, t0s=t0s):
                    yb = B[7] if och % 2 == 0 else B[3]
                    dst = T["x_res"][:, och, t0s:t0s + STRIPE]
                    return v.tensor_add(dst, dst, yb[:])

                members = [("tensor", ymm)]
                if och > 0:
                    members.append(("vector", prev_yadd))
                stages.add_parallel(members)
                prev_yadd = yadd
            if s < 3:
                stages.add_parallel([("gpsimd", cast_next), ("vector", prev_yadd)])
            else:
                stages.add("vector", prev_yadd)

        with nc.Block("ffn0") as blk:
            stages.attach(blk, ("tensor", "scalar", "vector", "gpsimd"))


def ph_load_xres(env):
    nc = env.nc
    T = env.t
    A = env.a
    dl = env.sem("dlx")
    with nc.Block("loadx") as blk:
        @blk.sync
        def _(s):
            s.dma_start(T["x_res"][:],
                        A["xres_in"].rearrange("(c p) t -> p c t", p=P)).then_inc(dl, 16)
            s.wait_ge(dl, 16)
        @blk.gpsimd
        def _(g):
            g.dma_start(T["id_bf"][:], A["ident"]).then_inc(dl, 16)
            g.wait_ge(dl, 32)
        @blk.scalar
        def _(sc):
            pass


def build_l0ffn(core):
    nc = bass.Bass("TRN2", target_bir_lowering=False)
    with ExitStack() as ctx:
        env = Env(nc, ctx)
        declare_l0ffn_io(nc, env)
        env.mkchains()
        if DBG_F0_DUMPS:
            A = env.a
            for nm, shp, dt_ in (("g_xn", [P, DCH * STRIPE], BF16), ("g_glog", [P, 32], F32),
                                 ("g_wn", [P, 32], F32), ("g_s1", [16, STRIPE], BF16),
                                 ("g_h2T", [P, 4 * P], BF16), ("g_rstdtok", [P, 8], F32),
                                 ("g_rows", [1, 3 * STRIPE], F32), ("g_rcol", [P, 16], F32), ("g_e", [P, 32], F32)):
                A[nm] = nc.dram_tensor(nm, shp, dt_, kind="ExternalOutput").ap()
        alloc_common_sbuf(env)
        alloc_l0ffn_sbuf(env)
        dsc = env.sem("dsc")
        with nc.Block("c0") as blk:
            @blk.sync
            def _(s):
                s.dma_start(env.t["consts"][:], env.a["consts"]).then_inc(dsc, 16)
                s.dma_start(env.t["id_f32"][:], env.a["ident"]).then_inc(dsc, 16)
                s.wait_ge(dsc, 32)
            @blk.gpsimd
            def _(g):
                g.memset(env.t["ones_row"][:], 1.0)
                g.memset(env.t["ones_col"][:], 1.0)
                g.memset(env.t["ones32"][:], 1.0)
                g.memset(env.t["ones_row512"][:], 1.0)
        ph_load_xres(env)
        ph_l0ffn_setup(env)
        ph_l0ffn(env)
        if DBG_F0_DUMPS:
            T = env.t
            A = env.a
            dd = env.sem("ddf")
            with nc.Block("fdump") as blk:
                @blk.sync
                def _(s):
                    s.dma_start(A["g_xn"], T["f_xn"][:].rearrange("p a b -> p (a b)")).then_inc(dd, 16)
                    s.dma_start(A["g_glog"], T["f_glog"][:].rearrange("p a b -> p (a b)")).then_inc(dd, 16)
                    s.dma_start(A["g_wn"], T["f_rsc3"][:].rearrange("p a b -> p (a b)")).then_inc(dd, 16)
                    s.dma_start(A["g_h2T"], T["f_h2T"][:].rearrange("p a b -> p (a b)")).then_inc(dd, 16)
                    s.dma_start(A["g_rstdtok"], T["f_rstdtok"][:]).then_inc(dd, 16)
                    s.dma_start(A["g_rcol"], T["f_rcol"][:]).then_inc(dd, 16)
                    s.dma_start(A["g_e"], T["f_rsc"][:].rearrange("p a b -> p (a b)")).then_inc(dd, 16)
                    s.wait_ge(dd, 112)
                @blk.gpsimd
                def _(g):
                    g.dma_start(A["g_s1"], T["f_s1"][:]).then_inc(dd, 16)
                    rows = [T["f_row_negm"], T["f_row_var"], T["f_row_rstd"]]
                    for i, r in enumerate(rows):
                        g.dma_start(A["g_rows"][0:1, i * STRIPE:(i + 1) * STRIPE], r[0:1, :]).then_inc(dd, 16)
                    g.wait_ge(dd, 176)
        ph_dump_xres(env)
    return nc





def declare_l1attn_io(nc, env, io_xres=True):
    A = env.a
    A["wqc"] = nc.dram_tensor("wqc", [D, D], F32, kind="ExternalInput").ap()
    A["wkc"] = nc.dram_tensor("wkc", [D, D], F32, kind="ExternalInput").ap()
    A["wvc"] = nc.dram_tensor("wvc", [D, D], F32, kind="ExternalInput").ap()
    A["woc"] = nc.dram_tensor("woc", [D, D], F32, kind="ExternalInput").ap()
    A["bq_row"] = nc.dram_tensor("bq_row", [1, D], F32, kind="ExternalInput").ap()
    A["bk_row"] = nc.dram_tensor("bk_row", [1, D], F32, kind="ExternalInput").ap()
    A["bv_row"] = nc.dram_tensor("bv_row", [1, D], F32, kind="ExternalInput").ap()
    A["boc_row"] = nc.dram_tensor("boc_row", [1, D], F32, kind="ExternalInput").ap()
    A["mrows"] = nc.dram_tensor("mrows", [P, 4 * P], F32, kind="ExternalInput").ap()
    if io_xres:
        A["xres_in"] = nc.dram_tensor("xres_in", [D, TOK], F32, kind="ExternalInput").ap()
        A["consts"] = nc.dram_tensor("consts", [P, 8], F32, kind="ExternalInput").ap()
        A["ident"] = nc.dram_tensor("ident", [P, P], F32, kind="ExternalInput").ap()
        A["xres_out"] = nc.dram_tensor("xres_out", [D, TOK], F32, kind="ExternalOutput").ap()


def alloc_l1attn_sbuf(env):
    sb = env.sb
    sb("a_wq", [P, DCH, D], BF16)
    sb("a_wk", [P, DCH, D], BF16)
    sb("a_wv", [P, DCH, D], BF16)
    sb("a_wo", [P, DCH, D], BF16)
    sb("a_bq", [1, DCH, P], BF16)
    sb("a_bk", [1, DCH, P], BF16)
    sb("a_bv", [1, D], BF16)
    sb("a_bo", [1, DCH, P], BF16)
    sb("a_bvbc", [P, D], BF16)
    sb("a_xn", [P, DCH, STRIPE], BF16)
    sb("a_sq", [P, 4, STRIPE], BF16)
    sb("a_qT", [P, DCH, STRIPE], BF16)
    sb("a_kT", [P, DCH, STRIPE], BF16)
    sb("a_vtok", [P, 4, 16 * 65], BF16)
    sb("a_vals", [P, 4, D], BF16)
    sb("a_valsT", [P, DCH, STRIPE], BF16)
    sb("a_wexp", [P, 2, P], BF16)
    sb("a_mrows", [P, 4, P], BF16)
    sb("a_iv", [P, 1], F32)
    sb("a_mbc", [P, STRIPE], F32)
    sb("a_mbc2", [P, STRIPE], F32)
    sb("a_rstd_bc", [P, STRIPE], F32)
    sb("a_row_negm", [1, STRIPE], F32)
    sb("a_row_var", [1, STRIPE], F32)
    sb("a_row_rstd", [1, STRIPE], F32)
    sb("a_negm_bf", [1, STRIPE], BF16)
    sb("a_rstd_bf", [1, STRIPE], BF16)


def ph_l1attn_setup(env):
    nc = env.nc
    T = env.t
    A = env.a
    dg = env.sem("adg")
    cs = env.sem("acs")
    with nc.Block("attn1_setup") as blk:
        @blk.gpsimd
        def _(g):
            n = 0
            for name, ap in (("a_wq", A["wqc"]), ("a_wk", A["wkc"]),
                             ("a_wv", A["wvc"]), ("a_wo", A["woc"])):
                g.dma_start(T[name][:], ap.rearrange("(c p) o -> p c o", p=P)).then_inc(dg, 16)
                n += 16
            for name, ap in (("a_bq", A["bq_row"]), ("a_bk", A["bk_row"]),
                             ("a_bo", A["boc_row"])):
                g.dma_start(T[name][:], ap.rearrange("one (c p) -> one c p", p=P)).then_inc(dg, 16)
                n += 16
            g.dma_start(T["a_bv"][:], A["bv_row"]).then_inc(dg, 16)
            g.dma_start(T["a_mrows"][:].rearrange("p a b -> p (a b)"), A["mrows"]).then_inc(dg, 16)
            n += 32
            g.wait_ge(dg, n)
            g.memset(T["a_vtok"][:].rearrange("p a (h c) -> p a h c", c=65)[:, :, :, 64:65], 1.0)



def ph_l1attn(env):
    nc = env.nc
    T = env.t
    stg = env.sem("astg")
    stages = Stages(stg)
    ps3 = env.sem("aps3")
    pv3 = env.sem("apv3")
    se3 = env.sem("ase3")
    ve3 = env.sem("ave3")

    import os
    PAIR = set(os.environ.get("A1PAIR", "qk,v,vt,y,pre").split(","))

    def addp(stages, members, key):
        if key in PAIR:
            stages.add_parallel(members)
        else:
            for m in members:
                stages.add(*m)

    with ExitStack() as pctx:
        B = [pctx.enter_context(nc.psum_tensor(f"ab{i}", [P, STRIPE], F32))
             for i in range(7)]
        Bb6 = B[6].bitcast(BF16)

        def bvb(t):
            t.matmul(B[0][:], T["ones_row"][:], T["a_bv"][0:1, 0:STRIPE], start=True, stop=True)
            return t.matmul(B[1][:], T["ones_row"][:], T["a_bv"][0:1, STRIPE:D],
                            start=True, stop=True)
        stages.add("tensor", bvb)

        def bvbe(v):
            env.step("vector", v.tensor_copy(T["a_bvbc"][:, 0:STRIPE], B[0][:]))
            return v.tensor_copy(T["a_bvbc"][:, STRIPE:D], B[1][:])
        stages.add("vector", bvbe)

        Bb5 = B[5].bitcast(BF16)
        pend_yadd = None
        for s in range(4):
            t0s = s * STRIPE

            def cast(g, t0s=t0s):
                for dch in range(DCH):
                    ins = g.tensor_copy(T["a_xn"][:, dch, :],
                                        T["x_res"][:, dch, t0s:t0s + STRIPE])
                return ins

            def cast_next(g, t0s=t0s + STRIPE):
                for dch in range(DCH):
                    ins = g.tensor_copy(T["a_xn"][:, dch, :],
                                        T["x_res"][:, dch, t0s:t0s + STRIPE])
                return ins

            def statsx(t):
                for dch in range(DCH):
                    ins = t.matmul(B[0][0:32, :], T["ones32"][:], T["a_xn"][:, dch, :],
                                   start=(dch == 0), stop=(dch == DCH - 1))
                return ins

            def mk_sq(half):
                def sq(sc, half=half):
                    for j in range(4):
                        ins = sc.activation(T["a_sq"][:, j, :], T["a_xn"][:, half * 4 + j, :],
                                            AF.Square)
                    return ins
                return sq

            def mk_stats(bi):
                def stats(t, bi=bi):
                    for j in range(4):
                        ins = t.matmul(B[bi][0:32, :], T["ones32"][:], T["a_sq"][:, j, :],
                                       start=(j == 0), stop=(j == 3))
                    return ins
                return stats

            if s == 0:
                stages.add("gpsimd", cast)
                addp(stages, [("tensor", statsx), ("scalar", mk_sq(0))], "pre")
            else:
                addp(stages, [("tensor", statsx), ("vector", pend_yadd),
                              ("scalar", mk_sq(0))], "pre")
            stages.add("tensor", mk_stats(1))
            stages.add("scalar", mk_sq(1))
            stages.add("tensor", mk_stats(2))

            def rows1(v):
                env.step("vector", v.tensor_copy(T["a_row_var"][0:1, :], B[1][0:1, :]))
                env.fence(v, "vector")
                env.step("vector", v.tensor_add(T["a_row_var"][0:1, :], T["a_row_var"][0:1, :], B[2][0:1, :]))
                env.step("vector", v.tensor_scalar(T["a_row_negm"][0:1, :], B[0][0:1, :], -1.0 / D, None, ALU.mult))
                env.fence(v, "vector")
                env.step("vector", v.tensor_tensor(T["a_row_rstd"][0:1, :], T["a_row_negm"][0:1, :],
                                                   T["a_row_negm"][0:1, :], ALU.mult))
                env.step("vector", v.tensor_scalar(T["a_row_var"][0:1, :], T["a_row_var"][0:1, :], 1.0 / D, None, ALU.mult))
                env.fence(v, "vector")
                env.step("vector", v.tensor_sub(T["a_row_var"][0:1, :], T["a_row_var"][0:1, :], T["a_row_rstd"][0:1, :]))
                return v.tensor_copy(T["a_negm_bf"][0:1, :], T["a_row_negm"][0:1, :])
            stages.add("vector", rows1)

            def rstd(sc):
                env.step("scalar", sc.activation(T["a_row_rstd"][0:1, :], T["a_row_var"][0:1, :], AF.Ln,
                                                 bias=T["consts"][0:1, C_LNEPS:C_LNEPS + 1], scale=1.0))
                env.fence(sc, "scalar")
                return sc.activation(T["a_row_var"][0:1, :], T["a_row_rstd"][0:1, :], AF.Exp,
                                     bias=T["consts"][0:1, C_ZERO:C_ZERO + 1],
                                     scale=T["consts"][0:1, C_NEGHALF:C_NEGHALF + 1])
            stages.add("scalar", rstd)

            def rows2b(v):
                return v.tensor_copy(T["a_rstd_bf"][0:1, :], T["a_row_var"][0:1, :])
            stages.add("vector", rows2b)

            def bc(t):
                t.matmul(B[3][:], T["ones_row"][:], T["a_rstd_bf"][0:1, :], start=True, stop=True)
                return t.matmul(B[4][:], T["ones_row"][:], T["a_negm_bf"][0:1, :], start=True, stop=True)
            stages.add("tensor", bc)

            def bcevac_xn(v, t0s=t0s):
                env.step("vector", v.tensor_copy(T["a_rstd_bc"][:], B[3][:]))
                env.step("vector", v.tensor_copy(T["a_mbc"][:], B[4][:]))
                for dch in range(DCH):
                    env.fence(v, "vector")
                    env.step("vector", v.tensor_add(T["a_mbc2"][:],
                                                    T["x_res"][:, dch, t0s:t0s + STRIPE],
                                                    T["a_mbc"][:]))
                    env.fence(v, "vector")
                    ins = v.tensor_mul(T["a_xn"][:, dch, :], T["a_mbc2"][:], T["a_rstd_bc"][:])
                    if dch < DCH - 1:
                        env.step("vector", ins)
                return ins
            stages.add("vector", bcevac_xn)

            pend = None
            for wname, bname, dst in (("a_wq", "a_bq", "a_qT"), ("a_wk", "a_bk", "a_kT")):
                for j in range(DCH):
                    def proj(t, wname=wname, bname=bname, j=j):
                        bsel = B[j % 2]
                        for dch in range(DCH):
                            t.matmul(bsel[:], T[wname][:, dch, j * P:(j + 1) * P],
                                     T["a_xn"][:, dch, :], start=(dch == 0), stop=False)
                        return t.matmul(bsel[:], T[bname][0:1, j, :], T["ones_row512"][0:1, :],
                                        start=False, stop=True)

                    def pevac(v, dst=dst, j=j):
                        return v.tensor_copy(T[dst][:, j, :], B[j % 2][:])

                    members = [("tensor", proj)]
                    if pend is not None:
                        members.append(("vector", pend))
                    addp(stages, members, "qk")
                    pend = pevac

            for tt in range(4):
                for half in range(2):
                    def vproj(t, tt=tt, half=half):
                        bsel = B[(tt * 2 + half) % 2]
                        for dch in range(DCH):
                            ins = t.matmul(bsel[:], T["a_xn"][:, dch, tt * P:(tt + 1) * P],
                                           T["a_wv"][:, dch, half * STRIPE:(half + 1) * STRIPE],
                                           start=(dch == 0), stop=(dch == DCH - 1))
                        return ins

                    def vevac(v, tt=tt, half=half):
                        bsel = B[(tt * 2 + half) % 2]
                        env.step("vector", v.tensor_add(T["a_mbc2"][:], bsel[:],
                                                        T["a_bvbc"][:, half * STRIPE:(half + 1) * STRIPE]))
                        env.fence(v, "vector")
                        dstv = T["a_vtok"][:, tt, :].rearrange("p (h c) -> p h c", c=65)
                        h0 = half * 8
                        return v.tensor_copy(dstv[:, h0:h0 + 8, 0:64],
                                             T["a_mbc2"][:].rearrange("p (h c) -> p h c", c=64))

                    members = [("tensor", vproj)]
                    if pend is not None:
                        members.append(("vector", pend))
                    addp(stages, members, "v")
                    pend = vevac
            stages.add("vector", pend)

            def attn_pe(t, s=s):
                for u in range(64):
                    U = s * 64 + u + 1
                    tt, h = u // 16, u % 16
                    hb = (h % 2) * 64
                    sbank = B[5] if u % 2 == 0 else B[6]
                    pbank = B[4] if u % 2 == 0 else B[3]
                    if U > 2:
                        t.wait_ge(se3, U - 2)
                    ksub = T["a_kT"][hb:hb + 64, h // 2, tt * P:(tt + 1) * P]
                    qsub = T["a_qT"][hb:hb + 64, h // 2, tt * P:(tt + 1) * P]
                    t.matmul(sbank[:, 0:P], ksub, qsub, start=True, stop=False,
                             tile_position=(hb, 0))
                    t.matmul(sbank[:, 0:P], T["a_mrows"][hb:hb + 1, 0, :],
                             T["a_mrows"][hb:hb + 1, 1, :], start=False, stop=False)
                    t.matmul(sbank[:, 0:P], T["a_mrows"][hb:hb + 1, 2, :],
                             T["a_mrows"][hb:hb + 1, 3, :], start=False,
                             stop=True).then_inc(ps3, 1)
                    t.wait_ge(se3, U)
                    if U > 2:
                        t.wait_ge(ve3, U - 2)
                    vsub = T["a_vtok"][:, tt, h * 65:h * 65 + 65]
                    t.matmul(pbank[:, 0:65], T["a_wexp"][:, u % 2, :], vsub,
                             start=True, stop=True).then_inc(pv3, 1)
                return t.nop()

            def attn_act(sc, s=s):
                for u in range(64):
                    U = s * 64 + u + 1
                    sbank = B[5] if u % 2 == 0 else B[6]
                    sc.wait_ge(ps3, U)
                    if U > 2:
                        sc.wait_ge(pv3, U - 2)
                    sc.activation(T["a_wexp"][:, u % 2, :], sbank[:, 0:P], AF.Exp,
                                  bias=T["consts"][:, C_ZERO:C_ZERO + 1],
                                  scale=T["consts"][:, C_EIGHTH:C_EIGHTH + 1]
                                  ).then_inc(se3, 1)
                return sc.nop()

            def attn_dve(v, s=s):
                for u in range(64):
                    U = s * 64 + u + 1
                    tt, h = u // 16, u % 16
                    pbank = B[4] if u % 2 == 0 else B[3]
                    v.wait_ge(pv3, U)
                    env.step("vector", v.reciprocal(T["a_iv"][:, 0:1], pbank[:, 64:65]))
                    env.fence(v, "vector")
                    v.tensor_single_scalar(
                        T["a_vals"][:, tt, h * 64:(h + 1) * 64],
                        pbank[:, 0:64],
                        T["a_iv"][:, 0:1], ALU.mult).then_inc(ve3, 1)
                return v.nop()

            stages.add_parallel([("tensor", attn_pe), ("scalar", attn_act),
                                 ("vector", attn_dve)])

            pend = None
            for tt in range(4):
                def vtr(t, tt=tt, s=s):
                    bb = Bb6 if tt % 2 == 0 else Bb5
                    t.wait_ge(ve3, (s + 1) * 64)
                    for dch in range(DCH):
                        ins = t.transpose(bb[:, dch * P:(dch + 1) * P],
                                          T["a_vals"][:, tt, dch * P:(dch + 1) * P],
                                          T["id_bf"][:])
                    return ins

                def vtre(v, tt=tt):
                    bb = Bb6 if tt % 2 == 0 else Bb5
                    for dch in range(DCH):
                        ins = v.tensor_copy(T["a_valsT"][:, dch, tt * P:(tt + 1) * P],
                                            bb[:, dch * P:(dch + 1) * P])
                    return ins

                members = [("tensor", vtr)]
                if pend is not None:
                    members.append(("vector", pend))
                addp(stages, members, "vt")
                pend = vtre
            stages.add("vector", pend)
            pend = None

            for och in range(DCH):
                def ymm(t, och=och):
                    yb = B[0] if och % 2 == 0 else B[1]
                    for dch in range(DCH):
                        t.matmul(yb[:], T["a_wo"][:, dch, och * P:(och + 1) * P],
                                 T["a_valsT"][:, dch, :], start=(dch == 0), stop=False)
                    return t.matmul(yb[:], T["a_bo"][0:1, och, :], T["ones_row512"][0:1, :],
                                    start=False, stop=True)

                def yadd(v, och=och, t0s=t0s):
                    yb = B[0] if och % 2 == 0 else B[1]
                    dst = T["x_res"][:, och, t0s:t0s + STRIPE]
                    return v.tensor_add(dst, dst, yb[:])

                members = [("tensor", ymm)]
                if pend is not None:
                    members.append(("vector", pend))
                if och == 0 and s < 3:
                    members.append(("gpsimd", cast_next))
                addp(stages, members, "y")
                pend = yadd
            pend_yadd = pend
        stages.add("vector", pend_yadd)

        with nc.Block("attn1") as blk:
            stages.attach(blk, ("tensor", "scalar", "vector", "gpsimd"))


def build_l1attn(core):
    nc = bass.Bass("TRN2", target_bir_lowering=False)
    with ExitStack() as ctx:
        env = Env(nc, ctx)
        declare_l1attn_io(nc, env)
        env.mkchains()
        alloc_common_sbuf(env)
        alloc_l1attn_sbuf(env)
        dsc = env.sem("dsc")
        with nc.Block("c0") as blk:
            @blk.sync
            def _(s):
                s.dma_start(env.t["consts"][:], env.a["consts"]).then_inc(dsc, 16)
                s.dma_start(env.t["id_f32"][:], env.a["ident"]).then_inc(dsc, 16)
                s.wait_ge(dsc, 32)
            @blk.gpsimd
            def _(g):
                g.dma_start(env.t["id_bf"][:], env.a["ident"]).then_inc(dsc, 16)
                g.wait_ge(dsc, 48)
                g.memset(env.t["ones_row"][:], 1.0)
                g.memset(env.t["ones_col"][:], 1.0)
                g.memset(env.t["ones32"][:], 1.0)
                g.memset(env.t["ones_row512"][:], 1.0)
        ph_load_xres(env)
        ph_l1attn_setup(env)
        ph_l1attn(env)
        ph_dump_xres(env)
    return nc





NE1, NFG1, NFC1 = 8, 4, 8


def declare_l1ffn_io(nc, env, io_xres=True):
    A = env.a
    A["Wg"] = nc.dram_tensor("Wg", [NE1, D, 4096], F32, kind="ExternalInput").ap()
    A["Wu"] = nc.dram_tensor("Wu", [NE1, D, 4096], F32, kind="ExternalInput").ap()
    A["Wd"] = nc.dram_tensor("Wd", [NE1, 4096, D], F32, kind="ExternalInput").ap()
    if io_xres:
        A["xres_in"] = nc.dram_tensor("xres_in", [D, TOK], F32, kind="ExternalInput").ap()
        A["consts"] = nc.dram_tensor("consts", [P, 8], F32, kind="ExternalInput").ap()
        A["ident"] = nc.dram_tensor("ident", [P, P], F32, kind="ExternalInput").ap()
        A["xres_out"] = nc.dram_tensor("xres_out", [D, TOK], F32, kind="ExternalOutput").ap()


def alloc_l1ffn_sbuf(env):
    sb = env.sb
    sb("z_xn", [P, DCH, TOK], BF16)
    sb("z_hT", [P, NFC1, STRIPE], BF16)
    sb("z_wgu", [P, 2, 2, DCH, 1024], BF16)
    sb("z_wd", [P, DCH, D], BF16)
    sb("z_tmp", [P, 2, STRIPE], BF16)
    sb("z_sq", [P, 4, STRIPE], BF16)
    sb("z_mbc", [P, STRIPE], F32)
    sb("z_mbc2", [P, STRIPE], F32)
    sb("z_rstd_bc", [P, STRIPE], F32)
    sb("z_row_negm", [1, STRIPE], F32)
    sb("z_row_var", [1, STRIPE], F32)
    sb("z_row_rstd", [1, STRIPE], F32)
    sb("z_negm_bf", [1, STRIPE], BF16)
    sb("z_rstd_bf", [1, STRIPE], BF16)


def ph_l1ffn_xn(env):
    nc = env.nc
    T = env.t
    stg = env.sem("zstg")
    stages = Stages(stg)
    with ExitStack() as pctx:
        B = [pctx.enter_context(nc.psum_tensor(f"zb{i}", [P, STRIPE], F32))
             for i in range(5)]
        for s in range(4):
            t0s = s * STRIPE

            def cast(g, t0s=t0s):
                for dch in range(DCH):
                    ins = g.tensor_copy(T["z_xn"][:, dch, t0s:t0s + STRIPE],
                                        T["x_res"][:, dch, t0s:t0s + STRIPE])
                return ins
            stages.add("gpsimd", cast)

            def statsx(t, t0s=t0s):
                for dch in range(DCH):
                    ins = t.matmul(B[0][0:32, :], T["ones32"][:],
                                   T["z_xn"][:, dch, t0s:t0s + STRIPE],
                                   start=(dch == 0), stop=(dch == DCH - 1))
                return ins
            stages.add("tensor", statsx)
            for half, bi in ((0, 1), (1, 2)):
                def sq(sc, half=half, t0s=t0s):
                    for j in range(4):
                        ins = sc.activation(T["z_sq"][:, j, :],
                                            T["z_xn"][:, half * 4 + j, t0s:t0s + STRIPE],
                                            AF.Square)
                    return ins
                stages.add("scalar", sq)

                def stats(t, bi=bi):
                    for j in range(4):
                        ins = t.matmul(B[bi][0:32, :], T["ones32"][:], T["z_sq"][:, j, :],
                                       start=(j == 0), stop=(j == 3))
                    return ins
                stages.add("tensor", stats)

            def rows1(v):
                env.step("vector", v.tensor_copy(T["z_row_var"][0:1, :], B[1][0:1, :]))
                env.fence(v, "vector")
                env.step("vector", v.tensor_add(T["z_row_var"][0:1, :], T["z_row_var"][0:1, :], B[2][0:1, :]))
                env.step("vector", v.tensor_scalar(T["z_row_negm"][0:1, :], B[0][0:1, :], -1.0 / D, None, ALU.mult))
                env.fence(v, "vector")
                env.step("vector", v.tensor_tensor(T["z_row_rstd"][0:1, :], T["z_row_negm"][0:1, :],
                                                   T["z_row_negm"][0:1, :], ALU.mult))
                env.step("vector", v.tensor_scalar(T["z_row_var"][0:1, :], T["z_row_var"][0:1, :], 1.0 / D, None, ALU.mult))
                env.fence(v, "vector")
                env.step("vector", v.tensor_sub(T["z_row_var"][0:1, :], T["z_row_var"][0:1, :], T["z_row_rstd"][0:1, :]))
                return v.tensor_copy(T["z_negm_bf"][0:1, :], T["z_row_negm"][0:1, :])
            stages.add("vector", rows1)

            def rstd(sc):
                env.step("scalar", sc.activation(T["z_row_rstd"][0:1, :], T["z_row_var"][0:1, :], AF.Ln,
                                                 bias=T["consts"][0:1, C_LNEPS:C_LNEPS + 1], scale=1.0))
                env.fence(sc, "scalar")
                return sc.activation(T["z_row_var"][0:1, :], T["z_row_rstd"][0:1, :], AF.Exp,
                                     bias=T["consts"][0:1, C_ZERO:C_ZERO + 1],
                                     scale=T["consts"][0:1, C_NEGHALF:C_NEGHALF + 1])
            stages.add("scalar", rstd)

            def rows2b(v):
                return v.tensor_copy(T["z_rstd_bf"][0:1, :], T["z_row_var"][0:1, :])
            stages.add("vector", rows2b)

            def bc(t):
                t.matmul(B[3][:], T["ones_row"][:], T["z_rstd_bf"][0:1, :], start=True, stop=True)
                return t.matmul(B[4][:], T["ones_row"][:], T["z_negm_bf"][0:1, :], start=True, stop=True)
            stages.add("tensor", bc)

            def bcevac(v):
                env.step("vector", v.tensor_copy(T["z_rstd_bc"][:], B[3][:]))
                return v.tensor_copy(T["z_mbc"][:], B[4][:])
            stages.add("vector", bcevac)

            def xn(v, t0s=t0s):
                for dch in range(DCH):
                    env.fence(v, "vector")
                    env.step("vector", v.tensor_add(T["z_mbc2"][:],
                                                    T["x_res"][:, dch, t0s:t0s + STRIPE],
                                                    T["z_mbc"][:]))
                    env.fence(v, "vector")
                    ins = v.tensor_mul(T["z_xn"][:, dch, t0s:t0s + STRIPE], T["z_mbc2"][:],
                                       T["z_rstd_bc"][:])
                    if dch < DCH - 1:
                        env.step("vector", ins)
                return ins
            stages.add("vector", xn)

        with nc.Block("l1ffn_xn") as blk:
            stages.attach(blk, ("tensor", "scalar", "vector", "gpsimd"))


def ph_l1ffn(env):
    nc = env.nc
    T = env.t
    A = env.a
    wl = env.sem("zwl")
    peg = env.sem("zpeg")
    pey = env.sem("zpey")
    sse = env.sem("zsse")
    vh = env.sem("zvh")
    vy = env.sem("zvy")

    NG = NE1 * NFG1

    with ExitStack() as pctx:
        pg = [pctx.enter_context(nc.psum_tensor(f"zpg{i}", [P, STRIPE], F32)) for i in range(2)]
        pu = [pctx.enter_context(nc.psum_tensor(f"zpu{i}", [P, STRIPE], F32)) for i in range(2)]
        py = [pctx.enter_context(nc.psum_tensor(f"zpy{i}", [P, STRIPE], F32)) for i in range(2)]

        with nc.Block("l1ffn") as blk:
            @blk.gpsimd
            def _(g):
                for G in range(NG):
                    e, fg = G // NFG1, G % NFG1
                    if G >= 2:
                        g.wait_ge(peg, (G - 1) * 32)
                    g.dma_start(T["z_wgu"][:, G % 2, 0, :, :],
                                A["Wg"][e, :, fg * 1024:(fg + 1) * 1024]
                                .rearrange("(c p) f -> p c f", p=P)).then_inc(wl, 16)
                    g.dma_start(T["z_wgu"][:, G % 2, 1, :, :],
                                A["Wu"][e, :, fg * 1024:(fg + 1) * 1024]
                                .rearrange("(c p) f -> p c f", p=P)).then_inc(wl, 16)
                    if G >= 1:
                        g.wait_ge(pey, G * 32)
                    g.dma_start(T["z_wd"][:],
                                A["Wd"][e, fg * 1024:(fg + 1) * 1024, :]
                                .rearrange("(c p) o -> p c o", p=P)).then_inc(wl, 16)

            @blk.tensor
            def _(t):
                ngu = 0
                ny = 0
                for G in range(NG):
                    t.wait_ge(wl, G * 48 + 32)
                    for s in range(4):
                        t0s = s * STRIPE
                        for fc in range(NFC1):
                            par = ngu % 2
                            if ngu >= 2:
                                t.wait_ge(sse, ngu - 1)
                                t.wait_ge(vh, ngu - 1)
                            for dch in range(DCH):
                                t.matmul(pg[par][:], T["z_wgu"][:, G % 2, 0, dch, fc * P:(fc + 1) * P],
                                         T["z_xn"][:, dch, t0s:t0s + STRIPE],
                                         start=(dch == 0), stop=(dch == DCH - 1))
                            for dch in range(DCH):
                                ins = t.matmul(pu[par][:], T["z_wgu"][:, G % 2, 1, dch, fc * P:(fc + 1) * P],
                                               T["z_xn"][:, dch, t0s:t0s + STRIPE],
                                               start=(dch == 0), stop=(dch == DCH - 1))
                            ins.then_inc(peg, 1)
                            ngu += 1
                        if s == 0:
                            t.wait_ge(wl, 48 * (G + 1))
                        t.wait_ge(vh, ngu)
                        for och in range(DCH):
                            ypar = ny % 2
                            if ny >= 2:
                                t.wait_ge(vy, ny - 1)
                            for fc in range(NFC1):
                                ins = t.matmul(py[ypar][:], T["z_wd"][:, fc, och * P:(och + 1) * P],
                                               T["z_hT"][:, fc, :],
                                               start=(fc == 0), stop=(fc == NFC1 - 1))
                            ins.then_inc(pey, 1)
                            ny += 1

            @blk.scalar
            def _(sc):
                ngu = 0
                for G in range(NG):
                    for s in range(4):
                        for fc in range(NFC1):
                            ngu += 1
                            sc.wait_ge(peg, ngu)
                            if ngu >= 3:
                                sc.wait_ge(vh, ngu - 2)
                            sc.activation(T["z_tmp"][:, (ngu - 1) % 2, :], pg[(ngu - 1) % 2][:],
                                          AF.Silu).then_inc(sse, 1)

            @blk.vector
            def _(v):
                ngu = 0
                ny = 0
                for G in range(NG):
                    for s in range(4):
                        t0s = s * STRIPE
                        for fc in range(NFC1):
                            ngu += 1
                            v.wait_ge(sse, ngu)
                            v.tensor_tensor(T["z_hT"][:, fc, :], T["z_tmp"][:, (ngu - 1) % 2, :],
                                            pu[(ngu - 1) % 2][:], ALU.mult).then_inc(vh, 1)
                        for och in range(DCH):
                            ny += 1
                            v.wait_ge(pey, ny)
                            dst = T["x_res"][:, och, t0s:t0s + STRIPE]
                            v.tensor_add(dst, dst, py[(ny - 1) % 2][:]).then_inc(vy, 1)


def build_l1ffn(core):
    nc = bass.Bass("TRN2", target_bir_lowering=False)
    with ExitStack() as ctx:
        env = Env(nc, ctx)
        declare_l1ffn_io(nc, env)
        env.mkchains()
        alloc_common_sbuf(env)
        alloc_l1ffn_sbuf(env)
        dsc = env.sem("dsc")
        with nc.Block("c0") as blk:
            @blk.sync
            def _(s):
                s.dma_start(env.t["consts"][:], env.a["consts"]).then_inc(dsc, 16)
                s.dma_start(env.t["id_f32"][:], env.a["ident"]).then_inc(dsc, 16)
                s.wait_ge(dsc, 32)
            @blk.gpsimd
            def _(g):
                g.memset(env.t["ones_row"][:], 1.0)
                g.memset(env.t["ones_col"][:], 1.0)
                g.memset(env.t["ones32"][:], 1.0)
                g.memset(env.t["ones_row512"][:], 1.0)
        ph_load_xres_nold(env)
        ph_l1ffn_xn(env)
        ph_l1ffn(env)
        ph_dump_xres(env)
    return nc


def ph_load_xres_nold(env):
    nc = env.nc
    T = env.t
    A = env.a
    dl = env.sem("dlx")
    with nc.Block("loadx") as blk:
        @blk.sync
        def _(s):
            s.dma_start(T["x_res"][:],
                        A["xres_in"].rearrange("(c p) t -> p c t", p=P)).then_inc(dl, 16)
            s.wait_ge(dl, 16)


def _consts_arr():
    f32 = np.float32
    consts = np.zeros((P, 8), f32)
    consts[:, C_ONE] = 1.0
    consts[:, C_NEGSQRT] = -SQRT_HD
    consts[:, C_INVSQRT] = 1.0 / SQRT_HD
    consts[:, C_LNEPS] = 1e-5
    consts[:, C_NEGHALF] = -0.5
    consts[:, C_HALF_PI] = math.pi / 2.0
    consts[:, C_EIGHTH] = 0.125
    return consts


def prep_l0ffn_inputs(I, xres_in):
    f32 = np.float32
    sAT = np.zeros((D, 32), f32)
    sAT[:, :16] = np.asarray(I["L0_ffn_shared_A"], f32).T
    rWT = np.zeros((D, 32), f32)
    rWT[:, :6] = np.asarray(I["L0_ffn_router_W"], f32).T
    rb = np.zeros((1, 32), f32)
    rb[0, :6] = np.asarray(I["L0_ffn_router_b"], f32)
    return {
        "bwT": np.ascontiguousarray(np.asarray(I["L0_ffn_base_W"], f32).T),
        "sAT": sAT,
        "sBT": np.ascontiguousarray(np.asarray(I["L0_ffn_shared_B"], f32).T),
        "rAT": np.ascontiguousarray(np.asarray(I["L0_ffn_routed_A"], f32).transpose(2, 0, 1).reshape(D, 96)),
        "rBT": np.ascontiguousarray(np.asarray(I["L0_ffn_routed_B"], f32).transpose(0, 2, 1).reshape(96, D)),
        "rWT": rWT,
        "rb_row": rb,
        "bb_row": np.ascontiguousarray(np.asarray(I["L0_ffn_base_b"], f32).reshape(1, D)),
        "xres_in": xres_in,
        "consts": _consts_arr(),
        "ident": np.eye(P, dtype=f32),
    }


def _mask_rows():
    f32 = np.float32
    m = np.zeros((P, 4, P), f32)
    m[:, 0, 64:128] = -1e9
    m[:, 1, 0:64] = 1.0
    m[:, 2, 0:64] = -1e9
    m[:, 3, 64:128] = 1.0
    return np.ascontiguousarray(m.reshape(P, 4 * P))


def prep_l1attn_inputs(I, xres_in):
    f32 = np.float32
    w = np.asarray(I["L1_attn_w_qkv"], f32).reshape(D, 16, 192)
    b = np.asarray(I["L1_attn_b_qkv"], f32).reshape(16, 192)
    return {
        "wqc": np.ascontiguousarray(w[:, :, 0:64].reshape(D, D)),
        "wkc": np.ascontiguousarray(w[:, :, 64:128].reshape(D, D)),
        "wvc": np.ascontiguousarray(w[:, :, 128:192].reshape(D, D)),
        "woc": np.ascontiguousarray(np.asarray(I["L1_attn_w_o"], f32)),
        "bq_row": np.ascontiguousarray(b[:, 0:64].reshape(1, D)),
        "bk_row": np.ascontiguousarray(b[:, 64:128].reshape(1, D)),
        "bv_row": np.ascontiguousarray(b[:, 128:192].reshape(1, D)),
        "boc_row": np.ascontiguousarray(np.asarray(I["L1_attn_b_o"], f32).reshape(1, D)),
        "mrows": _mask_rows(),
        "xres_in": xres_in,
        "consts": _consts_arr(),
        "ident": np.eye(P, dtype=f32),
    }


def prep_l1ffn_inputs(I, xres_in):
    f32 = np.float32
    return {
        "Wg": np.ascontiguousarray(np.asarray(I["L1_ffn_Wg"], f32)),
        "Wu": np.ascontiguousarray(np.asarray(I["L1_ffn_Wu"], f32)),
        "Wd": np.ascontiguousarray(np.asarray(I["L1_ffn_Wd"], f32)),
        "xres_in": xres_in,
        "consts": _consts_arr(),
        "ident": np.eye(P, dtype=f32),
    }


def prep_full(I, core):
    m = prep_l0attn_inputs(I, core)
    for d in (prep_l0ffn_inputs(I, None), prep_l1attn_inputs(I, None),
              prep_l1ffn_inputs(I, None)):
        d.pop("xres_in")
        d.pop("consts", None)
        d.pop("ident", None)
        m.update(d)
    return m


def build_full():
    nc = bass.Bass("TRN2", target_bir_lowering=False)
    with ExitStack() as ctx:
        env = Env(nc, ctx)
        declare_l0attn_io(nc, env)
        declare_l0ffn_io(nc, env, io_xres=False)
        declare_l1attn_io(nc, env, io_xres=False)
        declare_l1ffn_io(nc, env, io_xres=False)
        env.mkchains()
        alloc_common_sbuf(env)
        with env.scope():
            alloc_l0attn_sbuf(env)
            ph_l0_setup(env)
            ph_l0_cols(env)
            ph_l0_xres_init(env)
            ph_l0_prepass(env)
            ph_l0_scores(env)
            ph_l0_epilogue(env)
        with env.scope():
            alloc_l0ffn_sbuf(env)
            ph_l0ffn_setup(env)
            ph_l0ffn(env)
        with env.scope():
            alloc_l1attn_sbuf(env)
            ph_l1attn_setup(env)
            ph_l1attn(env)
        with env.scope():
            alloc_l1ffn_sbuf(env)
            ph_l1ffn_xn(env)
            ph_l1ffn(env)
        ph_dump_xres(env)
    return nc


def kernel(**inputs):
    cores = list(range(8))
    nc = build_full()
    r = run_bass_kernel_spmd(nc, [prep_full(inputs, c) for c in cores], core_ids=cores)
    out = np.empty((2, SEQ, D), np.float32)
    for c in cores:
        b, qoff = c // 4, (c % 4) * TOK
        out[b, qoff:qoff + TOK, :] = np.asarray(r.results[c]["xres_out"], np.float32).T
    return out
